# version 13
# baseline (speedup 1.0000x reference)
import numpy as np
import ml_dtypes
import concourse.bass as bass
import concourse.mybir as mybir
from concourse.bass_utils import run_bass_kernel_spmd

F32 = mybir.dt.float32
BF16 = mybir.dt.bfloat16
AF = mybir.ActivationFunctionType
ALU = mybir.AluOpType
AX = mybir.AxisListType

LN_EPS = 1e-5
ATTN_EPS64 = float(np.float32(1e-6) * 64.0)


def make_cfg(S=2048, NSEQ=4, L=3, debug=()):
    D, DFF, H = 1024, 4096, 16
    C = 256
    assert S % C == 0 and S % 512 == 0
    return dict(
        S=S, NSEQ=NSEQ, L=L, D=D, DFF=DFF, H=H,
        KT=D // 128, MT=DFF // 128,
        C=C, NCH=S // C, PT=C // 128, NTT=S // 128, NTS=S // 512,
        debug=set(debug),
    )



class _Cmd:
    __slots__ = ("kind", "eng", "fn", "mark", "gidx", "sem", "val", "target",
                 "sem_key", "dbg")

    def __init__(self, kind, eng, fn=None, target=None, dbg=""):
        self.kind, self.eng, self.fn = kind, eng, fn
        self.mark = False
        self.gidx = 0
        self.sem = None
        self.val = None
        self.target = target
        self.sem_key = None
        self.dbg = dbg


class _Rec:
    __slots__ = ("p0", "p1", "b0", "b1", "writer", "readers")

    def __init__(self, p0, p1, b0, b1, writer):
        self.p0, self.p1, self.b0, self.b1 = p0, p1, b0, b1
        self.writer = writer
        self.readers = {}


def _ap_box(ap):
    t = ap.tensor
    esz = mybir.dt.size(ap.dtype)
    dims = list(ap.ap)
    off = ap.offset
    sk = str(ap.space)
    if ("SB" in sk) or ("sb" in sk) or ("PSUM" in sk) or ("Psum" in sk) or ("psum" in sk):
        row = dims[0][0] if dims[0][0] else 1
        p0 = off // row
        b_off = (off - p0 * row) * esz
        span = 1
        for st, ct in dims[1:]:
            span += (ct - 1) * abs(st)
        return (t.name, p0, p0 + dims[0][1], b_off, b_off + span * esz)
    span = 1
    for st, ct in dims:
        span += (ct - 1) * abs(st)
    return (t.name, 0, 1, off * esz, (off + span) * esz)


class Planner:
    def __init__(self, nc):
        self.nc = nc
        self.cmds = {e: [] for e in ("pe", "ve", "sc", "qs", "qg")}
        self.recs = {}
        self.psum_spaces = set()
        self.last_drain = {e: -1 for e in self.cmds}
        self.highwater = {}
        self.streams = {}
        self.stream_last = {}
        self.n_op = 0

    def stream(self, name, phys):
        if name not in self.streams:
            self.streams[name] = phys
        return name

    def _sync_to(self, eng, pcmd):
        if pcmd is None:
            return
        if pcmd.kind == "dma":
            self._wait(eng, pcmd)
        elif pcmd.eng == eng:
            if eng == "pe":
                return
            if self.last_drain[eng] >= pcmd.gidx:
                return
            c = _Cmd("drain", eng)
            c.gidx = len(self.cmds[eng])
            self.cmds[eng].append(c)
            self.last_drain[eng] = c.gidx
        else:
            self._wait(eng, pcmd)

    def _wait(self, eng, pcmd):
        if pcmd.kind == "dma":
            pcmd = self.stream_last[pcmd.sem_key]
        key = pcmd.sem_key if pcmd.kind == "dma" else pcmd.eng
        if key == eng and pcmd.kind != "dma":
            return
        pcmd.mark = True
        hw = self.highwater.get((eng, key), -1)
        if hw >= pcmd.gidx:
            return
        c = _Cmd("wait", eng, target=pcmd)
        c.gidx = len(self.cmds[eng])
        self.cmds[eng].append(c)
        self.highwater[(eng, key)] = pcmd.gidx

    def _box(self, ap):
        name, p0, p1, b0, b1 = _ap_box(ap)
        if name in self.psum_spaces:
            return (name, 0, 128, 0, 1 << 30)
        return (name, p0, p1, b0, b1)

    def _access(self, eng, aps_r, aps_w):
        for ap in aps_r:
            space, p0, p1, b0, b1 = self._box(ap)
            for r in self.recs.get(space, ()):
                if r.p0 < p1 and p0 < r.p1 and r.b0 < b1 and b0 < r.b1 \
                        and r.writer is not None:
                    self._sync_to(eng, r.writer)
        for ap in aps_w:
            space, p0, p1, b0, b1 = self._box(ap)
            recs = self.recs.get(space)
            if recs is None:
                continue
            keep = []
            for r in recs:
                if r.p0 < p1 and p0 < r.p1 and r.b0 < b1 and b0 < r.b1:
                    if r.writer is not None:
                        self._sync_to(eng, r.writer)
                    for rd in r.readers.values():
                        self._sync_to(eng, rd)
                    if r.p0 >= p0 and r.p1 <= p1 and r.b0 >= b0 and r.b1 <= b1:
                        continue
                    keep.append(r)
                else:
                    keep.append(r)
            self.recs[space] = keep

    def _record(self, cmd, aps_r, aps_w):
        key = cmd.sem_key if cmd.kind == "dma" else cmd.eng
        for ap in aps_w:
            space, p0, p1, b0, b1 = self._box(ap)
            self.recs.setdefault(space, []).append(_Rec(p0, p1, b0, b1, cmd))
        for ap in aps_r:
            space, p0, p1, b0, b1 = self._box(ap)
            lst = self.recs.setdefault(space, [])
            hit = False
            for r in lst:
                if r.p0 == p0 and r.p1 == p1 and r.b0 == b0 and r.b1 == b1:
                    r.readers[key] = cmd
                    hit = True
            if not hit:
                rec = _Rec(p0, p1, b0, b1, None)
                rec.readers[key] = cmd
                lst.append(rec)

    def op(self, eng, fn, ins=(), outs=(), mark=False, dbg=""):
        self._access(eng, ins, outs)
        c = _Cmd("op", eng, fn=fn, dbg=dbg)
        c.gidx = len(self.cmds[eng])
        c.mark = mark
        self.cmds[eng].append(c)
        self._record(c, ins, outs)
        self.n_op += 1
        return c

    def dma(self, stream, out_ap, in_ap, dbg=""):
        phys = self.streams[stream]
        self._access(phys, [in_ap], [out_ap])
        c = _Cmd("dma", phys, dbg=dbg)
        c.fn = (out_ap, in_ap)
        c.sem_key = "st:" + stream
        c.gidx = len(self.cmds[phys])
        c.mark = True
        self.cmds[phys].append(c)
        self.stream_last[c.sem_key] = c
        self._record(c, [in_ap], [out_ap])
        self.n_op += 1
        return c

    def emit(self):
        nc = self.nc
        import contextlib
        stack = contextlib.ExitStack()
        sems = {}
        for e in ("pe", "ve", "sc", "qg", "qs"):
            sems[e] = stack.enter_context(nc.semaphore("s_" + e))
        for name in self.streams:
            sems["st:" + name] = stack.enter_context(nc.semaphore("s_" + name))
        counters = {k: 0 for k in sems}
        for e, lst in self.cmds.items():
            for c in lst:
                if c.kind == "op" and c.mark:
                    counters[c.eng] += 1
                    c.sem = sems[c.eng]
                    c.val = counters[c.eng]
                elif c.kind == "dma":
                    counters[c.sem_key] += 16
                    c.sem = sems[c.sem_key]
                    c.val = counters[c.sem_key]

        def run_engine(e):
            def f(handle):
                for c in self.cmds[e]:
                    if c.kind == "wait":
                        handle.wait_ge(c.target.sem, c.target.val)
                    elif c.kind == "drain":
                        handle.drain()
                    elif c.kind == "dma":
                        out_ap, in_ap = c.fn
                        handle.dma_start(out_ap, in_ap).then_inc(c.sem, 16)
                    else:
                        inst = c.fn(handle)
                        if c.mark:
                            inst.then_inc(c.sem, 1)
            return f

        with nc.Block() as block:
            block.tensor(run_engine("pe"))
            block.vector(run_engine("ve"))
            block.scalar(run_engine("sc"))
            block.gpsimd(run_engine("qg"))
            block.sync(run_engine("qs"))
        stack.close()



def host_consts(cfg):
    S, H, D = cfg["S"], cfg["H"], cfg["D"]
    c = {}
    c["ident"] = np.eye(128, dtype=np.float32)
    c["ones_col"] = np.ones((128, 1), dtype=np.float32)
    c["ones_row"] = np.ones((1, 128), dtype=np.float32)
    sgn = np.ones((128, 1), dtype=np.float32)
    sgn[32:64] = -1.0
    sgn[96:128] = -1.0
    c["sgn"] = sgn
    tri = np.triu(np.ones((128, 128), dtype=np.float32))
    c["mask0"] = np.concatenate([tri, np.ones((128, 128), np.float32)], axis=1)
    E = np.zeros((D, H), dtype=np.float32)
    for d in range(D):
        E[d, d // 64] = 1.0
    c["E"] = E
    P2 = np.zeros((H, 8 * 128), dtype=np.float32)
    for hp in range(8):
        for r in range(128):
            P2[hp * 2 + r // 64, hp * 128 + r] = 1.0
    c["P2b"] = P2
    half = 16
    inv = 10000.0 ** (-np.arange(half, dtype=np.float64) / half)
    ang = np.arange(S, dtype=np.float64)[:, None] * inv[None, :]
    c["rope_cos"] = np.cos(ang).astype(np.float32)
    c["rope_sin"] = np.sin(ang).astype(np.float32)
    return c


def _pack_cols(v):
    sh = v.shape[:-1]
    n = v.shape[-1] // 128
    return np.ascontiguousarray(np.asarray(v).reshape(*sh, n, 128).swapaxes(-1, -2))


def host_layout(cfg, inputs):
    g = {k: np.asarray(v) for k, v in inputs.items()}
    D = cfg["D"]
    out = {}
    for k in ("Wq", "Wk", "Wv", "Wo"):
        out["a_" + k] = np.ascontiguousarray(g["L0_attn_" + k])
    L = g["L0_attn_W1"].shape[0]
    out["a_W1"] = np.ascontiguousarray(
        g["L0_attn_W1"].reshape(L, 8, 128, 32, 128).transpose(0, 3, 2, 1, 4)
        .reshape(L, 32, 128, 1024))
    out["a_W2"] = np.ascontiguousarray(
        g["L0_attn_W2"].reshape(L, 32, 128, 8, 128).transpose(0, 3, 2, 1, 4)
        .reshape(L, 8, 128, 4096))
    for k in ("bq", "bk", "bo", "ln1_g", "ln1_b", "ln2_g", "ln2_b", "bf2"):
        out["a_" + k] = _pack_cols(g["L0_attn_" + k])
    out["a_bf1"] = _pack_cols(g["L0_attn_bf1"])
    out["a_bv"] = np.ascontiguousarray(g["L0_attn_bv"])
    om = g["L0_attn_omega"]
    L = om.shape[0]
    om4 = np.zeros((L, 128, 128), dtype=np.float32)
    for l in range(L):
        o = om[l]
        om4[l, 0:64, 0:32] = o
        om4[l, 0:64, 32:64] = o
        om4[l, 64:128, 64:96] = o
        om4[l, 64:128, 96:128] = o
    out["a_omega4"] = om4
    out["f_Wqt"] = np.ascontiguousarray(g["L0_ffn_Wq"].transpose(1, 0, 2).reshape(D, 1024))
    out["f_Wkt"] = np.ascontiguousarray(g["L0_ffn_Wk"].transpose(1, 0, 2).reshape(D, 1024))
    out["f_Wselv"] = np.ascontiguousarray(g["L0_ffn_Wsel_v"].transpose(2, 0, 1).reshape(D, 128))
    out["f_Wselo"] = np.ascontiguousarray(g["L0_ffn_Wsel_o"].transpose(2, 0, 1).reshape(D, 128))
    wvr = g["L0_ffn_Wv"].transpose(0, 2, 1, 3).reshape(16, D, 512)
    out["f_Wv"] = np.ascontiguousarray(
        wvr.reshape(16, 8, 128, 512).transpose(0, 2, 1, 3).reshape(16, 128, 4096))
    woa = g["L0_ffn_Wo"].reshape(16 * 8 * 64, D)
    out["f_Wo"] = np.ascontiguousarray(
        woa.reshape(8, 8, 128, 1024).transpose(0, 2, 1, 3).reshape(8, 128, 8192))
    out["f_W1"] = np.ascontiguousarray(
        g["L0_ffn_W1"].reshape(8, 128, 32, 128).transpose(2, 1, 0, 3)
        .reshape(32, 128, 1024))
    out["f_W2"] = np.ascontiguousarray(
        g["L0_ffn_W2"].reshape(32, 128, 8, 128).transpose(2, 1, 0, 3)
        .reshape(8, 128, 4096))
    for k in ("ln1_g", "ln1_b", "ln2_g", "ln2_b", "b2"):
        out["f_" + k] = _pack_cols(g["L0_ffn_" + k])
    out["f_b1"] = _pack_cols(g["L0_ffn_b1"])
    out.update(host_consts(cfg))
    return out



class Build:
    def __init__(self, cfg):
        self.cfg = cfg
        self.nc = bass.Bass("TRN2", target_bir_lowering=False)
        self.P = Planner(self.nc)

    def dram_in(self, name, shape, dt=F32):
        return self.nc.dram_tensor(name, list(shape), dt, kind="ExternalInput").ap()

    def build(self):
        cfg = self.cfg
        nc, P = self.nc, self.P
        import contextlib
        self._st = contextlib.ExitStack()
        en = self._st.enter_context
        S, NSEQ, L, D, DFF = cfg["S"], cfg["NSEQ"], cfg["L"], cfg["D"], cfg["DFF"]
        KT, MT, H, NTT = cfg["KT"], cfg["MT"], cfg["H"], cfg["NTT"]

        self.x = self.dram_in("x", (NSEQ, S, D))
        self.out = nc.dram_tensor("out", [NSEQ, S, D], F32, kind="ExternalOutput").ap()
        self.aW = {k: self.dram_in("a_" + k, (L, D, D)) for k in ("Wq", "Wk", "Wv", "Wo")}
        self.aW1 = self.dram_in("a_W1", (L, 32, 128, 1024))
        self.aW2 = self.dram_in("a_W2", (L, 8, 128, 4096))
        self.ab = {k: self.dram_in("a_" + k, (L, 128, KT)) for k in
                   ("bq", "bk", "bo", "ln1_g", "ln1_b", "ln2_g", "ln2_b", "bf2")}
        self.abf1 = self.dram_in("a_bf1", (L, 128, MT))
        self.abv = self.dram_in("a_bv", (L, D))
        self.aom4 = self.dram_in("a_omega4", (L, 128, 128))
        self.fWqt = self.dram_in("f_Wqt", (D, 1024))
        self.fWkt = self.dram_in("f_Wkt", (D, 1024))
        self.fWselv = self.dram_in("f_Wselv", (D, 128))
        self.fWselo = self.dram_in("f_Wselo", (D, 128))
        self.fWv = self.dram_in("f_Wv", (16, 128, 4096))
        self.fWo = self.dram_in("f_Wo", (8, 128, 8192))
        self.fW1 = self.dram_in("f_W1", (32, 128, 1024))
        self.fW2 = self.dram_in("f_W2", (8, 128, 4096))
        self.fb = {k: self.dram_in("f_" + k, (128, KT)) for k in
                   ("ln1_g", "ln1_b", "ln2_g", "ln2_b", "b2")}
        self.fb1 = self.dram_in("f_b1", (128, MT))
        self.ident_in = self.dram_in("ident", (128, 128))
        self.onesc_in = self.dram_in("ones_col", (128, 1))
        self.onesr_in = self.dram_in("ones_row", (1, 128))
        self.sgn_in = self.dram_in("sgn", (128, 1))
        self.mask0_in = self.dram_in("mask0", (128, 256))
        self.E_in = self.dram_in("E", (D, H))
        self.P2b_in = self.dram_in("P2b", (H, 8 * 128))
        self.cos_in = self.dram_in("rope_cos", (S, 16))
        self.sin_in = self.dram_in("rope_sin", (S, 16))

        self.wb = {}
        mir = lambda nm, ap: self.wb.setdefault(
            nm, nc.dram_tensor("wb_" + nm, list(ap.shape), BF16, kind="Internal").ap())
        for k in ("Wq", "Wk", "Wv", "Wo"):
            mir("a_" + k, self.aW[k])
        mir("a_W1", self.aW1)
        mir("a_W2", self.aW2)
        mir("f_Wqt", self.fWqt)
        mir("f_Wkt", self.fWkt)
        mir("f_Wselv", self.fWselv)
        mir("f_Wselo", self.fWselo)
        mir("f_Wv", self.fWv)
        mir("f_Wo", self.fWo)
        mir("f_W1", self.fW1)
        mir("f_W2", self.fW2)

        self.xn_sp = nc.dram_tensor("xn_sp", [KT, 128, S], BF16, kind="Internal").ap()
        self.xnf_sp = nc.dram_tensor("xnf_sp", [KT, 128, S], F32, kind="Internal").ap()
        self.q_sp = nc.dram_tensor("q_sp", [KT, 128, S], BF16, kind="Internal").ap()
        self.s_sp = nc.dram_tensor("s_sp", [KT, 128, S], F32, kind="Internal").ap()

        sb = lambda n, sh, dt: en(nc.sbuf_tensor(n, list(sh), dt))
        self.A1 = sb("A1", (128, KT * S), F32)
        self.A2 = sb("A2", (128, KT * S), BF16)
        self.BIGB = sb("BIGB", (128, 32768), BF16)
        self.CT = sb("CT", (128, 16384), BF16)
        self.POOL = sb("POOL", (128, 2304), F32)
        self.identf = sb("identf", (128, 128), F32)
        self.identb = sb("identb", (128, 128), BF16)
        self.onesc = sb("onesc", (128, 1), BF16)
        self.onesr = sb("onesr", (1, 128), F32)
        self.onesrb = sb("onesrb", (1, 128), BF16)
        self.sgn = sb("sgn_s", (128, 1), F32)
        self.P2b = sb("P2b_s", (16, 8 * 128), BF16)
        self.om4 = sb("om4", (128, 128), BF16)
        self.bvrow = sb("bvrow", (1, 1024), BF16)
        self.biasA = sb("biasA", (128, 96), F32)
        self.MRS = sb("MRS", (128, 2 * NTT), F32)

        self.Ssb = self.POOL[:, 512:1032]
        self.Sbf = self.CT[:, 15392:15912]
        self.mask0 = self.CT[:, 15912:16168]
        self.E_sb = self.CT[:, 16168:16296]

        ps = lambda n, sh, dt: en(nc.psum_tensor(n, list(sh), dt))
        self.PS = [ps(f"PS{i}", (128, 512), F32) for i in range(6)]
        self.PB = [ps(f"PB{i}", (128, 1024), BF16) for i in range(2)]
        for t in list(self.PS) + list(self.PB):
            P.psum_spaces.add(t.name)

        for nm in ("cst", "cs2", "prm", "wo_", "om4s"):
            P.stream(nm, "qg")
        for nm, n0 in (("a_Wq", 3), ("a_Wk", 3), ("a_Wv", 3), ("a_Wo", 3),
                       ("a_W1", 3), ("a_W2", 3), ("f_Wqt", 1), ("f_Wkt", 1),
                       ("f_Wselv", 1), ("f_Wselo", 1), ("f_Wv", 4), ("f_Wo", 4),
                       ("f_W1", 4), ("f_W2", 4)):
            for i in range(n0):
                P.stream(f"pp_{nm}{i}", "qg")
        for nm in ("lwq", "lwk", "lwv", "lwo"):
            P.stream(nm, "qs")
        for i in range(4):
            P.stream(f"r1{i}", "qs")
        for i in range(2):
            P.stream(f"r2{i}", "qs")
        for nm in ("cq", "x0", "x1", "spwx", "spwf", "spws", "spwq",
                   "sr0", "sr1", "sxf", "ss0", "ss1", "fs0", "fs1", "oq"):
            P.stream(nm, "qs")

        self.dbgmap = {}
        for name in cfg["debug"]:
            self.dbgmap[name] = nc.dram_tensor(
                "dbg_" + name, [NSEQ, 128, KT * S], F32, kind="ExternalOutput").ap()

        self._consts()
        for seq in range(NSEQ):
            self._phase0(seq)
            self._attn_branch(seq)
            self._ffn_branch(seq)
            self._final(seq)

        for c in reversed(P.cmds["qs"]):
            if c.kind == "dma":
                P._wait("ve", c)
                break
        P.emit()
        self._st.close()
        return nc

    def tap(self, name, seq):
        if name not in self.dbgmap:
            return
        self.P.dma("oq", self.dbgmap[name][seq], self.A1.ap(), dbg="tap" + name)

    def mmgroup(self, bank_ap, mms, mark_last=True, dbg=""):
        P = self.P
        n = len(mms)
        last = None
        for i, (l_ap, r_ap) in enumerate(mms):
            outs = [bank_ap] if (i == 0 or i == n - 1) else []
            st, sp = (i == 0), (i == n - 1)
            last = P.op("pe", lambda t, o=bank_ap, a=l_ap, b=r_ap, s1=st, s2=sp:
                        t.matmul(o, a, b, start=s1, stop=s2),
                        ins=[l_ap, r_ap], outs=outs, mark=(sp and mark_last), dbg=dbg)
        return last

    def transpose_batch(self, pairs, pb_idx, out_ap=None):
        P = self.P
        pb = self.PB[pb_idx % 2]
        n = len(pairs)
        for i, (d, s_) in enumerate(pairs):
            P.op("pe", lambda t, o=pb[:, i * 128:(i + 1) * 128], i_=s_:
                 t.transpose(o, i_, self.identb[:, :]),
                 ins=[s_, self.identb.ap()],
                 outs=[pb[:, i * 128:(i + 1) * 128]], mark=(i == n - 1))
        eng = ("ve", "sc")[pb_idx % 2]
        if out_ap is not None:
            src_ap = pb[:, 0:n * 128]
            if eng == "sc":
                P.op("sc", lambda s, o=out_ap, i=src_ap: s.copy(o, i),
                     ins=[src_ap], outs=[out_ap])
            else:
                P.op("ve", lambda v, o=out_ap, i=src_ap: v.tensor_copy(o, i),
                     ins=[src_ap], outs=[out_ap])
        else:
            for i, (d, s_) in enumerate(pairs):
                t_ap = pb[:, i * 128:(i + 1) * 128]
                if eng == "sc":
                    P.op("sc", lambda s, o=d, i_=t_ap: s.copy(o, i_),
                         ins=[t_ap], outs=[d])
                else:
                    P.op("ve", lambda v, o=d, i_=t_ap: v.tensor_copy(o, i_),
                         ins=[t_ap], outs=[d])

    def transpose_b(self, dst_ap, src_ap, pb_idx, eng=None):
        P = self.P
        pb = self.PB[pb_idx % 2]
        if eng is None:
            eng = ("ve", "sc")[pb_idx % 2]
        P.op("pe", lambda t, o=pb[:, 0:128], i=src_ap: t.transpose(
            o, i, self.identb[:, :]),
            ins=[src_ap, self.identb.ap()], outs=[pb[:, 0:128]], mark=True)
        if eng == "sc":
            P.op("sc", lambda s, o=dst_ap, i=pb[:, 0:128]: s.copy(o, i),
                 ins=[pb[:, 0:128]], outs=[dst_ap])
        else:
            P.op("ve", lambda v, o=dst_ap, i=pb[:, 0:128]: v.tensor_copy(o, i),
                 ins=[pb[:, 0:128]], outs=[dst_ap])

    def rstd_inplace(self, row_ap):
        P = self.P
        P.op("sc", lambda s, o=row_ap: s.activation(o, o, AF.Ln),
             ins=[row_ap], outs=[row_ap])
        P.op("sc", lambda s, o=row_ap: s.activation(o, o, AF.Exp, scale=-0.5),
             ins=[row_ap], outs=[row_ap])

    def _consts(self):
        P, cfg = self.P, self.cfg
        NTT = cfg["NTT"]
        P.dma("cq", self.identf[:, :], self.ident_in)
        P.dma("cst", self.identb[:, :], self.ident_in)
        P.dma("cst", self.onesc[:, :], self.onesc_in)
        P.dma("cq", self.onesr[:, :], self.onesr_in)
        P.dma("cst", self.onesrb[:, :], self.onesr_in)
        P.dma("cq", self.sgn[:, :], self.sgn_in)
        P.dma("cst", self.P2b[:, :], self.P2b_in)
        P.op("ve", lambda v, o=self.P2b[:, :]: v.tensor_scalar_mul(o, o, -1.0),
             ins=[self.P2b.ap()], outs=[self.P2b.ap()], dbg="negP2b")
        srcs = {"a_Wq": self.aW["Wq"], "a_Wk": self.aW["Wk"],
                "a_Wv": self.aW["Wv"], "a_Wo": self.aW["Wo"],
                "a_W1": self.aW1, "a_W2": self.aW2,
                "f_Wqt": self.fWqt, "f_Wkt": self.fWkt,
                "f_Wselv": self.fWselv, "f_Wselo": self.fWselo,
                "f_Wv": self.fWv, "f_Wo": self.fWo,
                "f_W1": self.fW1, "f_W2": self.fW2}
        order = ["a_Wq", "a_Wk", "a_Wv", "a_Wo", "a_W1", "a_W2",
                 "f_Wqt", "f_Wkt", "f_Wselv", "f_Wselo", "f_Wv", "f_Wo",
                 "f_W1", "f_W2"]
        nslice = {"a_Wq": 3, "a_Wk": 3, "a_Wv": 3, "a_Wo": 3, "a_W1": 3,
                  "a_W2": 3, "f_Wqt": 1, "f_Wkt": 1, "f_Wselv": 1,
                  "f_Wselo": 1, "f_Wv": 4, "f_Wo": 4, "f_W1": 4, "f_W2": 4}

        def ppk_emit(nm, i):
            s_ap, d_ap = srcs[nm], self.wb[nm]
            ns = min(nslice[nm], s_ap.shape[0])
            if ns == 1:
                P.dma(f"pp_{nm}0", d_ap, s_ap, dbg=f"ppk_{nm}")
                return
            step = s_ap.shape[0] // ns
            P.dma(f"pp_{nm}{i}", d_ap[i * step:(i + 1) * step],
                  s_ap[i * step:(i + 1) * step], dbg=f"ppk_{nm}{i}")

        self.deferred_ppk = []
        for nm in order:
            ns = min(nslice[nm], srcs[nm].shape[0])
            for i in range(ns):
                if nm in ("a_Wq", "a_Wk", "a_Wv", "a_Wo") and i == 0:
                    ppk_emit(nm, i)
                else:
                    self.deferred_ppk.append((nm, i))
        self._ppk_emit = ppk_emit

    def _phase0(self, seq):
        cfg, P = self.cfg, self.P
        S, D, KT, NTT = cfg["S"], cfg["D"], cfg["KT"], cfg["NTT"]
        A1, A2 = self.A1, self.A2
        for tt in range(NTT):
            xs = ("x0", "x1")[tt % 2]
            xt = self.POOL[:, (tt % 2) * 1024:(tt % 2) * 1024 + 1024]
            xn = self.CT[:, 1024 + (tt % 2) * 3072: 2048 + (tt % 2) * 3072].bitcast(F32)
            P.dma(xs, xt, self.x[seq, tt * 128:(tt + 1) * 128, :], dbg=f"x{tt}")
            mcol = self.MRS[:, tt:tt + 1]
            rcol = self.MRS[:, NTT + tt:NTT + tt + 1]
            P.op("ve", lambda v, o=mcol, i=xt: v.tensor_reduce(
                o.unsqueeze(2), i.rearrange("p (a n) -> p a n", a=1), AX.X, ALU.add),
                ins=[xt], outs=[mcol], dbg="p0mean")
            P.op("ve", lambda v, o=mcol: v.tensor_scalar_mul(o, o, -1.0 / D),
                 ins=[mcol], outs=[mcol])
            sq = self.CT[:, (tt % 2) * 3072: (tt % 2) * 3072 + 1024].bitcast(F32)
            ctb = self.CT[:, 2560 + (tt % 2) * 3072: 2576 + (tt % 2) * 3072].bitcast(F32)
            vcol = ctb[:, 0:1]
            vcol2 = ctb[:, 1:2]
            P.op("sc", lambda s, o=sq, i=xt[:, 0:512], b=mcol, a=vcol: s.activation(
                o, i, AF.Square, bias=b, scale=1.0, accum_out=a),
                ins=[xt[:, 0:512], mcol], outs=[sq, vcol], dbg="p0sq0")
            P.op("sc", lambda s, o=sq, i=xt[:, 512:1024], b=mcol, a=vcol2: s.activation(
                o, i, AF.Square, bias=b, scale=1.0, accum_out=a),
                ins=[xt[:, 512:1024], mcol], outs=[sq, vcol2], dbg="p0sq1")
            P.op("ve", lambda v, o=vcol, a=vcol, b=vcol2: v.tensor_add(o, a, b),
                 ins=[vcol, vcol2], outs=[vcol])
            P.op("ve", lambda v, o=vcol: v.tensor_scalar(
                o, o, 1.0 / D, LN_EPS, ALU.mult, ALU.add), ins=[vcol], outs=[vcol])
            self.rstd_inplace(vcol)
            P.op("ve", lambda v, o=rcol, i=vcol: v.tensor_copy(o, i),
                 ins=[vcol], outs=[rcol], dbg="p0rstd")
            for half in range(2):
                xh = xt[:, half * 512:(half + 1) * 512]
                P.op("ve", lambda v, o=xn, i=xh, m=mcol, r=rcol: v.tensor_scalar(
                    o, i, m, r, ALU.add, ALU.mult),
                    ins=[xh, mcol, rcol], outs=[xn], dbg="p0xn")
                xnb = self.CT[:, 2048 + (tt % 2) * 3072: 2560 + (tt % 2) * 3072]
                P.op("sc", lambda s, o=xnb, i=xn: s.copy(o, i), ins=[xn], outs=[xnb])
                out = A2.ap().rearrange("p (kt t) -> p kt t", t=S)[
                    :, half * 4: half * 4 + 4, tt * 128:tt * 128 + 128]
                self.transpose_batch(
                    [(None, xnb[:, k * 128:(k + 1) * 128]) for k in range(4)],
                    half, out_ap=out)
                for k in range(4):
                    kt = half * 4 + k
                    psf = self.PS[kt % 2]
                    src_f = xn[:, k * 128:(k + 1) * 128]
                    dst_f = A1[:, kt * S + tt * 128: kt * S + tt * 128 + 128]
                    P.op("pe", lambda t, o=psf[:, 0:128], i=src_f: t.transpose(
                        o, i, self.identf[:, :]),
                        ins=[src_f, self.identf.ap()], outs=[psf[:, 0:128]],
                        mark=True, dbg="p0trf")
                    P.op("sc", lambda s, o=dst_f, i=psf[:, 0:128]: s.copy(o, i),
                         ins=[psf[:, 0:128]], outs=[dst_f])
        for kt in range(KT):
            P.dma("spwx", self.xn_sp[kt], A2[:, kt * S:(kt + 1) * S], dbg="xnsp")
            P.dma("spwf", self.xnf_sp[kt], A1[:, kt * S:(kt + 1) * S], dbg="xnfsp")
        self.tap("p0_xn", seq)

    def ln_fm(self, gcol, bcol, write_a2, spill_s=False):
        cfg, P = self.cfg, self.P
        S, D, KT, NTS = cfg["S"], cfg["D"], cfg["KT"], cfg["NTS"]
        A1, A2 = self.A1, self.A2
        ROW2 = self.CT[:, 0:8192].bitcast(F32)
        ROW3 = self.CT[:, 8192:12288].bitcast(F32)
        mrow = ROW2[0:1, 0:S]
        vrow = ROW2[0:1, 2048:2048 + S]
        for ts in range(NTS):
            c0 = ts * 512
            for kt in range(KT):
                a1t = A1[:, kt * S + c0: kt * S + c0 + 512]
                a2t = A2[:, kt * S + c0: kt * S + c0 + 512]
                P.op("sc", lambda s, o=a2t, i=a1t: s.copy(o, i), ins=[a1t], outs=[a2t])
                sqt = self.CT[:, 8192 + kt * 512: 8192 + kt * 512 + 512]
                P.op("ve", lambda v, o=sqt, a=a2t, b=a2t: v.tensor_mul(o, a, b),
                     ins=[a2t], outs=[sqt])
            pssum = self.PS[2]
            pssq = self.PS[3]
            self.mmgroup(pssum[0:1, 0:512],
                         [(self.onesc[:, :], A2[:, kt * S + c0: kt * S + c0 + 512])
                          for kt in range(KT)])
            self.mmgroup(pssq[0:1, 0:512],
                         [(self.onesc[:, :],
                           self.CT[:, 8192 + kt * 512: 8192 + kt * 512 + 512])
                          for kt in range(KT)])
            P.op("ve", lambda v, o=mrow[0:1, c0:c0 + 512], i=pssum[0:1, 0:512]:
                 v.tensor_scalar_mul(o, i, -1.0 / D),
                 ins=[pssum[0:1, 0:512]], outs=[mrow[0:1, c0:c0 + 512]])
            P.op("ve", lambda v, o=vrow[0:1, c0:c0 + 512], i=pssq[0:1, 0:512]:
                 v.tensor_scalar_mul(o, i, 1.0 / D),
                 ins=[pssq[0:1, 0:512]], outs=[vrow[0:1, c0:c0 + 512]])
        m2 = ROW3[0:1, 0:S]
        P.op("sc", lambda s, o=m2, i=mrow: s.activation(o, i, AF.Square),
             ins=[mrow], outs=[m2])
        P.op("ve", lambda v, o=vrow, a=vrow, b=m2: v.tensor_sub(o, a, b),
             ins=[vrow, m2], outs=[vrow])
        P.op("ve", lambda v, o=vrow: v.tensor_scalar_add(o, o, LN_EPS),
             ins=[vrow], outs=[vrow])
        self.rstd_inplace(vrow)
        P.op("ve", lambda v, o=mrow, a=mrow, b=vrow: v.tensor_mul(o, a, b),
             ins=[mrow, vrow], outs=[mrow])
        for ts in range(NTS):
            c0 = ts * 512
            psr = self.PS[4]
            psm = self.PS[5]
            P.op("pe", lambda t, o=psr[:, 0:512], a=self.onesr[:, :],
                 b=vrow[0:1, c0:c0 + 512]: t.matmul(o, a, b, start=True, stop=True),
                 ins=[self.onesr.ap(), vrow[0:1, c0:c0 + 512]],
                 outs=[psr[:, 0:512]], mark=True)
            P.op("pe", lambda t, o=psm[:, 0:512], a=self.onesr[:, :],
                 b=mrow[0:1, c0:c0 + 512]: t.matmul(o, a, b, start=True, stop=True),
                 ins=[self.onesr.ap(), mrow[0:1, c0:c0 + 512]],
                 outs=[psm[:, 0:512]], mark=True)
            for kt in range(KT):
                a1t = A1[:, kt * S + c0: kt * S + c0 + 512]
                P.op("ve", lambda v, o=a1t, i=a1t, r=psr[:, 0:512]:
                     v.tensor_mul(o, i, r), ins=[a1t, psr[:, 0:512]], outs=[a1t])
                P.op("ve", lambda v, o=a1t, i=a1t, m=psm[:, 0:512]:
                     v.tensor_add(o, i, m), ins=[a1t, psm[:, 0:512]], outs=[a1t])
                g1 = gcol[:, kt:kt + 1]
                b1 = bcol[:, kt:kt + 1]
                P.op("ve", lambda v, o=a1t, i=a1t, g=g1, b=b1: v.tensor_scalar(
                    o, i, g, b, ALU.mult, ALU.add), ins=[a1t, g1, b1], outs=[a1t])
                if write_a2:
                    a2t = A2[:, kt * S + c0: kt * S + c0 + 512]
                    P.op("sc", lambda s, o=a2t, i=a1t: s.copy(o, i),
                         ins=[a1t], outs=[a2t])
                if spill_s:
                    P.dma("spws", self.s_sp[kt][:, c0:c0 + 512], a1t, dbg="ssp")

    def ffn_mlp(self, w1_src, w2_src, b1col, b2col):
        cfg, P = self.cfg, self.P
        S, KT, MT = cfg["S"], cfg["KT"], cfg["MT"]
        A1, A2, HID = self.A1, self.A2, self.BIGB
        tsw = min(1024, S)
        for tsl in range(S // tsw):
            for m in range(MT):
                slot = m % 4
                r1 = self.CT[:, slot * 1024: slot * 1024 + 1024]
                P.dma(f"r1{slot}", r1, w1_src(m), dbg=f"w1m{m}")
                for nh in range(tsw // 512):
                    psb = self.PS[nh % 2]
                    rhs0 = tsl * tsw + nh * 512
                    self.mmgroup(psb[:, 0:512],
                                 [(r1[:, kt * 128:(kt + 1) * 128],
                                   A2[:, kt * S + rhs0: kt * S + rhs0 + 512])
                                  for kt in range(KT)], dbg=f"W1m{m}nh{nh}")
                    ht = HID[:, m * tsw + nh * 512: m * tsw + nh * 512 + 512]
                    bc = b1col[:, m:m + 1]
                    P.op("sc", lambda s, o=ht, i=psb[:, 0:512], b=bc:
                         s.activation(o, i, AF.Relu, bias=b),
                         ins=[psb[:, 0:512], bc], outs=[ht])
            for dt in range(KT):
                slot = dt % 2
                r2 = self.CT[:, 4096 + slot * 4096: 4096 + slot * 4096 + 4096]
                P.dma(f"r2{slot}", r2, w2_src(dt), dbg=f"w2d{dt}")
                for nh in range(tsw // 512):
                    psb = self.PS[2 + nh % 2]
                    self.mmgroup(psb[:, 0:512],
                                 [(r2[:, m * 128:(m + 1) * 128],
                                   HID[:, m * tsw + nh * 512: m * tsw + nh * 512 + 512])
                                  for m in range(MT)])
                    c0 = tsl * tsw + nh * 512
                    a1t = A1[:, dt * S + c0: dt * S + c0 + 512]
                    bc = b2col[:, dt:dt + 1]
                    P.op("ve", lambda v, o=a1t, i=psb[:, 0:512], b=bc, r=a1t:
                         v.scalar_tensor_tensor(o, i, b, r, ALU.add, ALU.add),
                         ins=[psb[:, 0:512], bc, a1t], outs=[a1t])

    def _attn_branch(self, seq):
        P, cfg = self.P, self.cfg
        P.dma("cs2", self.mask0, self.mask0_in)
        P.dma("cs2", self.E_sb.rearrange("p (kt h) -> p kt h", kt=cfg["KT"]),
              self.E_in.rearrange("(kt p) h -> p kt h", p=128))
        for l in range(self.cfg["L"]):
            self._attn_layer(seq, l)

    def _attn_layer(self, seq, l):
        cfg, P = self.cfg, self.P
        S, KT, H = cfg["S"], cfg["KT"], cfg["H"]
        C, NCH, PT = cfg["C"], cfg["NCH"], cfg["PT"]
        A1, A2, CT = self.A1, self.A2, self.CT
        c4 = float(64.0 ** -0.25)

        pm = ("bq", "bk", "bo", "ln1_g", "ln1_b", "ln2_g", "ln2_b", "bf2")
        for i, k in enumerate(pm):
            P.dma("prm", self.biasA[:, i * 8:(i + 1) * 8], self.ab[k][l], dbg="prm" + k)
        P.dma("prm", self.biasA[:, 64:96], self.abf1[l], dbg="prmbf1")
        P.dma("om4s", self.om4[:, :], self.aom4[l], dbg="om4")
        for p0, c0 in ((0, 32), (64, 96)):
            sl = self.om4[p0:p0 + 64, c0:c0 + 32]
            P.op("ve", lambda v, o=sl: v.tensor_scalar_mul(o, o, -1.0),
                 ins=[sl], outs=[sl])
        P.dma("wo_", self.bvrow[:, :], self.abv[l].unsqueeze(0), dbg="bvrow")
        for i in (0, 1):
            ba = self.biasA[:, i * 8:(i + 1) * 8]
            P.op("ve", lambda v, o=ba: v.tensor_scalar_mul(o, o, c4),
                 ins=[ba], outs=[ba])

        WQ = self.BIGB[:, 0:8192]
        WK = self.BIGB[:, 8192:16384]
        WV = self.BIGB[:, 16384:24576]
        WO = self.BIGB[:, 24576:32768]
        for nm, buf, src in (("lwq", WQ, self.wb["a_Wq"]), ("lwk", WK, self.wb["a_Wk"]),
                             ("lwv", WV, self.wb["a_Wv"]), ("lwo", WO, self.wb["a_Wo"])):
            for kt in range(KT):
                P.dma(nm, buf[:, kt * 1024:(kt + 1) * 1024],
                      src[l, kt * 128:(kt + 1) * 128, :], dbg=nm + str(kt))

        XS = CT[:, 0:2048]
        SQc = CT[:, 2048:4096]
        QFc = CT[:, 4096:6144]
        KFc = CT[:, 6144:8192]
        KTm = CT[:, 8192:10240]
        ATm = CT[:, 10240:12288]
        ATs_ = [CT[:, 12288 + i * 384: 12288 + (i + 1) * 384] for i in range(2)]
        VAch = CT[:, 13056:15136]
        HLb = CT[:, 15136:15392]
        UF = self.POOL[:, 0:256]
        HL = self.POOL[0:16, 256:512]
        va3 = VAch[:, 0:PT * 16 * 65].rearrange("p (g j) -> p g j", j=65)
        P.op("ve", lambda v, ap=va3[:, :, 64:65]: v.memset(ap, 1.0),
             outs=[va3[:, :, 64:65]], dbg="va1")

        def favor(out_buf, bias_i):
            psh = self.PS[4]
            self.mmgroup(psh[0:16, 0:C],
                         [(self.E_sb[:, kt * 16:(kt + 1) * 16],
                           SQc[:, kt * C:(kt + 1) * C]) for kt in range(KT)])
            P.op("sc", lambda s, o=HLb[0:16, 0:C], i=psh[0:16, 0:C]:
                 s.activation(o, i, AF.Identity, scale=0.5),
                 ins=[psh[0:16, 0:C]], outs=[HLb[0:16, 0:C]])
            for hp in range(8):
                psu = self.PS[4 + hp % 2]
                P.op("pe", lambda t, o=psu[:, 0:C], a=self.om4[:, :],
                     b=XS[:, hp * C:(hp + 1) * C]: t.matmul(o, a, b, start=True, stop=False),
                     ins=[self.om4.ap(), XS[:, hp * C:(hp + 1) * C]],
                     outs=[psu[:, 0:C]])
                P.op("pe", lambda t, o=psu[:, 0:C],
                     a=self.P2b[:, hp * 128:(hp + 1) * 128], b=HLb[0:16, 0:C]:
                     t.matmul(o, a, b, start=False, stop=True),
                     ins=[self.P2b[:, hp * 128:(hp + 1) * 128], HLb[0:16, 0:C]],
                     outs=[psu[:, 0:C]], mark=True)
                ot = out_buf[:, hp * C:(hp + 1) * C]
                P.op("sc", lambda s, o=ot, i=psu[:, 0:C]: s.activation(o, i, AF.Exp),
                     ins=[psu[:, 0:C]], outs=[ot])

        for ch in range(NCH):
            base = ch * C
            if ch == 1 and self.deferred_ppk:
                for nm_i in self.deferred_ppk:
                    self._ppk_emit(*nm_i)
                self.deferred_ppk = []
            for side, Wbuf, bia, outf, bq in (("q", WQ, 0, QFc, 0), ("k", WK, 1, KFc, 2)):
                for mt in range(KT):
                    psb = self.PS[mt % 4]
                    self.mmgroup(psb[:, 0:C],
                                 [(Wbuf[:, kt * 1024 + mt * 128: kt * 1024 + mt * 128 + 128],
                                   A2[:, kt * S + base: kt * S + base + C])
                                  for kt in range(KT)])
                    bc = self.biasA[:, bia * 8 + mt: bia * 8 + mt + 1]
                    xst = XS[:, mt * C:(mt + 1) * C]
                    sqt = SQc[:, mt * C:(mt + 1) * C]
                    P.op("sc", lambda s, o=xst, i=psb[:, 0:C], b=bc:
                         s.activation(o, i, AF.Identity, bias=b, scale=c4),
                         ins=[psb[:, 0:C], bc], outs=[xst])
                    P.op("sc", lambda s, o=sqt, i=psb[:, 0:C], b=bc:
                         s.activation(o, i, AF.Square, bias=b, scale=c4),
                         ins=[psb[:, 0:C], bc], outs=[sqt])
                favor(outf, bia)
                if side == "q":
                    for pt in range(PT):
                        for nh in range(2):
                            psb = self.PS[nh]
                            mms = [(A2[:, kt * S + base + pt * 128: kt * S + base + pt * 128 + 128],
                                    WV[:, kt * 1024 + nh * 512: kt * 1024 + nh * 512 + 512])
                                   for kt in range(KT)]
                            mms.append((self.onesrb[0:1, 0:128],
                                        self.bvrow[0:1, nh * 512: nh * 512 + 512]))
                            self.mmgroup(psb[:, 0:512], mms)
                            dst = va3[:, pt * 16 + nh * 8: pt * 16 + (nh + 1) * 8, 0:64]
                            P.op("sc", lambda s, o=dst, i=psb[:, 0:512].rearrange(
                                "p (h d) -> p h d", d=64): s.copy(o, i),
                                ins=[psb[:, 0:512]], outs=[dst])
            for pt in range(PT):
                pbh = self.PB[pt % 2]
                hsl = HLb[0:16, pt * 128: pt * 128 + 128]
                P.op("pe", lambda t, o=pbh[:, 0:16], i=hsl: t.transpose(
                    o, i, self.identb[0:16, 0:16]),
                    ins=[hsl, self.identb.ap()], outs=[pbh[:, 0:16]], mark=True)
                ht = self.POOL[:, 1040 + pt * 16: 1056 + pt * 16]
                P.op("sc", lambda s, o=ht, i=pbh[:, 0:16]: s.activation(
                    o, i, AF.Identity, scale=-1.0), ins=[pbh[:, 0:16]], outs=[ht])
                for h in range(H):
                    hp, r0 = h // 2, (h % 2) * 64
                    psu = self.PS[2 + h % 2]
                    P.op("pe", lambda t, o=psu[:, 0:64],
                         a=XS[r0:r0 + 64, hp * C + pt * 128: hp * C + pt * 128 + 128],
                         b=self.om4[r0:r0 + 64, r0:r0 + 64]:
                         t.matmul(o, a, b, start=True, stop=True),
                         ins=[XS[r0:r0 + 64, hp * C + pt * 128: hp * C + pt * 128 + 128],
                              self.om4[r0:r0 + 64, r0:r0 + 64]],
                         outs=[psu[:, 0:64]], mark=True)
                    kdst = KTm[:, pt * 1024 + hp * 128 + r0: pt * 1024 + hp * 128 + r0 + 64]
                    P.op("sc", lambda s, o=kdst, i=psu[:, 0:64], b=ht[:, h:h + 1]:
                         s.activation(o, i, AF.Exp, bias=b),
                         ins=[psu[:, 0:64], ht], outs=[kdst])
            def head_at(h):
                hp, r0 = h // 2, (h % 2) * 64
                ATs = ATs_[h % 2]
                qf = lambda c0, cn: QFc[r0:r0 + 64, hp * C + c0: hp * C + c0 + cn]
                kf = lambda c0, cn: KFc[r0:r0 + 64, hp * C + c0: hp * C + c0 + cn]
                for skp in range(PT):
                    psb = self.PS[skp % 2]
                    ncols = C - skp * 128
                    P.op("pe", lambda t, o=psb[:, 0:ncols], a=kf(skp * 128, 128),
                         b=qf(skp * 128, ncols): t.matmul(o, a, b, start=True, stop=True),
                         ins=[kf(skp * 128, 128), qf(skp * 128, ncols)],
                         outs=[psb[:, 0:ncols]], mark=True)
                    at = ATs[:, skp * 256: skp * 256 + ncols]
                    P.op("ve", lambda v, o=at, i=psb[:, 0:ncols],
                         m=self.mask0[:, 0:ncols]: v.tensor_mul(o, i, m),
                         ins=[psb[:, 0:ncols], self.mask0[:, 0:ncols]], outs=[at])

            def head_rest(h, ch):
                hp, r0 = h // 2, (h % 2) * 64
                ATs = ATs_[h % 2]
                qf = lambda c0, cn: QFc[r0:r0 + 64, hp * C + c0: hp * C + c0 + cn]
                Sb = self.Sbf[r0:r0 + 64, hp * 65: hp * 65 + 65]
                Sf = self.Ssb[r0:r0 + 64, hp * 65: hp * 65 + 65]
                for pq in range(PT):
                    pso = self.PS[2 + pq % 2]
                    mms = []
                    for skp in range(pq + 1):
                        lhs = ATs[:, skp * 256 + (pq - skp) * 128:
                                  skp * 256 + (pq - skp) * 128 + 128]
                        rhs = va3[:, skp * 16 + h, :]
                        mms.append((lhs, rhs))
                    if ch > 0:
                        mms.append((qf(pq * 128, 128), Sb))
                    self.mmgroup(pso[:, 0:65], mms)
                    dc = self.POOL[:, 1032 + (h * 2 + pq) % 8:
                                    1033 + (h * 2 + pq) % 8]
                    P.op("ve", lambda v, o=dc, i=pso[:, 64:65]: v.tensor_scalar_add(
                        o, i, ATTN_EPS64), ins=[pso[:, 64:65]], outs=[dc])
                    P.op("ve", lambda v, o=dc: v.reciprocal(o, o), ins=[dc], outs=[dc])
                    av = ATm[:, pq * 1024 + h * 64: pq * 1024 + h * 64 + 64]
                    P.op("ve", lambda v, o=av, i=pso[:, 0:64], d=dc:
                         v.tensor_scalar_mul(o, i, d),
                         ins=[pso[:, 0:64], dc], outs=[av])
                pss = self.PS[4 + h % 2]
                mms = [(KTm[:, pt * 1024 + hp * 128 + r0: pt * 1024 + hp * 128 + r0 + 64],
                        va3[:, pt * 16 + h, :]) for pt in range(PT)]
                self.mmgroup(pss[r0:r0 + 64, 0:65], mms)
                if ch == 0:
                    P.op("ve", lambda v, o=Sf, i=pss[r0:r0 + 64, 0:65]:
                         v.tensor_copy(o, i), ins=[pss[r0:r0 + 64, 0:65]], outs=[Sf])
                else:
                    P.op("ve", lambda v, o=Sf, a=Sf, b=pss[r0:r0 + 64, 0:65]:
                         v.tensor_add(o, a, b),
                         ins=[Sf, pss[r0:r0 + 64, 0:65]], outs=[Sf])
                P.op("sc", lambda s, o=Sb, i=Sf: s.copy(o, i), ins=[Sf], outs=[Sb])

            for h in range(H):
                head_at(h)
                if h > 0:
                    head_rest(h - 1, ch)
            head_rest(H - 1, ch)
            AFm = QFc
            for pt in range(PT):
                for half in range(2):
                    ats = list(range(half * 4, half * 4 + 4))
                    out = AFm.rearrange("p (at c) -> p at c", c=C)[
                        :, ats[0]:ats[0] + 4, pt * 128:pt * 128 + 128]
                    self.transpose_batch(
                        [(None, ATm[:, pt * 1024 + at * 128: pt * 1024 + at * 128 + 128])
                         for at in ats], half, out_ap=out)
            for dt in range(KT):
                psb = self.PS[dt % 4]
                self.mmgroup(psb[:, 0:C],
                             [(WO[:, at * 1024 + dt * 128: at * 1024 + dt * 128 + 128],
                               AFm[:, at * C:(at + 1) * C]) for at in range(KT)])
                a1t = A1[:, dt * S + base: dt * S + base + C]
                bc = self.biasA[:, 16 + dt: 16 + dt + 1]
                P.op("ve", lambda v, o=a1t, i=psb[:, 0:C], b=bc, r=a1t:
                     v.scalar_tensor_tensor(o, i, b, r, ALU.add, ALU.add),
                     ins=[psb[:, 0:C], bc, a1t], outs=[a1t])

        self.ln_fm(self.biasA[:, 24:32], self.biasA[:, 32:40], write_a2=True)
        w1_src = lambda m: self.wb["a_W1"][l, m]
        w2_src = lambda dt: self.wb["a_W2"][l, dt]
        self.ffn_mlp(w1_src, w2_src, self.biasA[:, 64:96], self.biasA[:, 56:64])
        last = (l == self.cfg["L"] - 1)
        self.ln_fm(self.biasA[:, 40:48], self.biasA[:, 48:56],
                   write_a2=not last, spill_s=last)
        if last:
            self.tap("attn_s", seq)

    def _ffn_branch(self, seq):
        cfg, P = self.cfg, self.P
        S, KT, H, NTT, NTS = cfg["S"], cfg["KT"], cfg["H"], cfg["NTT"], cfg["NTS"]
        A1, A2, CT, BIGB = self.A1, self.A2, self.CT, self.BIGB

        WQT = BIGB[:, 0:8192]
        WKT = BIGB[:, 8192:16384]
        WSEL = BIGB[:, 16384:18432]
        XNALL = A1.ap().bitcast(BF16)
        QAS = BIGB[:, 20480:21504]
        ROPB = BIGB[:, 21504:21504 + 32 * NTT]
        SCR = BIGB[:, 22528:24064]
        GVF = BIGB[:, 28160:30208]
        go = BIGB[:, 30208:32256]
        KFM = CT

        for kt in range(KT):
            P.dma("lwq", WQT[:, kt * 1024:(kt + 1) * 1024],
                  self.wb["f_Wqt"][kt * 128:(kt + 1) * 128, :], dbg="fwq")
            P.dma("lwk", WKT[:, kt * 1024:(kt + 1) * 1024],
                  self.wb["f_Wkt"][kt * 128:(kt + 1) * 128, :], dbg="fwk")
        P.dma("lwo", WSEL[:, 0:1024].rearrange("p (kt c) -> p kt c", c=128),
              self.wb["f_Wselv"].rearrange("(kt p) c -> p kt c", p=128), dbg="fwsv")
        P.dma("lwo", WSEL[:, 1024:2048].rearrange("p (kt c) -> p kt c", c=128),
              self.wb["f_Wselo"].rearrange("(kt p) c -> p kt c", p=128), dbg="fwso")
        P.dma("wo_", ROPB[:, 0:16 * NTT].rearrange("p (t j) -> p t j", j=16),
              self.cos_in.rearrange("(t p) j -> p t j", p=128), dbg="ropc")
        P.dma("wo_", ROPB[:, 16 * NTT:32 * NTT].rearrange("p (t j) -> p t j", j=16),
              self.sin_in.rearrange("(t p) j -> p t j", p=128), dbg="rops")

        for kt in range(KT):
            P.dma("sr0", XNALL[:, kt * S:(kt + 1) * S], self.xn_sp[kt], dbg="xnall")

        def xn_tiles(tt):
            return [XNALL[:, kt * S + tt * 128: kt * S + tt * 128 + 128]
                    for kt in range(KT)]

        SCRQ = SCR[:, 0:1024]
        SCRK = BIGB[:, 24064:25088]
        E1 = SCR[:, 1024:1280]
        E2 = SCR[:, 1280:1536]
        for tt in range(NTT):
            xnt = xn_tiles(tt)
            cosb = ROPB[:, tt * 16: tt * 16 + 16]
            sinb = ROPB[:, 16 * NTT + tt * 16: 16 * NTT + tt * 16 + 16]
            cb = cosb.unsqueeze(1).broadcast_to([128, 16, 16])
            sbb = sinb.unsqueeze(1).broadcast_to([128, 16, 16])
            for which, wt, qt, b0 in (("q", WQT, SCRQ, 0), ("k", WKT, SCRK, 2)):
                for nh in range(2):
                    psb = self.PS[b0 + nh]
                    self.mmgroup(psb[:, 0:512],
                                 [(xnt[kt],
                                   wt[:, kt * 1024 + nh * 512: kt * 1024 + nh * 512 + 512])
                                  for kt in range(KT)])
                    P.op("sc", lambda s, o=qt[:, nh * 512:(nh + 1) * 512],
                         i=psb[:, 0:512]: s.copy(o, i),
                         ins=[psb[:, 0:512]], outs=[qt[:, nh * 512:(nh + 1) * 512]])
            E3 = BIGB[:, 25088:25344]
            E4 = BIGB[:, 25344:25600]
            for qt in (SCRQ, SCRK):
                q3 = qt.rearrange("p (h d) -> p h d", d=64)
                x1 = q3[:, :, 0:16]
                x2 = q3[:, :, 16:32]
                e13 = E1.rearrange("p (h d) -> p h d", d=16)
                e23 = E2.rearrange("p (h d) -> p h d", d=16)
                e33 = E3.rearrange("p (h d) -> p h d", d=16)
                e43 = E4.rearrange("p (h d) -> p h d", d=16)
                P.op("ve", lambda v, o=e13, a=x1, b=cb: v.tensor_mul(o, a, b),
                     ins=[qt, ROPB], outs=[E1])
                P.op("ve", lambda v, o=e23, a=x2, b=sbb: v.tensor_mul(o, a, b),
                     ins=[qt, ROPB], outs=[E2])
                P.op("ve", lambda v, o=e33, a=x1, b=sbb: v.tensor_mul(o, a, b),
                     ins=[qt, ROPB], outs=[E3])
                P.op("ve", lambda v, o=e43, a=x2, b=cb: v.tensor_mul(o, a, b),
                     ins=[qt, ROPB], outs=[E4])
                P.op("ve", lambda v, o=x1, a=e13, b=e23: v.tensor_sub(o, a, b),
                     ins=[E1, E2], outs=[qt])
                P.op("ve", lambda v, o=x2, a=e33, b=e43: v.tensor_add(o, a, b),
                     ins=[E3, E4], outs=[qt])
            for half in range(2):
                kds = list(range(half * 4, half * 4 + 4))
                self.transpose_batch(
                    [(None, SCRQ[:, kd * 128:(kd + 1) * 128]) for kd in kds],
                    half, out_ap=QAS[:, half * 512: half * 512 + 512])
            P.dma("spwq", self.q_sp[:, :, tt * 128:(tt + 1) * 128].rearrange(
                "kt p c -> p kt c"),
                QAS.rearrange("p (kt c) -> p kt c", c=128), dbg=f"qsp{tt}")
            for half in range(2):
                kds = list(range(half * 4, half * 4 + 4))
                out = KFM.rearrange("p (kd t) -> p kd t", t=S)[
                    :, kds[0]:kds[0] + 4, tt * 128:tt * 128 + 128]
                self.transpose_batch(
                    [(None, SCRK[:, kd * 128:(kd + 1) * 128]) for kd in kds],
                    half, out_ap=out)
        for tt in range(NTT):
            xnt = xn_tiles(tt)
            for gi in range(2):
                wsl = WSEL[:, gi * 1024:(gi + 1) * 1024]
                psb = self.PS[4 + gi]
                self.mmgroup(psb[:, 0:128],
                             [(xnt[kt], wsl[:, kt * 128:(kt + 1) * 128])
                              for kt in range(KT)])
                sg = self.POOL[:, 16 + gi * 128:144 + gi * 128]
                P.op("sc", lambda s, o=sg, i=psb[:, 0:128]: s.activation(
                    o, i, AF.Sigmoid), ins=[psb[:, 0:128]], outs=[sg])
                m8 = self.POOL[:, 0:8] if gi == 0 else self.POOL[:, 8:16]
                dst = (GVF[:, tt * 128:(tt + 1) * 128] if gi == 0
                       else go[:, tt * 128:(tt + 1) * 128])
                for hh in range(16):
                    sgh = sg[:, hh * 8:(hh + 1) * 8]
                    P.op("ve", lambda v, o=m8, i=sgh: v.max(o, i),
                         ins=[sgh], outs=[m8])
                    P.op("ve", lambda v, o=dst[:, hh * 8:(hh + 1) * 8], i=sgh,
                         m=m8[:, 1:2]: v.scalar_tensor_tensor(
                         o, i, m, i, ALU.is_ge, ALU.mult),
                         ins=[sgh, m8], outs=[dst[:, hh * 8:(hh + 1) * 8]])

        VAF = BIGB[:, 8192:24832]
        va3 = VAF[:, 0:16 * NTT * 65].rearrange("p (g j) -> p g j", j=65)
        P.op("ve", lambda v, ap=va3[:, :, 64:65]: v.memset(ap, 1.0),
             outs=[va3[:, :, 64:65]], dbg="fva1")
        WVR = [BIGB[:, i * 4096:(i + 1) * 4096] for i in range(2)]
        VG_ = [self.POOL[:, 1088 + i * 512: 1088 + (i + 1) * 512] for i in range(2)]
        for h in range(16):
            wvr = WVR[h % 2]
            P.dma(("lwv", "lwq")[h % 2], wvr, self.wb["f_Wv"][h], dbg=f"fwv{h}")
            for tt in range(NTT):
                xnt = xn_tiles(tt)
                psb = self.PS[tt % 2]
                VG = VG_[tt % 2]
                self.mmgroup(psb[:, 0:512],
                             [(xnt[kt],
                               wvr[:, kt * 512:(kt + 1) * 512]) for kt in range(KT)])
                gvs = GVF[:, tt * 128 + h * 8: tt * 128 + (h + 1) * 8]
                gb = gvs.unsqueeze(2).broadcast_to([128, 8, 64])
                P.op("ve", lambda v, o=VG.rearrange("p (e d) -> p e d", d=64),
                     a=psb[:, 0:512].rearrange("p (e d) -> p e d", d=64), b=gb:
                     v.tensor_mul(o, a, b), ins=[psb[:, 0:512], gvs], outs=[VG])
                vr = self.POOL[:, 2112 + (tt % 2) * 64: 2176 + (tt % 2) * 64]
                P.op("ve", lambda v, o=vr.unsqueeze(2),
                     i=VG.rearrange("p (e d) -> p d e", e=8):
                     v.tensor_reduce(o, i, AX.X, ALU.add), ins=[VG], outs=[vr])
                dst = va3[:, h * NTT + tt, 0:64]
                P.op("sc", lambda s, o=dst, i=vr: s.copy(o, i), ins=[vr], outs=[dst])

        XSLB = BIGB[:, 0:8192]
        QP = [BIGB[:, 24832 + i * 2048: 24832 + (i + 1) * 2048] for i in range(2)]
        for h in range(16):
            hp, r0 = h // 2, (h % 2) * 64
            if h % 2 == 0:
                P.dma(("ss0", "ss1")[hp % 2], QP[hp % 2][:, 0:S],
                      self.q_sp[hp], dbg=f"qld{hp}")
            qp = QP[hp % 2]
            for sl in range(NTS):
                for st in range(NTT):
                    psb = self.PS[st % 4]
                    P.op("pe", lambda t, o=psb[:, 0:512],
                         a=KFM[r0:r0 + 64, hp * S + st * 128: hp * S + st * 128 + 128],
                         b=qp[r0:r0 + 64, sl * 512:(sl + 1) * 512]:
                         t.matmul(o, a, b, start=True, stop=True),
                         ins=[KFM[r0:r0 + 64, hp * S + st * 128: hp * S + st * 128 + 128],
                              qp[r0:r0 + 64, sl * 512:(sl + 1) * 512]],
                         outs=[psb[:, 0:512]], mark=True)
                    xt = XSLB[:, st * 512:(st + 1) * 512]
                    P.op("sc", lambda s, o=xt, i=psb[:, 0:512]: s.activation(
                        o, i, AF.Exp, scale=0.125), ins=[psb[:, 0:512]], outs=[xt])
                for sqc in range(4):
                    pso = self.PS[4 + sqc % 2]
                    self.mmgroup(pso[:, 0:65],
                                 [(XSLB[:, st * 512 + sqc * 128: st * 512 + sqc * 128 + 128],
                                   va3[:, h * NTT + st, :]) for st in range(NTT)])
                    dc = self.POOL[:, 1032 + sqc % 8:1033 + sqc % 8]
                    P.op("ve", lambda v, o=dc, i=pso[:, 64:65]: v.reciprocal(o, i),
                         ins=[pso[:, 64:65]], outs=[dc])
                    tt = sl * 4 + sqc
                    ot = A2[:, tt * 1024 + h * 64: tt * 1024 + h * 64 + 64]
                    P.op("ve", lambda v, o=ot, i=pso[:, 0:64], d=dc:
                         v.tensor_scalar_mul(o, i, d),
                         ins=[pso[:, 0:64], dc], outs=[ot])

        for kt in range(KT):
            P.dma("sxf", A1[:, kt * S:(kt + 1) * S], self.xnf_sp[kt], dbg="xnfld")
        OGF = BIGB[:, 0:16384]
        OGT_ = [BIGB[:, 16384 + i * 1024: 16384 + (i + 1) * 1024] for i in range(2)]
        WOH = BIGB[:, 18432:26624]
        for hg in range(8):
            P.dma(("lwo", "lwv")[hg % 2], WOH, self.wb["f_Wo"][hg], dbg=f"fwo{hg}")

            def outproj_sl(sl):
                for dt in range(KT):
                    psb = self.PS[dt % 4]
                    self.mmgroup(psb[:, 0:512],
                                 [(WOH[:, r * 1024 + dt * 128: r * 1024 + dt * 128 + 128],
                                   OGF[:, r * S + sl * 512: r * S + sl * 512 + 512])
                                  for r in range(KT)])
                    a1t = A1[:, dt * S + sl * 512: dt * S + sl * 512 + 512]
                    P.op("ve", lambda v, o=a1t, a=a1t, b=psb[:, 0:512]:
                         v.tensor_add(o, a, b),
                         ins=[a1t, psb[:, 0:512]], outs=[a1t])

            for tt in range(NTT):
                OGT = OGT_[tt % 2]
                for hh in range(2):
                    h = hg * 2 + hh
                    ov = A2[:, tt * 1024 + h * 64: tt * 1024 + h * 64 + 64]
                    gos = go[:, tt * 128 + h * 8: tt * 128 + (h + 1) * 8]
                    o_in = ov.unsqueeze(1).broadcast_to([128, 8, 64])
                    g_in = gos.unsqueeze(2).broadcast_to([128, 8, 64])
                    ogt = OGT[:, hh * 512:(hh + 1) * 512]
                    P.op("ve", lambda v, o=ogt.rearrange("p (e d) -> p e d", d=64),
                         a=o_in, b=g_in: v.tensor_mul(o, a, b),
                         ins=[ov, gos], outs=[ogt])
                for half in range(2):
                    rs = list(range(half * 4, half * 4 + 4))
                    out = OGF.rearrange("p (r t) -> p r t", t=S)[
                        :, rs[0]:rs[0] + 4, tt * 128:tt * 128 + 128]
                    self.transpose_batch(
                        [(None, OGT[:, r * 128:(r + 1) * 128]) for r in rs],
                        half, out_ap=out)
                if tt % 4 == 3:
                    outproj_sl(tt // 4)

        for i, k in enumerate(("ln1_g", "ln1_b", "ln2_g", "ln2_b", "b2")):
            P.dma("prm", self.biasA[:, i * 8:(i + 1) * 8], self.fb[k], dbg="fprm")
        P.dma("prm", self.biasA[:, 64:96], self.fb1, dbg="fprmb1")
        self.ln_fm(self.biasA[:, 0:8], self.biasA[:, 8:16], write_a2=True)
        w1_src = lambda m: self.wb["f_W1"][m]
        w2_src = lambda dt: self.wb["f_W2"][dt]
        self.ffn_mlp(w1_src, w2_src, self.biasA[:, 64:96], self.biasA[:, 32:40])
        self.ln_fm(self.biasA[:, 16:24], self.biasA[:, 24:32], write_a2=False)
        self.tap("ffn_g", seq)

    def _final(self, seq):
        cfg, P = self.cfg, self.P
        S, KT, NTT = cfg["S"], cfg["KT"], cfg["NTT"]
        A1 = self.A1
        CTF = self.CT.ap().bitcast(F32)
        for tt in range(NTT):
            par = tt % 2
            xt = CTF[:, 2048 + par * 1024: 2048 + par * 1024 + 1024]
            st_ = CTF[:, par * 1024: par * 1024 + 1024]
            P.dma(("x0", "x1")[par], xt,
                  self.x[seq, tt * 128:(tt + 1) * 128, :], dbg=f"fx{tt}")
            P.dma(("fs0", "fs1")[par],
                  st_.rearrange("p (kt c) -> p kt c", c=128),
                  self.s_sp[:, :, tt * 128:(tt + 1) * 128].rearrange(
                      "kt p c -> p kt c"), dbg=f"fsld{tt}")
            for kt in range(KT):
                k2 = kt % 2
                gtile = CTF[:, 4096 + k2 * 128: 4224 + k2 * 128]
                sg = CTF[:, 4352 + k2 * 128: 4480 + k2 * 128]
                ob = CTF[:, 4608 + k2 * 128: 4736 + k2 * 128]
                a1t = A1[:, kt * S + tt * 128: kt * S + tt * 128 + 128]
                P.op("sc", lambda s, o=gtile, i=a1t: s.activation(o, i, AF.Silu),
                     ins=[a1t], outs=[gtile])
                P.op("ve", lambda v, o=sg, a=st_[:, kt * 128:(kt + 1) * 128],
                     b=gtile: v.tensor_mul(o, a, b),
                     ins=[st_, gtile], outs=[sg])
                psf = self.PS[4 + k2]
                P.op("pe", lambda t, o=psf[:, 0:128], i=sg: t.transpose(
                    o, i, self.identf[:, :]),
                    ins=[sg, self.identf.ap()], outs=[psf[:, 0:128]], mark=True)
                P.op("ve", lambda v, o=ob, a=psf[:, 0:128],
                     b=xt[:, kt * 128:(kt + 1) * 128]: v.tensor_add(o, a, b),
                     ins=[psf[:, 0:128], xt], outs=[ob], mark=True)
                P.dma("oq", self.out[seq, tt * 128:(tt + 1) * 128,
                                     kt * 128:(kt + 1) * 128], ob, dbg="outw")



def kernel(**inputs):
    cfg = make_cfg()
    lay = host_layout(cfg, inputs)
    x = np.asarray(inputs["L0_attn_x"], dtype=np.float32)
    b = Build(cfg)
    nc = b.build()
    in_maps = []
    for core in range(8):
        m = dict(lay)
        m["x"] = np.ascontiguousarray(x[core * 4:(core + 1) * 4])
        in_maps.append(m)
    res = run_bass_kernel_spmd(nc, in_maps, core_ids=list(range(8)))
    return np.concatenate([r["out"] for r in res.results], axis=0)


# version 14
# speedup vs baseline: 1.0262x; 1.0262x over previous
import numpy as np
import ml_dtypes
import concourse.bass as bass
import concourse.mybir as mybir
from concourse.bass_utils import run_bass_kernel_spmd

F32 = mybir.dt.float32
BF16 = mybir.dt.bfloat16
AF = mybir.ActivationFunctionType
ALU = mybir.AluOpType
AX = mybir.AxisListType

LN_EPS = 1e-5
ATTN_EPS64 = float(np.float32(1e-6) * 64.0)


def make_cfg(S=2048, NSEQ=4, L=3, debug=()):
    D, DFF, H = 1024, 4096, 16
    C = 256
    assert S % C == 0 and S % 512 == 0
    return dict(
        S=S, NSEQ=NSEQ, L=L, D=D, DFF=DFF, H=H,
        KT=D // 128, MT=DFF // 128,
        C=C, NCH=S // C, PT=C // 128, NTT=S // 128, NTS=S // 512,
        debug=set(debug),
    )



class _Cmd:
    __slots__ = ("kind", "eng", "fn", "mark", "gidx", "sem", "val", "target",
                 "sem_key", "dbg")

    def __init__(self, kind, eng, fn=None, target=None, dbg=""):
        self.kind, self.eng, self.fn = kind, eng, fn
        self.mark = False
        self.gidx = 0
        self.sem = None
        self.val = None
        self.target = target
        self.sem_key = None
        self.dbg = dbg


class _Rec:
    __slots__ = ("p0", "p1", "b0", "b1", "writer", "readers")

    def __init__(self, p0, p1, b0, b1, writer):
        self.p0, self.p1, self.b0, self.b1 = p0, p1, b0, b1
        self.writer = writer
        self.readers = {}


def _ap_box(ap):
    t = ap.tensor
    esz = mybir.dt.size(ap.dtype)
    dims = list(ap.ap)
    off = ap.offset
    sk = str(ap.space)
    if ("SB" in sk) or ("sb" in sk) or ("PSUM" in sk) or ("Psum" in sk) or ("psum" in sk):
        row = dims[0][0] if dims[0][0] else 1
        p0 = off // row
        b_off = (off - p0 * row) * esz
        span = 1
        for st, ct in dims[1:]:
            span += (ct - 1) * abs(st)
        return (t.name, p0, p0 + dims[0][1], b_off, b_off + span * esz)
    span = 1
    for st, ct in dims:
        span += (ct - 1) * abs(st)
    return (t.name, 0, 1, off * esz, (off + span) * esz)


class Planner:
    def __init__(self, nc):
        self.nc = nc
        self.cmds = {e: [] for e in ("pe", "ve", "sc", "qs", "qg")}
        self.recs = {}
        self.psum_spaces = set()
        self.last_drain = {e: -1 for e in self.cmds}
        self.highwater = {}
        self.streams = {}
        self.stream_last = {}
        self.n_op = 0

    def stream(self, name, phys):
        if name not in self.streams:
            self.streams[name] = phys
        return name

    def _sync_to(self, eng, pcmd):
        if pcmd is None:
            return
        if pcmd.kind == "dma":
            self._wait(eng, pcmd)
        elif pcmd.eng == eng:
            if eng == "pe":
                return
            if self.last_drain[eng] >= pcmd.gidx:
                return
            c = _Cmd("drain", eng)
            c.gidx = len(self.cmds[eng])
            self.cmds[eng].append(c)
            self.last_drain[eng] = c.gidx
        else:
            self._wait(eng, pcmd)

    def _wait(self, eng, pcmd):
        if pcmd.kind == "dma":
            pcmd = self.stream_last[pcmd.sem_key]
        key = pcmd.sem_key if pcmd.kind == "dma" else pcmd.eng
        if key == eng and pcmd.kind != "dma":
            return
        pcmd.mark = True
        hw = self.highwater.get((eng, key), -1)
        if hw >= pcmd.gidx:
            return
        c = _Cmd("wait", eng, target=pcmd)
        c.gidx = len(self.cmds[eng])
        self.cmds[eng].append(c)
        self.highwater[(eng, key)] = pcmd.gidx

    def _box(self, ap):
        name, p0, p1, b0, b1 = _ap_box(ap)
        if name in self.psum_spaces:
            return (name, 0, 128, 0, 1 << 30)
        return (name, p0, p1, b0, b1)

    def _access(self, eng, aps_r, aps_w):
        for ap in aps_r:
            space, p0, p1, b0, b1 = self._box(ap)
            for r in self.recs.get(space, ()):
                if r.p0 < p1 and p0 < r.p1 and r.b0 < b1 and b0 < r.b1 \
                        and r.writer is not None:
                    self._sync_to(eng, r.writer)
        for ap in aps_w:
            space, p0, p1, b0, b1 = self._box(ap)
            recs = self.recs.get(space)
            if recs is None:
                continue
            keep = []
            for r in recs:
                if r.p0 < p1 and p0 < r.p1 and r.b0 < b1 and b0 < r.b1:
                    if r.writer is not None:
                        self._sync_to(eng, r.writer)
                    for rd in r.readers.values():
                        self._sync_to(eng, rd)
                    if r.p0 >= p0 and r.p1 <= p1 and r.b0 >= b0 and r.b1 <= b1:
                        continue
                    keep.append(r)
                else:
                    keep.append(r)
            self.recs[space] = keep

    def _record(self, cmd, aps_r, aps_w):
        key = cmd.sem_key if cmd.kind == "dma" else cmd.eng
        for ap in aps_w:
            space, p0, p1, b0, b1 = self._box(ap)
            self.recs.setdefault(space, []).append(_Rec(p0, p1, b0, b1, cmd))
        for ap in aps_r:
            space, p0, p1, b0, b1 = self._box(ap)
            lst = self.recs.setdefault(space, [])
            hit = False
            for r in lst:
                if r.p0 == p0 and r.p1 == p1 and r.b0 == b0 and r.b1 == b1:
                    r.readers[key] = cmd
                    hit = True
            if not hit:
                rec = _Rec(p0, p1, b0, b1, None)
                rec.readers[key] = cmd
                lst.append(rec)

    def op(self, eng, fn, ins=(), outs=(), mark=False, dbg=""):
        self._access(eng, ins, outs)
        c = _Cmd("op", eng, fn=fn, dbg=dbg)
        c.gidx = len(self.cmds[eng])
        c.mark = mark
        self.cmds[eng].append(c)
        self._record(c, ins, outs)
        self.n_op += 1
        return c

    def dma(self, stream, out_ap, in_ap, dbg=""):
        phys = self.streams[stream]
        self._access(phys, [in_ap], [out_ap])
        c = _Cmd("dma", phys, dbg=dbg)
        c.fn = (out_ap, in_ap)
        c.sem_key = "st:" + stream
        c.gidx = len(self.cmds[phys])
        c.mark = True
        self.cmds[phys].append(c)
        self.stream_last[c.sem_key] = c
        self._record(c, [in_ap], [out_ap])
        self.n_op += 1
        return c

    def emit(self):
        nc = self.nc
        import contextlib
        stack = contextlib.ExitStack()
        sems = {}
        for e in ("pe", "ve", "sc"):
            sems[e] = stack.enter_context(nc.semaphore("s_" + e))
        for name in self.streams:
            sems["st:" + name] = stack.enter_context(nc.semaphore("s_" + name))
        counters = {k: 0 for k in sems}
        for e, lst in self.cmds.items():
            for c in lst:
                if c.kind == "op" and c.mark:
                    counters[c.eng] += 1
                    c.sem = sems[c.eng]
                    c.val = counters[c.eng]
                elif c.kind == "dma":
                    counters[c.sem_key] += 16
                    c.sem = sems[c.sem_key]
                    c.val = counters[c.sem_key]

        def run_engine(e):
            def f(handle):
                for c in self.cmds[e]:
                    if c.kind == "wait":
                        handle.wait_ge(c.target.sem, c.target.val)
                    elif c.kind == "drain":
                        handle.drain()
                    elif c.kind == "dma":
                        out_ap, in_ap = c.fn
                        handle.dma_start(out_ap, in_ap).then_inc(c.sem, 16)
                    else:
                        inst = c.fn(handle)
                        if c.mark:
                            inst.then_inc(c.sem, 1)
            return f

        with nc.Block() as block:
            block.tensor(run_engine("pe"))
            block.vector(run_engine("ve"))
            block.scalar(run_engine("sc"))
            block.gpsimd(run_engine("qg"))
            block.sync(run_engine("qs"))
        stack.close()



def host_consts(cfg):
    S, H, D = cfg["S"], cfg["H"], cfg["D"]
    c = {}
    c["ident"] = np.eye(128, dtype=np.float32)
    c["ones_col"] = np.ones((128, 1), dtype=np.float32)
    c["ones_row"] = np.ones((1, 128), dtype=np.float32)
    sgn = np.ones((128, 1), dtype=np.float32)
    sgn[32:64] = -1.0
    sgn[96:128] = -1.0
    c["sgn"] = sgn
    tri = np.triu(np.ones((128, 128), dtype=np.float32))
    c["mask0"] = np.concatenate([tri, np.ones((128, 128), np.float32)], axis=1)
    E = np.zeros((D, H), dtype=np.float32)
    for d in range(D):
        E[d, d // 64] = 1.0
    c["E"] = E
    P2 = np.zeros((H, 8 * 128), dtype=np.float32)
    for hp in range(8):
        for r in range(128):
            P2[hp * 2 + r // 64, hp * 128 + r] = 1.0
    c["P2b"] = P2
    half = 16
    inv = 10000.0 ** (-np.arange(half, dtype=np.float64) / half)
    ang = np.arange(S, dtype=np.float64)[:, None] * inv[None, :]
    c["rope_cos"] = np.cos(ang).astype(np.float32)
    c["rope_sin"] = np.sin(ang).astype(np.float32)
    return c


def _pack_cols(v):
    sh = v.shape[:-1]
    n = v.shape[-1] // 128
    return np.ascontiguousarray(np.asarray(v).reshape(*sh, n, 128).swapaxes(-1, -2))


def host_layout(cfg, inputs):
    g = {k: np.asarray(v) for k, v in inputs.items()}
    D = cfg["D"]
    out = {}
    for k in ("Wq", "Wk", "Wv", "Wo"):
        out["a_" + k] = np.ascontiguousarray(g["L0_attn_" + k])
    L = g["L0_attn_W1"].shape[0]
    out["a_W1"] = np.ascontiguousarray(
        g["L0_attn_W1"].reshape(L, 8, 128, 32, 128).transpose(0, 3, 2, 1, 4)
        .reshape(L, 32, 128, 1024))
    out["a_W2"] = np.ascontiguousarray(
        g["L0_attn_W2"].reshape(L, 32, 128, 8, 128).transpose(0, 3, 2, 1, 4)
        .reshape(L, 8, 128, 4096))
    for k in ("bq", "bk", "bo", "ln1_g", "ln1_b", "ln2_g", "ln2_b", "bf2"):
        out["a_" + k] = _pack_cols(g["L0_attn_" + k])
    out["a_bf1"] = _pack_cols(g["L0_attn_bf1"])
    out["a_bv"] = np.ascontiguousarray(g["L0_attn_bv"])
    om = g["L0_attn_omega"]
    L = om.shape[0]
    om4 = np.zeros((L, 128, 128), dtype=np.float32)
    for l in range(L):
        o = om[l]
        om4[l, 0:64, 0:32] = o
        om4[l, 0:64, 32:64] = o
        om4[l, 64:128, 64:96] = o
        om4[l, 64:128, 96:128] = o
    out["a_omega4"] = om4
    out["f_Wqt"] = np.ascontiguousarray(g["L0_ffn_Wq"].transpose(1, 0, 2).reshape(D, 1024))
    out["f_Wkt"] = np.ascontiguousarray(g["L0_ffn_Wk"].transpose(1, 0, 2).reshape(D, 1024))
    out["f_Wselv"] = np.ascontiguousarray(g["L0_ffn_Wsel_v"].transpose(2, 0, 1).reshape(D, 128))
    out["f_Wselo"] = np.ascontiguousarray(g["L0_ffn_Wsel_o"].transpose(2, 0, 1).reshape(D, 128))
    wvr = g["L0_ffn_Wv"].transpose(0, 2, 1, 3).reshape(16, D, 512)
    out["f_Wv"] = np.ascontiguousarray(
        wvr.reshape(16, 8, 128, 512).transpose(0, 2, 1, 3).reshape(16, 128, 4096))
    woa = g["L0_ffn_Wo"].reshape(16 * 8 * 64, D)
    out["f_Wo"] = np.ascontiguousarray(
        woa.reshape(8, 8, 128, 1024).transpose(0, 2, 1, 3).reshape(8, 128, 8192))
    out["f_W1"] = np.ascontiguousarray(
        g["L0_ffn_W1"].reshape(8, 128, 32, 128).transpose(2, 1, 0, 3)
        .reshape(32, 128, 1024))
    out["f_W2"] = np.ascontiguousarray(
        g["L0_ffn_W2"].reshape(32, 128, 8, 128).transpose(2, 1, 0, 3)
        .reshape(8, 128, 4096))
    for k in ("ln1_g", "ln1_b", "ln2_g", "ln2_b", "b2"):
        out["f_" + k] = _pack_cols(g["L0_ffn_" + k])
    out["f_b1"] = _pack_cols(g["L0_ffn_b1"])
    out.update(host_consts(cfg))
    return out



class Build:
    def __init__(self, cfg):
        self.cfg = cfg
        self.nc = bass.Bass("TRN2", target_bir_lowering=False)
        self.P = Planner(self.nc)

    def dram_in(self, name, shape, dt=F32):
        return self.nc.dram_tensor(name, list(shape), dt, kind="ExternalInput").ap()

    def build(self):
        cfg = self.cfg
        nc, P = self.nc, self.P
        import contextlib
        self._st = contextlib.ExitStack()
        en = self._st.enter_context
        S, NSEQ, L, D, DFF = cfg["S"], cfg["NSEQ"], cfg["L"], cfg["D"], cfg["DFF"]
        KT, MT, H, NTT = cfg["KT"], cfg["MT"], cfg["H"], cfg["NTT"]

        self.x = self.dram_in("x", (NSEQ, S, D))
        self.out = nc.dram_tensor("out", [NSEQ, S, D], F32, kind="ExternalOutput").ap()
        self.aW = {k: self.dram_in("a_" + k, (L, D, D)) for k in ("Wq", "Wk", "Wv", "Wo")}
        self.aW1 = self.dram_in("a_W1", (L, 32, 128, 1024))
        self.aW2 = self.dram_in("a_W2", (L, 8, 128, 4096))
        self.ab = {k: self.dram_in("a_" + k, (L, 128, KT)) for k in
                   ("bq", "bk", "bo", "ln1_g", "ln1_b", "ln2_g", "ln2_b", "bf2")}
        self.abf1 = self.dram_in("a_bf1", (L, 128, MT))
        self.abv = self.dram_in("a_bv", (L, D))
        self.aom4 = self.dram_in("a_omega4", (L, 128, 128))
        self.fWqt = self.dram_in("f_Wqt", (D, 1024))
        self.fWkt = self.dram_in("f_Wkt", (D, 1024))
        self.fWselv = self.dram_in("f_Wselv", (D, 128))
        self.fWselo = self.dram_in("f_Wselo", (D, 128))
        self.fWv = self.dram_in("f_Wv", (16, 128, 4096))
        self.fWo = self.dram_in("f_Wo", (8, 128, 8192))
        self.fW1 = self.dram_in("f_W1", (32, 128, 1024))
        self.fW2 = self.dram_in("f_W2", (8, 128, 4096))
        self.fb = {k: self.dram_in("f_" + k, (128, KT)) for k in
                   ("ln1_g", "ln1_b", "ln2_g", "ln2_b", "b2")}
        self.fb1 = self.dram_in("f_b1", (128, MT))
        self.ident_in = self.dram_in("ident", (128, 128))
        self.onesc_in = self.dram_in("ones_col", (128, 1))
        self.onesr_in = self.dram_in("ones_row", (1, 128))
        self.sgn_in = self.dram_in("sgn", (128, 1))
        self.mask0_in = self.dram_in("mask0", (128, 256))
        self.E_in = self.dram_in("E", (D, H))
        self.P2b_in = self.dram_in("P2b", (H, 8 * 128))
        self.cos_in = self.dram_in("rope_cos", (S, 16))
        self.sin_in = self.dram_in("rope_sin", (S, 16))

        self.wb = {}
        mir = lambda nm, ap: self.wb.setdefault(
            nm, nc.dram_tensor("wb_" + nm, list(ap.shape), BF16, kind="Internal").ap())
        for k in ("Wq", "Wk", "Wv", "Wo"):
            mir("a_" + k, self.aW[k])
        mir("a_W1", self.aW1)
        mir("a_W2", self.aW2)
        mir("f_Wqt", self.fWqt)
        mir("f_Wkt", self.fWkt)
        mir("f_Wselv", self.fWselv)
        mir("f_Wselo", self.fWselo)
        mir("f_Wv", self.fWv)
        mir("f_Wo", self.fWo)
        mir("f_W1", self.fW1)
        mir("f_W2", self.fW2)

        self.xn_sp = nc.dram_tensor("xn_sp", [KT, 128, S], BF16, kind="Internal").ap()
        self.xnf_sp = nc.dram_tensor("xnf_sp", [KT, 128, S], F32, kind="Internal").ap()
        self.q_sp = nc.dram_tensor("q_sp", [KT, 128, S], BF16, kind="Internal").ap()
        self.s_sp = nc.dram_tensor("s_sp", [KT, 128, S], F32, kind="Internal").ap()

        sb = lambda n, sh, dt: en(nc.sbuf_tensor(n, list(sh), dt))
        self.A1 = sb("A1", (128, KT * S), F32)
        self.A2 = sb("A2", (128, KT * S), BF16)
        self.BIGB = sb("BIGB", (128, 32768), BF16)
        self.CT = sb("CT", (128, 16384), BF16)
        self.POOL = sb("POOL", (128, 2304), F32)
        self.identf = sb("identf", (128, 128), F32)
        self.identb = sb("identb", (128, 128), BF16)
        self.onesc = sb("onesc", (128, 1), BF16)
        self.onesr = sb("onesr", (1, 128), F32)
        self.onesrb = sb("onesrb", (1, 128), BF16)
        self.sgn = sb("sgn_s", (128, 1), F32)
        self.P2b = sb("P2b_s", (16, 8 * 128), BF16)
        self.om4 = sb("om4", (128, 128), BF16)
        self.bvrow = sb("bvrow", (1, 1024), BF16)
        self.biasA = sb("biasA", (128, 96), F32)
        self.MRS = sb("MRS", (128, 2 * NTT), F32)

        self.Ssb = self.POOL[:, 512:1032]
        self.Sbf = self.CT[:, 15392:15912]
        self.mask0 = self.CT[:, 15912:16168]
        self.E_sb = self.CT[:, 16168:16296]

        ps = lambda n, sh, dt: en(nc.psum_tensor(n, list(sh), dt))
        self.PS = [ps(f"PS{i}", (128, 512), F32) for i in range(6)]
        self.PB = [ps(f"PB{i}", (128, 1024), BF16) for i in range(2)]
        for t in list(self.PS) + list(self.PB):
            P.psum_spaces.add(t.name)

        for nm in ("cst", "cs2", "prm", "wo_", "om4s"):
            P.stream(nm, "qg")
        for nm, n0 in (("a_Wq", 3), ("a_Wk", 3), ("a_Wv", 3), ("a_Wo", 3),
                       ("a_W1", 3), ("a_W2", 3), ("f_Wqt", 1), ("f_Wkt", 1),
                       ("f_Wselv", 1), ("f_Wselo", 1), ("f_Wv", 4), ("f_Wo", 4),
                       ("f_W1", 4), ("f_W2", 4)):
            for i in range(n0):
                P.stream(f"pp_{nm}{i}", "qg")
        for nm in ("lwq", "lwk", "lwv", "lwo"):
            P.stream(nm, "qs")
        for i in range(4):
            P.stream(f"r1{i}", "qs")
        for i in range(2):
            P.stream(f"r2{i}", "qs")
        for nm in ("cq", "x0", "x1", "spwx", "spwf", "spws", "spwq",
                   "sr0", "sr1", "sxf", "ss0", "ss1", "fs0", "fs1", "oq"):
            P.stream(nm, "qs")

        self.dbgmap = {}
        for name in cfg["debug"]:
            self.dbgmap[name] = nc.dram_tensor(
                "dbg_" + name, [NSEQ, 128, KT * S], F32, kind="ExternalOutput").ap()

        self._consts()
        for seq in range(NSEQ):
            self._phase0(seq)
            self._attn_branch(seq)
            self._ffn_branch(seq)
            self._final(seq)

        for c in reversed(P.cmds["qs"]):
            if c.kind == "dma":
                P._wait("ve", c)
                break
        P.emit()
        self._st.close()
        return nc

    def tap(self, name, seq):
        if name not in self.dbgmap:
            return
        self.P.dma("oq", self.dbgmap[name][seq], self.A1.ap(), dbg="tap" + name)

    def mmgroup(self, bank_ap, mms, mark_last=True, dbg=""):
        P = self.P
        n = len(mms)
        last = None
        for i, (l_ap, r_ap) in enumerate(mms):
            outs = [bank_ap] if (i == 0 or i == n - 1) else []
            st, sp = (i == 0), (i == n - 1)
            last = P.op("pe", lambda t, o=bank_ap, a=l_ap, b=r_ap, s1=st, s2=sp:
                        t.matmul(o, a, b, start=s1, stop=s2),
                        ins=[l_ap, r_ap], outs=outs, mark=(sp and mark_last), dbg=dbg)
        return last

    def transpose_batch(self, pairs, pb_idx, out_ap=None):
        P = self.P
        pb = self.PB[pb_idx % 2]
        n = len(pairs)
        for i, (d, s_) in enumerate(pairs):
            P.op("pe", lambda t, o=pb[:, i * 128:(i + 1) * 128], i_=s_:
                 t.transpose(o, i_, self.identb[:, :]),
                 ins=[s_, self.identb.ap()],
                 outs=[pb[:, i * 128:(i + 1) * 128]], mark=(i == n - 1))
        eng = ("ve", "sc")[pb_idx % 2]
        if out_ap is not None:
            src_ap = pb[:, 0:n * 128]
            if eng == "sc":
                P.op("sc", lambda s, o=out_ap, i=src_ap: s.copy(o, i),
                     ins=[src_ap], outs=[out_ap])
            else:
                P.op("ve", lambda v, o=out_ap, i=src_ap: v.tensor_copy(o, i),
                     ins=[src_ap], outs=[out_ap])
        else:
            for i, (d, s_) in enumerate(pairs):
                t_ap = pb[:, i * 128:(i + 1) * 128]
                if eng == "sc":
                    P.op("sc", lambda s, o=d, i_=t_ap: s.copy(o, i_),
                         ins=[t_ap], outs=[d])
                else:
                    P.op("ve", lambda v, o=d, i_=t_ap: v.tensor_copy(o, i_),
                         ins=[t_ap], outs=[d])

    def transpose_b(self, dst_ap, src_ap, pb_idx, eng=None):
        P = self.P
        pb = self.PB[pb_idx % 2]
        if eng is None:
            eng = ("ve", "sc")[pb_idx % 2]
        P.op("pe", lambda t, o=pb[:, 0:128], i=src_ap: t.transpose(
            o, i, self.identb[:, :]),
            ins=[src_ap, self.identb.ap()], outs=[pb[:, 0:128]], mark=True)
        if eng == "sc":
            P.op("sc", lambda s, o=dst_ap, i=pb[:, 0:128]: s.copy(o, i),
                 ins=[pb[:, 0:128]], outs=[dst_ap])
        else:
            P.op("ve", lambda v, o=dst_ap, i=pb[:, 0:128]: v.tensor_copy(o, i),
                 ins=[pb[:, 0:128]], outs=[dst_ap])

    def rstd_inplace(self, row_ap):
        P = self.P
        P.op("sc", lambda s, o=row_ap: s.activation(o, o, AF.Ln),
             ins=[row_ap], outs=[row_ap])
        P.op("sc", lambda s, o=row_ap: s.activation(o, o, AF.Exp, scale=-0.5),
             ins=[row_ap], outs=[row_ap])

    def _consts(self):
        P, cfg = self.P, self.cfg
        NTT = cfg["NTT"]
        P.dma("cq", self.identf[:, :], self.ident_in)
        P.dma("cst", self.identb[:, :], self.ident_in)
        P.dma("cst", self.onesc[:, :], self.onesc_in)
        P.dma("cq", self.onesr[:, :], self.onesr_in)
        P.dma("cst", self.onesrb[:, :], self.onesr_in)
        P.dma("cq", self.sgn[:, :], self.sgn_in)
        P.dma("cst", self.P2b[:, :], self.P2b_in)
        P.op("ve", lambda v, o=self.P2b[:, :]: v.tensor_scalar_mul(o, o, -1.0),
             ins=[self.P2b.ap()], outs=[self.P2b.ap()], dbg="negP2b")
        srcs = {"a_Wq": self.aW["Wq"], "a_Wk": self.aW["Wk"],
                "a_Wv": self.aW["Wv"], "a_Wo": self.aW["Wo"],
                "a_W1": self.aW1, "a_W2": self.aW2,
                "f_Wqt": self.fWqt, "f_Wkt": self.fWkt,
                "f_Wselv": self.fWselv, "f_Wselo": self.fWselo,
                "f_Wv": self.fWv, "f_Wo": self.fWo,
                "f_W1": self.fW1, "f_W2": self.fW2}
        order = ["a_Wq", "a_Wk", "a_Wv", "a_Wo", "a_W1", "a_W2",
                 "f_Wqt", "f_Wkt", "f_Wselv", "f_Wselo", "f_Wv", "f_Wo",
                 "f_W1", "f_W2"]
        nslice = {"a_Wq": 3, "a_Wk": 3, "a_Wv": 3, "a_Wo": 3, "a_W1": 3,
                  "a_W2": 3, "f_Wqt": 1, "f_Wkt": 1, "f_Wselv": 1,
                  "f_Wselo": 1, "f_Wv": 4, "f_Wo": 4, "f_W1": 4, "f_W2": 4}

        def ppk_emit(nm, i):
            s_ap, d_ap = srcs[nm], self.wb[nm]
            ns = min(nslice[nm], s_ap.shape[0])
            if ns == 1:
                P.dma(f"pp_{nm}0", d_ap, s_ap, dbg=f"ppk_{nm}")
                return
            step = s_ap.shape[0] // ns
            P.dma(f"pp_{nm}{i}", d_ap[i * step:(i + 1) * step],
                  s_ap[i * step:(i + 1) * step], dbg=f"ppk_{nm}{i}")

        self.deferred_ppk = []
        for nm in order:
            ns = min(nslice[nm], srcs[nm].shape[0])
            for i in range(ns):
                if nm in ("a_Wq", "a_Wk", "a_Wv", "a_Wo") and i == 0:
                    ppk_emit(nm, i)
                else:
                    self.deferred_ppk.append((nm, i))
        self._ppk_emit = ppk_emit

    def _phase0(self, seq):
        cfg, P = self.cfg, self.P
        S, D, KT, NTT = cfg["S"], cfg["D"], cfg["KT"], cfg["NTT"]
        A1, A2 = self.A1, self.A2
        for tt in range(NTT):
            xs = ("x0", "x1")[tt % 2]
            xt = self.POOL[:, (tt % 2) * 1024:(tt % 2) * 1024 + 1024]
            xn = self.CT[:, 1024 + (tt % 2) * 3072: 2048 + (tt % 2) * 3072].bitcast(F32)
            P.dma(xs, xt, self.x[seq, tt * 128:(tt + 1) * 128, :], dbg=f"x{tt}")
            mcol = self.MRS[:, tt:tt + 1]
            rcol = self.MRS[:, NTT + tt:NTT + tt + 1]
            P.op("ve", lambda v, o=mcol, i=xt: v.tensor_reduce(
                o.unsqueeze(2), i.rearrange("p (a n) -> p a n", a=1), AX.X, ALU.add),
                ins=[xt], outs=[mcol], dbg="p0mean")
            P.op("ve", lambda v, o=mcol: v.tensor_scalar_mul(o, o, -1.0 / D),
                 ins=[mcol], outs=[mcol])
            sq = self.CT[:, (tt % 2) * 3072: (tt % 2) * 3072 + 1024].bitcast(F32)
            ctb = self.CT[:, 2560 + (tt % 2) * 3072: 2576 + (tt % 2) * 3072].bitcast(F32)
            vcol = ctb[:, 0:1]
            vcol2 = ctb[:, 1:2]
            P.op("sc", lambda s, o=sq, i=xt[:, 0:512], b=mcol, a=vcol: s.activation(
                o, i, AF.Square, bias=b, scale=1.0, accum_out=a),
                ins=[xt[:, 0:512], mcol], outs=[sq, vcol], dbg="p0sq0")
            P.op("sc", lambda s, o=sq, i=xt[:, 512:1024], b=mcol, a=vcol2: s.activation(
                o, i, AF.Square, bias=b, scale=1.0, accum_out=a),
                ins=[xt[:, 512:1024], mcol], outs=[sq, vcol2], dbg="p0sq1")
            P.op("ve", lambda v, o=vcol, a=vcol, b=vcol2: v.tensor_add(o, a, b),
                 ins=[vcol, vcol2], outs=[vcol])
            P.op("ve", lambda v, o=vcol: v.tensor_scalar(
                o, o, 1.0 / D, LN_EPS, ALU.mult, ALU.add), ins=[vcol], outs=[vcol])
            self.rstd_inplace(vcol)
            P.op("ve", lambda v, o=rcol, i=vcol: v.tensor_copy(o, i),
                 ins=[vcol], outs=[rcol], dbg="p0rstd")
            for half in range(2):
                xh = xt[:, half * 512:(half + 1) * 512]
                P.op("ve", lambda v, o=xn, i=xh, m=mcol, r=rcol: v.tensor_scalar(
                    o, i, m, r, ALU.add, ALU.mult),
                    ins=[xh, mcol, rcol], outs=[xn], dbg="p0xn")
                xnb = self.CT[:, 2048 + (tt % 2) * 3072: 2560 + (tt % 2) * 3072]
                P.op("sc", lambda s, o=xnb, i=xn: s.copy(o, i), ins=[xn], outs=[xnb])
                out = A2.ap().rearrange("p (kt t) -> p kt t", t=S)[
                    :, half * 4: half * 4 + 4, tt * 128:tt * 128 + 128]
                self.transpose_batch(
                    [(None, xnb[:, k * 128:(k + 1) * 128]) for k in range(4)],
                    half, out_ap=out)
                for k in range(4):
                    kt = half * 4 + k
                    psf = self.PS[kt % 2]
                    src_f = xn[:, k * 128:(k + 1) * 128]
                    dst_f = A1[:, kt * S + tt * 128: kt * S + tt * 128 + 128]
                    P.op("pe", lambda t, o=psf[:, 0:128], i=src_f: t.transpose(
                        o, i, self.identf[:, :]),
                        ins=[src_f, self.identf.ap()], outs=[psf[:, 0:128]],
                        mark=True, dbg="p0trf")
                    P.op("sc", lambda s, o=dst_f, i=psf[:, 0:128]: s.copy(o, i),
                         ins=[psf[:, 0:128]], outs=[dst_f])
        for kt in range(KT):
            P.dma("spwx", self.xn_sp[kt], A2[:, kt * S:(kt + 1) * S], dbg="xnsp")
            P.dma("spwf", self.xnf_sp[kt], A1[:, kt * S:(kt + 1) * S], dbg="xnfsp")
        self.tap("p0_xn", seq)

    def ln_fm(self, gcol, bcol, write_a2, spill_s=False):
        cfg, P = self.cfg, self.P
        S, D, KT, NTS = cfg["S"], cfg["D"], cfg["KT"], cfg["NTS"]
        A1, A2 = self.A1, self.A2
        ROW2 = self.CT[:, 0:8192].bitcast(F32)
        ROW3 = self.CT[:, 8192:12288].bitcast(F32)
        mrow = ROW2[0:1, 0:S]
        vrow = ROW2[0:1, 2048:2048 + S]
        for ts in range(NTS):
            c0 = ts * 512
            for kt in range(KT):
                a1t = A1[:, kt * S + c0: kt * S + c0 + 512]
                a2t = A2[:, kt * S + c0: kt * S + c0 + 512]
                P.op("sc", lambda s, o=a2t, i=a1t: s.copy(o, i), ins=[a1t], outs=[a2t])
                sqt = self.CT[:, 8192 + kt * 512: 8192 + kt * 512 + 512]
                P.op("ve", lambda v, o=sqt, a=a2t, b=a2t: v.tensor_mul(o, a, b),
                     ins=[a2t], outs=[sqt])
            pssum = self.PS[2]
            pssq = self.PS[3]
            self.mmgroup(pssum[0:1, 0:512],
                         [(self.onesc[:, :], A2[:, kt * S + c0: kt * S + c0 + 512])
                          for kt in range(KT)])
            self.mmgroup(pssq[0:1, 0:512],
                         [(self.onesc[:, :],
                           self.CT[:, 8192 + kt * 512: 8192 + kt * 512 + 512])
                          for kt in range(KT)])
            P.op("ve", lambda v, o=mrow[0:1, c0:c0 + 512], i=pssum[0:1, 0:512]:
                 v.tensor_scalar_mul(o, i, -1.0 / D),
                 ins=[pssum[0:1, 0:512]], outs=[mrow[0:1, c0:c0 + 512]])
            P.op("ve", lambda v, o=vrow[0:1, c0:c0 + 512], i=pssq[0:1, 0:512]:
                 v.tensor_scalar_mul(o, i, 1.0 / D),
                 ins=[pssq[0:1, 0:512]], outs=[vrow[0:1, c0:c0 + 512]])
        m2 = ROW3[0:1, 0:S]
        P.op("sc", lambda s, o=m2, i=mrow: s.activation(o, i, AF.Square),
             ins=[mrow], outs=[m2])
        P.op("ve", lambda v, o=vrow, a=vrow, b=m2: v.tensor_sub(o, a, b),
             ins=[vrow, m2], outs=[vrow])
        P.op("ve", lambda v, o=vrow: v.tensor_scalar_add(o, o, LN_EPS),
             ins=[vrow], outs=[vrow])
        self.rstd_inplace(vrow)
        P.op("ve", lambda v, o=mrow, a=mrow, b=vrow: v.tensor_mul(o, a, b),
             ins=[mrow, vrow], outs=[mrow])
        for ts in range(NTS):
            c0 = ts * 512
            psr = self.PS[4]
            psm = self.PS[5]
            P.op("pe", lambda t, o=psr[:, 0:512], a=self.onesr[:, :],
                 b=vrow[0:1, c0:c0 + 512]: t.matmul(o, a, b, start=True, stop=True),
                 ins=[self.onesr.ap(), vrow[0:1, c0:c0 + 512]],
                 outs=[psr[:, 0:512]], mark=True)
            P.op("pe", lambda t, o=psm[:, 0:512], a=self.onesr[:, :],
                 b=mrow[0:1, c0:c0 + 512]: t.matmul(o, a, b, start=True, stop=True),
                 ins=[self.onesr.ap(), mrow[0:1, c0:c0 + 512]],
                 outs=[psm[:, 0:512]], mark=True)
            for kt in range(KT):
                a1t = A1[:, kt * S + c0: kt * S + c0 + 512]
                P.op("ve", lambda v, o=a1t, i=a1t, r=psr[:, 0:512]:
                     v.tensor_mul(o, i, r), ins=[a1t, psr[:, 0:512]], outs=[a1t])
                P.op("ve", lambda v, o=a1t, i=a1t, m=psm[:, 0:512]:
                     v.tensor_add(o, i, m), ins=[a1t, psm[:, 0:512]], outs=[a1t])
                g1 = gcol[:, kt:kt + 1]
                b1 = bcol[:, kt:kt + 1]
                P.op("ve", lambda v, o=a1t, i=a1t, g=g1, b=b1: v.tensor_scalar(
                    o, i, g, b, ALU.mult, ALU.add), ins=[a1t, g1, b1], outs=[a1t])
                if write_a2:
                    a2t = A2[:, kt * S + c0: kt * S + c0 + 512]
                    P.op("sc", lambda s, o=a2t, i=a1t: s.copy(o, i),
                         ins=[a1t], outs=[a2t])
                if spill_s:
                    P.dma("spws", self.s_sp[kt][:, c0:c0 + 512], a1t, dbg="ssp")

    def ffn_mlp(self, w1_src, w2_src, b1col, b2col):
        cfg, P = self.cfg, self.P
        S, KT, MT = cfg["S"], cfg["KT"], cfg["MT"]
        A1, A2, HID = self.A1, self.A2, self.BIGB
        tsw = min(1024, S)
        for tsl in range(S // tsw):
            for m in range(MT):
                slot = m % 4
                r1 = self.CT[:, slot * 1024: slot * 1024 + 1024]
                P.dma(f"r1{slot}", r1, w1_src(m), dbg=f"w1m{m}")
                for nh in range(tsw // 512):
                    psb = self.PS[nh % 2]
                    rhs0 = tsl * tsw + nh * 512
                    self.mmgroup(psb[:, 0:512],
                                 [(r1[:, kt * 128:(kt + 1) * 128],
                                   A2[:, kt * S + rhs0: kt * S + rhs0 + 512])
                                  for kt in range(KT)], dbg=f"W1m{m}nh{nh}")
                    ht = HID[:, m * tsw + nh * 512: m * tsw + nh * 512 + 512]
                    bc = b1col[:, m:m + 1]
                    P.op("sc", lambda s, o=ht, i=psb[:, 0:512], b=bc:
                         s.activation(o, i, AF.Relu, bias=b),
                         ins=[psb[:, 0:512], bc], outs=[ht])
            for dt in range(KT):
                slot = dt % 2
                r2 = self.CT[:, 4096 + slot * 4096: 4096 + slot * 4096 + 4096]
                P.dma(f"r2{slot}", r2, w2_src(dt), dbg=f"w2d{dt}")
                for nh in range(tsw // 512):
                    psb = self.PS[2 + nh % 2]
                    self.mmgroup(psb[:, 0:512],
                                 [(r2[:, m * 128:(m + 1) * 128],
                                   HID[:, m * tsw + nh * 512: m * tsw + nh * 512 + 512])
                                  for m in range(MT)])
                    c0 = tsl * tsw + nh * 512
                    a1t = A1[:, dt * S + c0: dt * S + c0 + 512]
                    bc = b2col[:, dt:dt + 1]
                    P.op("ve", lambda v, o=a1t, i=psb[:, 0:512], b=bc, r=a1t:
                         v.scalar_tensor_tensor(o, i, b, r, ALU.add, ALU.add),
                         ins=[psb[:, 0:512], bc, a1t], outs=[a1t])

    def _attn_branch(self, seq):
        P, cfg = self.P, self.cfg
        P.dma("cs2", self.mask0, self.mask0_in)
        P.dma("cs2", self.E_sb.rearrange("p (kt h) -> p kt h", kt=cfg["KT"]),
              self.E_in.rearrange("(kt p) h -> p kt h", p=128))
        for l in range(self.cfg["L"]):
            self._attn_layer(seq, l)

    def _attn_layer(self, seq, l):
        cfg, P = self.cfg, self.P
        S, KT, H = cfg["S"], cfg["KT"], cfg["H"]
        C, NCH, PT = cfg["C"], cfg["NCH"], cfg["PT"]
        A1, A2, CT = self.A1, self.A2, self.CT
        c4 = float(64.0 ** -0.25)

        pm = ("bq", "bk", "bo", "ln1_g", "ln1_b", "ln2_g", "ln2_b", "bf2")
        for i, k in enumerate(pm):
            P.dma("prm", self.biasA[:, i * 8:(i + 1) * 8], self.ab[k][l], dbg="prm" + k)
        P.dma("prm", self.biasA[:, 64:96], self.abf1[l], dbg="prmbf1")
        P.dma("om4s", self.om4[:, :], self.aom4[l], dbg="om4")
        for p0, c0 in ((0, 32), (64, 96)):
            sl = self.om4[p0:p0 + 64, c0:c0 + 32]
            P.op("ve", lambda v, o=sl: v.tensor_scalar_mul(o, o, -1.0),
                 ins=[sl], outs=[sl])
        P.dma("wo_", self.bvrow[:, :], self.abv[l].unsqueeze(0), dbg="bvrow")
        for i in (0, 1):
            ba = self.biasA[:, i * 8:(i + 1) * 8]
            P.op("ve", lambda v, o=ba: v.tensor_scalar_mul(o, o, c4),
                 ins=[ba], outs=[ba])

        WQ = self.BIGB[:, 0:8192]
        WK = self.BIGB[:, 8192:16384]
        WV = self.BIGB[:, 16384:24576]
        WO = self.BIGB[:, 24576:32768]
        for nm, buf, src in (("lwq", WQ, self.wb["a_Wq"]), ("lwk", WK, self.wb["a_Wk"]),
                             ("lwv", WV, self.wb["a_Wv"]), ("lwo", WO, self.wb["a_Wo"])):
            for kt in range(KT):
                P.dma(nm, buf[:, kt * 1024:(kt + 1) * 1024],
                      src[l, kt * 128:(kt + 1) * 128, :], dbg=nm + str(kt))

        XS = CT[:, 0:2048]
        SQc = CT[:, 2048:4096]
        QFc = CT[:, 4096:6144]
        KFc = CT[:, 6144:8192]
        KTm = CT[:, 8192:10240]
        ATm = CT[:, 10240:12288]
        ATs_ = [CT[:, 12288 + i * 384: 12288 + (i + 1) * 384] for i in range(2)]
        VAch = CT[:, 13056:15136]
        HLb = CT[:, 15136:15392]
        UF = self.POOL[:, 0:256]
        HL = self.POOL[0:16, 256:512]
        va3 = VAch[:, 0:PT * 16 * 65].rearrange("p (g j) -> p g j", j=65)
        P.op("ve", lambda v, ap=va3[:, :, 64:65]: v.memset(ap, 1.0),
             outs=[va3[:, :, 64:65]], dbg="va1")

        def favor(out_buf, bias_i):
            psh = self.PS[2]
            self.mmgroup(psh[0:16, 0:C],
                         [(self.E_sb[:, kt * 16:(kt + 1) * 16],
                           SQc[:, kt * C:(kt + 1) * C]) for kt in range(KT)])
            P.op("sc", lambda s, o=HLb[0:16, 0:C], i=psh[0:16, 0:C]:
                 s.activation(o, i, AF.Identity, scale=0.5),
                 ins=[psh[0:16, 0:C]], outs=[HLb[0:16, 0:C]])
            for hp in range(8):
                psu = self.PS[3 + hp % 3]
                P.op("pe", lambda t, o=psu[:, 0:C], a=self.om4[:, :],
                     b=XS[:, hp * C:(hp + 1) * C]: t.matmul(o, a, b, start=True, stop=False),
                     ins=[self.om4.ap(), XS[:, hp * C:(hp + 1) * C]],
                     outs=[psu[:, 0:C]])
                P.op("pe", lambda t, o=psu[:, 0:C],
                     a=self.P2b[:, hp * 128:(hp + 1) * 128], b=HLb[0:16, 0:C]:
                     t.matmul(o, a, b, start=False, stop=True),
                     ins=[self.P2b[:, hp * 128:(hp + 1) * 128], HLb[0:16, 0:C]],
                     outs=[psu[:, 0:C]], mark=True)
                ot = out_buf[:, hp * C:(hp + 1) * C]
                P.op("sc", lambda s, o=ot, i=psu[:, 0:C]: s.activation(o, i, AF.Exp),
                     ins=[psu[:, 0:C]], outs=[ot])

        for ch in range(NCH):
            base = ch * C
            if ch == 1 and self.deferred_ppk:
                for nm_i in self.deferred_ppk:
                    self._ppk_emit(*nm_i)
                self.deferred_ppk = []
            for side, Wbuf, bia, outf, bq in (("q", WQ, 0, QFc, 0), ("k", WK, 1, KFc, 2)):
                for mt in range(KT):
                    psb = self.PS[mt % 4]
                    self.mmgroup(psb[:, 0:C],
                                 [(Wbuf[:, kt * 1024 + mt * 128: kt * 1024 + mt * 128 + 128],
                                   A2[:, kt * S + base: kt * S + base + C])
                                  for kt in range(KT)])
                    bc = self.biasA[:, bia * 8 + mt: bia * 8 + mt + 1]
                    xst = XS[:, mt * C:(mt + 1) * C]
                    sqt = SQc[:, mt * C:(mt + 1) * C]
                    P.op("sc", lambda s, o=xst, i=psb[:, 0:C], b=bc:
                         s.activation(o, i, AF.Identity, bias=b, scale=c4),
                         ins=[psb[:, 0:C], bc], outs=[xst])
                    P.op("sc", lambda s, o=sqt, i=psb[:, 0:C], b=bc:
                         s.activation(o, i, AF.Square, bias=b, scale=c4),
                         ins=[psb[:, 0:C], bc], outs=[sqt])
                favor(outf, bia)
                if side == "q":
                    for pt in range(PT):
                        for nh in range(2):
                            psb = self.PS[nh]
                            mms = [(A2[:, kt * S + base + pt * 128: kt * S + base + pt * 128 + 128],
                                    WV[:, kt * 1024 + nh * 512: kt * 1024 + nh * 512 + 512])
                                   for kt in range(KT)]
                            mms.append((self.onesrb[0:1, 0:128],
                                        self.bvrow[0:1, nh * 512: nh * 512 + 512]))
                            self.mmgroup(psb[:, 0:512], mms)
                            dst = va3[:, pt * 16 + nh * 8: pt * 16 + (nh + 1) * 8, 0:64]
                            P.op("sc", lambda s, o=dst, i=psb[:, 0:512].rearrange(
                                "p (h d) -> p h d", d=64): s.copy(o, i),
                                ins=[psb[:, 0:512]], outs=[dst])
            for pt in range(PT):
                for half in range(2):
                    mts = range(half * 4, half * 4 + 4)
                    self.transpose_batch(
                        [(None, KFc[:, mt * C + pt * 128: mt * C + pt * 128 + 128])
                         for mt in mts], half,
                        out_ap=KTm[:, pt * 1024 + half * 512: pt * 1024 + half * 512 + 512])
            def head_at(h):
                hp, r0 = h // 2, (h % 2) * 64
                ATs = ATs_[h % 2]
                qf = lambda c0, cn: QFc[r0:r0 + 64, hp * C + c0: hp * C + c0 + cn]
                kf = lambda c0, cn: KFc[r0:r0 + 64, hp * C + c0: hp * C + c0 + cn]
                for skp in range(PT):
                    psb = self.PS[skp % 2]
                    ncols = C - skp * 128
                    P.op("pe", lambda t, o=psb[:, 0:ncols], a=kf(skp * 128, 128),
                         b=qf(skp * 128, ncols): t.matmul(o, a, b, start=True, stop=True),
                         ins=[kf(skp * 128, 128), qf(skp * 128, ncols)],
                         outs=[psb[:, 0:ncols]], mark=True)
                    at = ATs[:, skp * 256: skp * 256 + ncols]
                    P.op("ve", lambda v, o=at, i=psb[:, 0:ncols],
                         m=self.mask0[:, 0:ncols]: v.tensor_mul(o, i, m),
                         ins=[psb[:, 0:ncols], self.mask0[:, 0:ncols]], outs=[at])

            def head_rest(h, ch):
                hp, r0 = h // 2, (h % 2) * 64
                ATs = ATs_[h % 2]
                qf = lambda c0, cn: QFc[r0:r0 + 64, hp * C + c0: hp * C + c0 + cn]
                Sb = self.Sbf[r0:r0 + 64, hp * 65: hp * 65 + 65]
                Sf = self.Ssb[r0:r0 + 64, hp * 65: hp * 65 + 65]
                for pq in range(PT):
                    pso = self.PS[2 + pq % 2]
                    mms = []
                    for skp in range(pq + 1):
                        lhs = ATs[:, skp * 256 + (pq - skp) * 128:
                                  skp * 256 + (pq - skp) * 128 + 128]
                        rhs = va3[:, skp * 16 + h, :]
                        mms.append((lhs, rhs))
                    if ch > 0:
                        mms.append((qf(pq * 128, 128), Sb))
                    self.mmgroup(pso[:, 0:65], mms)
                    slot = (h * 2 + pq) % 4
                    oa = self.POOL[:, 1056 + slot * 66: 1056 + slot * 66 + 65]
                    P.op("ve", lambda v, o=oa, i=pso[:, 0:65]: v.tensor_copy(o, i),
                         ins=[pso[:, 0:65]], outs=[oa])
                    dc = self.POOL[:, 1032 + slot: 1033 + slot]
                    P.op("ve", lambda v, o=dc, i=oa[:, 64:65]: v.tensor_scalar_add(
                        o, i, ATTN_EPS64), ins=[oa], outs=[dc])
                    P.op("ve", lambda v, o=dc: v.reciprocal(o, o), ins=[dc], outs=[dc])
                    av = ATm[:, pq * 1024 + h * 64: pq * 1024 + h * 64 + 64]
                    P.op("ve", lambda v, o=av, i=oa[:, 0:64], d=dc:
                         v.tensor_scalar_mul(o, i, d),
                         ins=[oa, dc], outs=[av])
                pss = self.PS[4 + h % 2]
                mms = [(KTm[:, pt * 1024 + hp * 128 + r0: pt * 1024 + hp * 128 + r0 + 64],
                        va3[:, pt * 16 + h, :]) for pt in range(PT)]
                self.mmgroup(pss[r0:r0 + 64, 0:65], mms)
                if ch == 0:
                    P.op("ve", lambda v, o=Sf, i=pss[r0:r0 + 64, 0:65]:
                         v.tensor_copy(o, i), ins=[pss[r0:r0 + 64, 0:65]], outs=[Sf])
                else:
                    P.op("ve", lambda v, o=Sf, a=Sf, b=pss[r0:r0 + 64, 0:65]:
                         v.tensor_add(o, a, b),
                         ins=[Sf, pss[r0:r0 + 64, 0:65]], outs=[Sf])
                P.op("sc", lambda s, o=Sb, i=Sf: s.copy(o, i), ins=[Sf], outs=[Sb])

            for h in range(H):
                head_at(h)
                if h > 0:
                    head_rest(h - 1, ch)
            head_rest(H - 1, ch)
            AFm = QFc
            for pt in range(PT):
                for half in range(2):
                    ats = list(range(half * 4, half * 4 + 4))
                    out = AFm.rearrange("p (at c) -> p at c", c=C)[
                        :, ats[0]:ats[0] + 4, pt * 128:pt * 128 + 128]
                    self.transpose_batch(
                        [(None, ATm[:, pt * 1024 + at * 128: pt * 1024 + at * 128 + 128])
                         for at in ats], half, out_ap=out)
            for dt in range(KT):
                psb = self.PS[dt % 4]
                self.mmgroup(psb[:, 0:C],
                             [(WO[:, at * 1024 + dt * 128: at * 1024 + dt * 128 + 128],
                               AFm[:, at * C:(at + 1) * C]) for at in range(KT)])
                a1t = A1[:, dt * S + base: dt * S + base + C]
                bc = self.biasA[:, 16 + dt: 16 + dt + 1]
                P.op("ve", lambda v, o=a1t, i=psb[:, 0:C], b=bc, r=a1t:
                     v.scalar_tensor_tensor(o, i, b, r, ALU.add, ALU.add),
                     ins=[psb[:, 0:C], bc, a1t], outs=[a1t])

        self.ln_fm(self.biasA[:, 24:32], self.biasA[:, 32:40], write_a2=True)
        w1_src = lambda m: self.wb["a_W1"][l, m]
        w2_src = lambda dt: self.wb["a_W2"][l, dt]
        self.ffn_mlp(w1_src, w2_src, self.biasA[:, 64:96], self.biasA[:, 56:64])
        last = (l == self.cfg["L"] - 1)
        self.ln_fm(self.biasA[:, 40:48], self.biasA[:, 48:56],
                   write_a2=not last, spill_s=last)
        if last:
            self.tap("attn_s", seq)

    def _ffn_branch(self, seq):
        cfg, P = self.cfg, self.P
        S, KT, H, NTT, NTS = cfg["S"], cfg["KT"], cfg["H"], cfg["NTT"], cfg["NTS"]
        A1, A2, CT, BIGB = self.A1, self.A2, self.CT, self.BIGB

        WQT = BIGB[:, 0:8192]
        WKT = BIGB[:, 8192:16384]
        WSEL = BIGB[:, 16384:18432]
        XNALL = A1.ap().bitcast(BF16)
        QAS = BIGB[:, 20480:21504]
        ROPB = BIGB[:, 21504:21504 + 32 * NTT]
        SCR = BIGB[:, 22528:24064]
        GVF = BIGB[:, 28160:30208]
        go = BIGB[:, 30208:32256]
        KFM = CT

        for kt in range(KT):
            P.dma("lwq", WQT[:, kt * 1024:(kt + 1) * 1024],
                  self.wb["f_Wqt"][kt * 128:(kt + 1) * 128, :], dbg="fwq")
            P.dma("lwk", WKT[:, kt * 1024:(kt + 1) * 1024],
                  self.wb["f_Wkt"][kt * 128:(kt + 1) * 128, :], dbg="fwk")
        P.dma("lwo", WSEL[:, 0:1024].rearrange("p (kt c) -> p kt c", c=128),
              self.wb["f_Wselv"].rearrange("(kt p) c -> p kt c", p=128), dbg="fwsv")
        P.dma("lwo", WSEL[:, 1024:2048].rearrange("p (kt c) -> p kt c", c=128),
              self.wb["f_Wselo"].rearrange("(kt p) c -> p kt c", p=128), dbg="fwso")
        P.dma("wo_", ROPB[:, 0:16 * NTT].rearrange("p (t j) -> p t j", j=16),
              self.cos_in.rearrange("(t p) j -> p t j", p=128), dbg="ropc")
        P.dma("wo_", ROPB[:, 16 * NTT:32 * NTT].rearrange("p (t j) -> p t j", j=16),
              self.sin_in.rearrange("(t p) j -> p t j", p=128), dbg="rops")

        for kt in range(KT):
            P.dma("sr0", XNALL[:, kt * S:(kt + 1) * S], self.xn_sp[kt], dbg="xnall")

        def xn_tiles(tt):
            return [XNALL[:, kt * S + tt * 128: kt * S + tt * 128 + 128]
                    for kt in range(KT)]

        SCRQ = SCR[:, 0:1024]
        SCRK = BIGB[:, 24064:25088]
        E1 = SCR[:, 1024:1280]
        E2 = SCR[:, 1280:1536]
        for tt in range(NTT):
            xnt = xn_tiles(tt)
            cosb = ROPB[:, tt * 16: tt * 16 + 16]
            sinb = ROPB[:, 16 * NTT + tt * 16: 16 * NTT + tt * 16 + 16]
            cb = cosb.unsqueeze(1).broadcast_to([128, 16, 16])
            sbb = sinb.unsqueeze(1).broadcast_to([128, 16, 16])
            for which, wt, qt, b0 in (("q", WQT, SCRQ, 0), ("k", WKT, SCRK, 2)):
                for nh in range(2):
                    psb = self.PS[b0 + nh]
                    self.mmgroup(psb[:, 0:512],
                                 [(xnt[kt],
                                   wt[:, kt * 1024 + nh * 512: kt * 1024 + nh * 512 + 512])
                                  for kt in range(KT)])
                    P.op("sc", lambda s, o=qt[:, nh * 512:(nh + 1) * 512],
                         i=psb[:, 0:512]: s.copy(o, i),
                         ins=[psb[:, 0:512]], outs=[qt[:, nh * 512:(nh + 1) * 512]])
            E3 = BIGB[:, 25088:25344]
            E4 = BIGB[:, 25344:25600]
            for qt in (SCRQ, SCRK):
                q3 = qt.rearrange("p (h d) -> p h d", d=64)
                x1 = q3[:, :, 0:16]
                x2 = q3[:, :, 16:32]
                e13 = E1.rearrange("p (h d) -> p h d", d=16)
                e23 = E2.rearrange("p (h d) -> p h d", d=16)
                e33 = E3.rearrange("p (h d) -> p h d", d=16)
                e43 = E4.rearrange("p (h d) -> p h d", d=16)
                P.op("ve", lambda v, o=e13, a=x1, b=cb: v.tensor_mul(o, a, b),
                     ins=[qt, ROPB], outs=[E1])
                P.op("ve", lambda v, o=e23, a=x2, b=sbb: v.tensor_mul(o, a, b),
                     ins=[qt, ROPB], outs=[E2])
                P.op("ve", lambda v, o=e33, a=x1, b=sbb: v.tensor_mul(o, a, b),
                     ins=[qt, ROPB], outs=[E3])
                P.op("ve", lambda v, o=e43, a=x2, b=cb: v.tensor_mul(o, a, b),
                     ins=[qt, ROPB], outs=[E4])
                P.op("ve", lambda v, o=x1, a=e13, b=e23: v.tensor_sub(o, a, b),
                     ins=[E1, E2], outs=[qt])
                P.op("ve", lambda v, o=x2, a=e33, b=e43: v.tensor_add(o, a, b),
                     ins=[E3, E4], outs=[qt])
            for half in range(2):
                kds = list(range(half * 4, half * 4 + 4))
                self.transpose_batch(
                    [(None, SCRQ[:, kd * 128:(kd + 1) * 128]) for kd in kds],
                    half, out_ap=QAS[:, half * 512: half * 512 + 512])
            P.dma("spwq", self.q_sp[:, :, tt * 128:(tt + 1) * 128].rearrange(
                "kt p c -> p kt c"),
                QAS.rearrange("p (kt c) -> p kt c", c=128), dbg=f"qsp{tt}")
            for half in range(2):
                kds = list(range(half * 4, half * 4 + 4))
                out = KFM.rearrange("p (kd t) -> p kd t", t=S)[
                    :, kds[0]:kds[0] + 4, tt * 128:tt * 128 + 128]
                self.transpose_batch(
                    [(None, SCRK[:, kd * 128:(kd + 1) * 128]) for kd in kds],
                    half, out_ap=out)
        for tt in range(NTT):
            xnt = xn_tiles(tt)
            for gi in range(2):
                wsl = WSEL[:, gi * 1024:(gi + 1) * 1024]
                psb = self.PS[4 + gi]
                self.mmgroup(psb[:, 0:128],
                             [(xnt[kt], wsl[:, kt * 128:(kt + 1) * 128])
                              for kt in range(KT)])
                sg = self.POOL[:, 16 + gi * 128:144 + gi * 128]
                P.op("sc", lambda s, o=sg, i=psb[:, 0:128]: s.activation(
                    o, i, AF.Sigmoid), ins=[psb[:, 0:128]], outs=[sg])
                m8 = self.POOL[:, 0:8] if gi == 0 else self.POOL[:, 8:16]
                dst = (GVF[:, tt * 128:(tt + 1) * 128] if gi == 0
                       else go[:, tt * 128:(tt + 1) * 128])
                for hh in range(16):
                    sgh = sg[:, hh * 8:(hh + 1) * 8]
                    P.op("ve", lambda v, o=m8, i=sgh: v.max(o, i),
                         ins=[sgh], outs=[m8])
                    P.op("ve", lambda v, o=dst[:, hh * 8:(hh + 1) * 8], i=sgh,
                         m=m8[:, 1:2]: v.scalar_tensor_tensor(
                         o, i, m, i, ALU.is_ge, ALU.mult),
                         ins=[sgh, m8], outs=[dst[:, hh * 8:(hh + 1) * 8]])

        VAF = BIGB[:, 8192:24832]
        va3 = VAF[:, 0:16 * NTT * 65].rearrange("p (g j) -> p g j", j=65)
        P.op("ve", lambda v, ap=va3[:, :, 64:65]: v.memset(ap, 1.0),
             outs=[va3[:, :, 64:65]], dbg="fva1")
        WVR = [BIGB[:, i * 4096:(i + 1) * 4096] for i in range(2)]
        VG_ = [self.POOL[:, 1088 + i * 512: 1088 + (i + 1) * 512] for i in range(2)]
        for h in range(16):
            wvr = WVR[h % 2]
            P.dma(("lwv", "lwq")[h % 2], wvr, self.wb["f_Wv"][h], dbg=f"fwv{h}")
            for tt in range(NTT):
                xnt = xn_tiles(tt)
                psb = self.PS[tt % 2]
                VG = VG_[tt % 2]
                self.mmgroup(psb[:, 0:512],
                             [(xnt[kt],
                               wvr[:, kt * 512:(kt + 1) * 512]) for kt in range(KT)])
                gvs = GVF[:, tt * 128 + h * 8: tt * 128 + (h + 1) * 8]
                gb = gvs.unsqueeze(2).broadcast_to([128, 8, 64])
                P.op("ve", lambda v, o=VG.rearrange("p (e d) -> p e d", d=64),
                     a=psb[:, 0:512].rearrange("p (e d) -> p e d", d=64), b=gb:
                     v.tensor_mul(o, a, b), ins=[psb[:, 0:512], gvs], outs=[VG])
                vr = self.POOL[:, 2112 + (tt % 2) * 64: 2176 + (tt % 2) * 64]
                P.op("ve", lambda v, o=vr.unsqueeze(2),
                     i=VG.rearrange("p (e d) -> p d e", e=8):
                     v.tensor_reduce(o, i, AX.X, ALU.add), ins=[VG], outs=[vr])
                dst = va3[:, h * NTT + tt, 0:64]
                P.op("sc", lambda s, o=dst, i=vr: s.copy(o, i), ins=[vr], outs=[dst])

        XSLB = BIGB[:, 0:8192]
        QP = [BIGB[:, 24832 + i * 2048: 24832 + (i + 1) * 2048] for i in range(2)]
        for h in range(16):
            hp, r0 = h // 2, (h % 2) * 64
            if h % 2 == 0:
                P.dma(("ss0", "ss1")[hp % 2], QP[hp % 2][:, 0:S],
                      self.q_sp[hp], dbg=f"qld{hp}")
            qp = QP[hp % 2]
            for sl in range(NTS):
                for st in range(NTT):
                    psb = self.PS[st % 4]
                    P.op("pe", lambda t, o=psb[:, 0:512],
                         a=KFM[r0:r0 + 64, hp * S + st * 128: hp * S + st * 128 + 128],
                         b=qp[r0:r0 + 64, sl * 512:(sl + 1) * 512]:
                         t.matmul(o, a, b, start=True, stop=True),
                         ins=[KFM[r0:r0 + 64, hp * S + st * 128: hp * S + st * 128 + 128],
                              qp[r0:r0 + 64, sl * 512:(sl + 1) * 512]],
                         outs=[psb[:, 0:512]], mark=True)
                    xt = XSLB[:, st * 512:(st + 1) * 512]
                    P.op("sc", lambda s, o=xt, i=psb[:, 0:512]: s.activation(
                        o, i, AF.Exp, scale=0.125), ins=[psb[:, 0:512]], outs=[xt])
                for sqc in range(4):
                    pso = self.PS[4 + sqc % 2]
                    self.mmgroup(pso[:, 0:65],
                                 [(XSLB[:, st * 512 + sqc * 128: st * 512 + sqc * 128 + 128],
                                   va3[:, h * NTT + st, :]) for st in range(NTT)])
                    dc = self.POOL[:, 1032 + sqc % 8:1033 + sqc % 8]
                    P.op("ve", lambda v, o=dc, i=pso[:, 64:65]: v.reciprocal(o, i),
                         ins=[pso[:, 64:65]], outs=[dc])
                    tt = sl * 4 + sqc
                    ot = A2[:, tt * 1024 + h * 64: tt * 1024 + h * 64 + 64]
                    P.op("ve", lambda v, o=ot, i=pso[:, 0:64], d=dc:
                         v.tensor_scalar_mul(o, i, d),
                         ins=[pso[:, 0:64], dc], outs=[ot])

        for kt in range(KT):
            P.dma("sxf", A1[:, kt * S:(kt + 1) * S], self.xnf_sp[kt], dbg="xnfld")
        OGF = BIGB[:, 0:16384]
        OGT_ = [BIGB[:, 16384 + i * 1024: 16384 + (i + 1) * 1024] for i in range(2)]
        WOH = BIGB[:, 18432:26624]
        for hg in range(8):
            P.dma(("lwo", "lwv")[hg % 2], WOH, self.wb["f_Wo"][hg], dbg=f"fwo{hg}")

            def outproj_sl(sl):
                for dt in range(KT):
                    psb = self.PS[dt % 4]
                    self.mmgroup(psb[:, 0:512],
                                 [(WOH[:, r * 1024 + dt * 128: r * 1024 + dt * 128 + 128],
                                   OGF[:, r * S + sl * 512: r * S + sl * 512 + 512])
                                  for r in range(KT)])
                    a1t = A1[:, dt * S + sl * 512: dt * S + sl * 512 + 512]
                    P.op("ve", lambda v, o=a1t, a=a1t, b=psb[:, 0:512]:
                         v.tensor_add(o, a, b),
                         ins=[a1t, psb[:, 0:512]], outs=[a1t])

            for tt in range(NTT):
                OGT = OGT_[tt % 2]
                for hh in range(2):
                    h = hg * 2 + hh
                    ov = A2[:, tt * 1024 + h * 64: tt * 1024 + h * 64 + 64]
                    gos = go[:, tt * 128 + h * 8: tt * 128 + (h + 1) * 8]
                    o_in = ov.unsqueeze(1).broadcast_to([128, 8, 64])
                    g_in = gos.unsqueeze(2).broadcast_to([128, 8, 64])
                    ogt = OGT[:, hh * 512:(hh + 1) * 512]
                    P.op("ve", lambda v, o=ogt.rearrange("p (e d) -> p e d", d=64),
                         a=o_in, b=g_in: v.tensor_mul(o, a, b),
                         ins=[ov, gos], outs=[ogt])
                for half in range(2):
                    rs = list(range(half * 4, half * 4 + 4))
                    out = OGF.rearrange("p (r t) -> p r t", t=S)[
                        :, rs[0]:rs[0] + 4, tt * 128:tt * 128 + 128]
                    self.transpose_batch(
                        [(None, OGT[:, r * 128:(r + 1) * 128]) for r in rs],
                        half, out_ap=out)
                if tt % 4 == 3:
                    outproj_sl(tt // 4)

        for i, k in enumerate(("ln1_g", "ln1_b", "ln2_g", "ln2_b", "b2")):
            P.dma("prm", self.biasA[:, i * 8:(i + 1) * 8], self.fb[k], dbg="fprm")
        P.dma("prm", self.biasA[:, 64:96], self.fb1, dbg="fprmb1")
        self.ln_fm(self.biasA[:, 0:8], self.biasA[:, 8:16], write_a2=True)
        w1_src = lambda m: self.wb["f_W1"][m]
        w2_src = lambda dt: self.wb["f_W2"][dt]
        self.ffn_mlp(w1_src, w2_src, self.biasA[:, 64:96], self.biasA[:, 32:40])
        self.ln_fm(self.biasA[:, 16:24], self.biasA[:, 24:32], write_a2=False)
        self.tap("ffn_g", seq)

    def _final(self, seq):
        cfg, P = self.cfg, self.P
        S, KT, NTT = cfg["S"], cfg["KT"], cfg["NTT"]
        A1 = self.A1
        CTF = self.CT.ap().bitcast(F32)
        for tt in range(NTT):
            par = tt % 2
            xt = CTF[:, 2048 + par * 1024: 2048 + par * 1024 + 1024]
            st_ = CTF[:, par * 1024: par * 1024 + 1024]
            P.dma(("x0", "x1")[par], xt,
                  self.x[seq, tt * 128:(tt + 1) * 128, :], dbg=f"fx{tt}")
            P.dma(("fs0", "fs1")[par],
                  st_.rearrange("p (kt c) -> p kt c", c=128),
                  self.s_sp[:, :, tt * 128:(tt + 1) * 128].rearrange(
                      "kt p c -> p kt c"), dbg=f"fsld{tt}")
            for kt in range(KT):
                k2 = kt % 2
                gtile = CTF[:, 4096 + k2 * 128: 4224 + k2 * 128]
                sg = CTF[:, 4352 + k2 * 128: 4480 + k2 * 128]
                ob = CTF[:, 4608 + k2 * 128: 4736 + k2 * 128]
                a1t = A1[:, kt * S + tt * 128: kt * S + tt * 128 + 128]
                P.op("sc", lambda s, o=gtile, i=a1t: s.activation(o, i, AF.Silu),
                     ins=[a1t], outs=[gtile])
                P.op("ve", lambda v, o=sg, a=st_[:, kt * 128:(kt + 1) * 128],
                     b=gtile: v.tensor_mul(o, a, b),
                     ins=[st_, gtile], outs=[sg])
                psf = self.PS[4 + k2]
                P.op("pe", lambda t, o=psf[:, 0:128], i=sg: t.transpose(
                    o, i, self.identf[:, :]),
                    ins=[sg, self.identf.ap()], outs=[psf[:, 0:128]], mark=True)
                P.op("ve", lambda v, o=ob, a=psf[:, 0:128],
                     b=xt[:, kt * 128:(kt + 1) * 128]: v.tensor_add(o, a, b),
                     ins=[psf[:, 0:128], xt], outs=[ob], mark=True)
                P.dma("oq", self.out[seq, tt * 128:(tt + 1) * 128,
                                     kt * 128:(kt + 1) * 128], ob, dbg="outw")



def kernel(**inputs):
    cfg = make_cfg()
    lay = host_layout(cfg, inputs)
    x = np.asarray(inputs["L0_attn_x"], dtype=np.float32)
    b = Build(cfg)
    nc = b.build()
    in_maps = []
    for core in range(8):
        m = dict(lay)
        m["x"] = np.ascontiguousarray(x[core * 4:(core + 1) * 4])
        in_maps.append(m)
    res = run_bass_kernel_spmd(nc, in_maps, core_ids=list(range(8)))
    return np.concatenate([r["out"] for r in res.results], axis=0)


# version 17
# speedup vs baseline: 1.0742x; 1.0469x over previous
import numpy as np
import ml_dtypes
import concourse.bass as bass
import concourse.mybir as mybir
from concourse.bass_utils import run_bass_kernel_spmd

F32 = mybir.dt.float32
BF16 = mybir.dt.bfloat16
AF = mybir.ActivationFunctionType
ALU = mybir.AluOpType
AX = mybir.AxisListType

LN_EPS = 1e-5
ATTN_EPS64 = float(np.float32(1e-6) * 64.0)


def make_cfg(S=2048, NSEQ=4, L=3, debug=()):
    D, DFF, H = 1024, 4096, 16
    C = 256
    assert S % C == 0 and S % 512 == 0
    return dict(
        S=S, NSEQ=NSEQ, L=L, D=D, DFF=DFF, H=H,
        KT=D // 128, MT=DFF // 128,
        C=C, NCH=S // C, PT=C // 128, NTT=S // 128, NTS=S // 512,
        debug=set(debug),
    )



class _Cmd:
    __slots__ = ("kind", "eng", "fn", "mark", "gidx", "sem", "val", "target",
                 "sem_key", "dbg")

    def __init__(self, kind, eng, fn=None, target=None, dbg=""):
        self.kind, self.eng, self.fn = kind, eng, fn
        self.mark = False
        self.gidx = 0
        self.sem = None
        self.val = None
        self.target = target
        self.sem_key = None
        self.dbg = dbg


class _Rec:
    __slots__ = ("p0", "p1", "b0", "b1", "writer", "readers")

    def __init__(self, p0, p1, b0, b1, writer):
        self.p0, self.p1, self.b0, self.b1 = p0, p1, b0, b1
        self.writer = writer
        self.readers = {}


def _ap_box(ap):
    t = ap.tensor
    esz = mybir.dt.size(ap.dtype)
    dims = list(ap.ap)
    off = ap.offset
    sk = str(ap.space)
    if ("SB" in sk) or ("sb" in sk) or ("PSUM" in sk) or ("Psum" in sk) or ("psum" in sk):
        row = dims[0][0] if dims[0][0] else 1
        p0 = off // row
        b_off = (off - p0 * row) * esz
        span = 1
        for st, ct in dims[1:]:
            span += (ct - 1) * abs(st)
        return (t.name, p0, p0 + dims[0][1], b_off, b_off + span * esz)
    span = 1
    for st, ct in dims:
        span += (ct - 1) * abs(st)
    return (t.name, 0, 1, off * esz, (off + span) * esz)


class Planner:
    def __init__(self, nc):
        self.nc = nc
        self.cmds = {e: [] for e in ("pe", "ve", "sc", "qs", "qg")}
        self.recs = {}
        self.psum_spaces = set()
        self.last_drain = {e: -1 for e in self.cmds}
        self.highwater = {}
        self.streams = {}
        self.stream_last = {}
        self.n_op = 0

    def stream(self, name, phys):
        if name not in self.streams:
            self.streams[name] = phys
        return name

    def _sync_to(self, eng, pcmd):
        if pcmd is None:
            return
        if pcmd.kind == "dma":
            self._wait(eng, pcmd)
        elif pcmd.eng == eng:
            if eng == "pe":
                return
            if self.last_drain[eng] >= pcmd.gidx:
                return
            c = _Cmd("drain", eng)
            c.gidx = len(self.cmds[eng])
            self.cmds[eng].append(c)
            self.last_drain[eng] = c.gidx
        else:
            self._wait(eng, pcmd)

    def _wait(self, eng, pcmd):
        if pcmd.kind == "dma":
            pcmd = self.stream_last[pcmd.sem_key]
        key = pcmd.sem_key if pcmd.kind == "dma" else pcmd.eng
        if key == eng and pcmd.kind != "dma":
            return
        pcmd.mark = True
        hw = self.highwater.get((eng, key), -1)
        if hw >= pcmd.gidx:
            return
        c = _Cmd("wait", eng, target=pcmd)
        c.gidx = len(self.cmds[eng])
        self.cmds[eng].append(c)
        self.highwater[(eng, key)] = pcmd.gidx

    def _box(self, ap):
        name, p0, p1, b0, b1 = _ap_box(ap)
        if name in self.psum_spaces:
            return (name, 0, 128, 0, 1 << 30)
        return (name, p0, p1, b0, b1)

    def _access(self, eng, aps_r, aps_w):
        for ap in aps_r:
            space, p0, p1, b0, b1 = self._box(ap)
            for r in self.recs.get(space, ()):
                if r.p0 < p1 and p0 < r.p1 and r.b0 < b1 and b0 < r.b1 \
                        and r.writer is not None:
                    self._sync_to(eng, r.writer)
        for ap in aps_w:
            space, p0, p1, b0, b1 = self._box(ap)
            recs = self.recs.get(space)
            if recs is None:
                continue
            keep = []
            for r in recs:
                if r.p0 < p1 and p0 < r.p1 and r.b0 < b1 and b0 < r.b1:
                    if r.writer is not None:
                        self._sync_to(eng, r.writer)
                    for rd in r.readers.values():
                        self._sync_to(eng, rd)
                    if r.p0 >= p0 and r.p1 <= p1 and r.b0 >= b0 and r.b1 <= b1:
                        continue
                    keep.append(r)
                else:
                    keep.append(r)
            self.recs[space] = keep

    def _record(self, cmd, aps_r, aps_w):
        key = cmd.sem_key if cmd.kind == "dma" else cmd.eng
        for ap in aps_w:
            space, p0, p1, b0, b1 = self._box(ap)
            self.recs.setdefault(space, []).append(_Rec(p0, p1, b0, b1, cmd))
        for ap in aps_r:
            space, p0, p1, b0, b1 = self._box(ap)
            lst = self.recs.setdefault(space, [])
            hit = False
            for r in lst:
                if r.p0 == p0 and r.p1 == p1 and r.b0 == b0 and r.b1 == b1:
                    r.readers[key] = cmd
                    hit = True
            if not hit:
                rec = _Rec(p0, p1, b0, b1, None)
                rec.readers[key] = cmd
                lst.append(rec)

    def op(self, eng, fn, ins=(), outs=(), mark=False, dbg=""):
        self._access(eng, ins, outs)
        c = _Cmd("op", eng, fn=fn, dbg=dbg)
        c.gidx = len(self.cmds[eng])
        c.mark = mark
        self.cmds[eng].append(c)
        self._record(c, ins, outs)
        self.n_op += 1
        return c

    def dma(self, stream, out_ap, in_ap, dbg=""):
        phys = self.streams[stream]
        self._access(phys, [in_ap], [out_ap])
        c = _Cmd("dma", phys, dbg=dbg)
        c.fn = (out_ap, in_ap)
        c.sem_key = "st:" + stream
        c.gidx = len(self.cmds[phys])
        c.mark = True
        self.cmds[phys].append(c)
        self.stream_last[c.sem_key] = c
        self._record(c, [in_ap], [out_ap])
        self.n_op += 1
        return c

    def emit(self):
        nc = self.nc
        import contextlib
        stack = contextlib.ExitStack()
        sems = {}
        for e in ("pe", "ve", "sc"):
            sems[e] = stack.enter_context(nc.semaphore("s_" + e))
        for name in self.streams:
            sems["st:" + name] = stack.enter_context(nc.semaphore("s_" + name))
        counters = {k: 0 for k in sems}
        for e, lst in self.cmds.items():
            for c in lst:
                if c.kind == "op" and c.mark:
                    counters[c.eng] += 1
                    c.sem = sems[c.eng]
                    c.val = counters[c.eng]
                elif c.kind == "dma":
                    counters[c.sem_key] += 16
                    c.sem = sems[c.sem_key]
                    c.val = counters[c.sem_key]

        def run_engine(e):
            def f(handle):
                for c in self.cmds[e]:
                    if c.kind == "wait":
                        handle.wait_ge(c.target.sem, c.target.val)
                    elif c.kind == "drain":
                        handle.drain()
                    elif c.kind == "dma":
                        out_ap, in_ap = c.fn
                        handle.dma_start(out_ap, in_ap).then_inc(c.sem, 16)
                    else:
                        inst = c.fn(handle)
                        if c.mark:
                            inst.then_inc(c.sem, 1)
            return f

        with nc.Block() as block:
            block.tensor(run_engine("pe"))
            block.vector(run_engine("ve"))
            block.scalar(run_engine("sc"))
            block.gpsimd(run_engine("qg"))
            block.sync(run_engine("qs"))
        stack.close()



def host_consts(cfg):
    S, H, D = cfg["S"], cfg["H"], cfg["D"]
    c = {}
    c["ident"] = np.eye(128, dtype=np.float32)
    c["ones_col"] = np.ones((128, 1), dtype=np.float32)
    c["ones_row"] = np.ones((1, 128), dtype=np.float32)
    sgn = np.ones((128, 1), dtype=np.float32)
    sgn[32:64] = -1.0
    sgn[96:128] = -1.0
    c["sgn"] = sgn
    tri = np.triu(np.ones((128, 128), dtype=np.float32))
    c["mask0"] = np.concatenate([tri, np.ones((128, 128), np.float32)], axis=1)
    E = np.zeros((D, H), dtype=np.float32)
    for d in range(D):
        E[d, d // 64] = 1.0
    c["E"] = E
    P2 = np.zeros((H, 8 * 128), dtype=np.float32)
    for hp in range(8):
        for r in range(128):
            P2[hp * 2 + r // 64, hp * 128 + r] = 1.0
    c["P2b"] = P2
    half = 16
    inv = 10000.0 ** (-np.arange(half, dtype=np.float64) / half)
    ang = np.arange(S, dtype=np.float64)[:, None] * inv[None, :]
    c["rope_cos"] = np.cos(ang).astype(np.float32)
    c["rope_sin"] = np.sin(ang).astype(np.float32)
    return c


def _pack_cols(v):
    sh = v.shape[:-1]
    n = v.shape[-1] // 128
    return np.ascontiguousarray(np.asarray(v).reshape(*sh, n, 128).swapaxes(-1, -2))


def host_layout(cfg, inputs):
    g = {k: np.asarray(v) for k, v in inputs.items()}
    D = cfg["D"]
    out = {}
    for k in ("Wq", "Wk", "Wv", "Wo"):
        out["a_" + k] = np.ascontiguousarray(g["L0_attn_" + k])
    L = g["L0_attn_W1"].shape[0]
    out["a_W1"] = np.ascontiguousarray(
        g["L0_attn_W1"].reshape(L, 8, 128, 32, 128).transpose(0, 3, 2, 1, 4)
        .reshape(L, 32, 128, 1024))
    out["a_W2"] = np.ascontiguousarray(
        g["L0_attn_W2"].reshape(L, 32, 128, 8, 128).transpose(0, 3, 2, 1, 4)
        .reshape(L, 8, 128, 4096))
    for k in ("bq", "bk", "bo", "ln1_g", "ln1_b", "ln2_g", "ln2_b", "bf2"):
        out["a_" + k] = _pack_cols(g["L0_attn_" + k])
    out["a_bf1"] = _pack_cols(g["L0_attn_bf1"])
    out["a_bv"] = np.ascontiguousarray(g["L0_attn_bv"])
    om = g["L0_attn_omega"]
    L = om.shape[0]
    om4 = np.zeros((L, 128, 128), dtype=np.float32)
    for l in range(L):
        o = om[l]
        om4[l, 0:64, 0:32] = o
        om4[l, 0:64, 32:64] = o
        om4[l, 64:128, 64:96] = o
        om4[l, 64:128, 96:128] = o
    out["a_omega4"] = om4
    out["f_Wqt"] = np.ascontiguousarray(g["L0_ffn_Wq"].transpose(1, 0, 2).reshape(D, 1024))
    out["f_Wkt"] = np.ascontiguousarray(g["L0_ffn_Wk"].transpose(1, 0, 2).reshape(D, 1024))
    out["f_Wselv"] = np.ascontiguousarray(g["L0_ffn_Wsel_v"].transpose(2, 0, 1).reshape(D, 128))
    out["f_Wselo"] = np.ascontiguousarray(g["L0_ffn_Wsel_o"].transpose(2, 0, 1).reshape(D, 128))
    wvr = g["L0_ffn_Wv"].transpose(0, 2, 1, 3).reshape(16, D, 512)
    out["f_Wv"] = np.ascontiguousarray(
        wvr.reshape(16, 8, 128, 512).transpose(0, 2, 1, 3).reshape(16, 128, 4096))
    woa = g["L0_ffn_Wo"].reshape(16 * 8 * 64, D)
    out["f_Wo"] = np.ascontiguousarray(
        woa.reshape(8, 8, 128, 1024).transpose(0, 2, 1, 3).reshape(8, 128, 8192))
    out["f_W1"] = np.ascontiguousarray(
        g["L0_ffn_W1"].reshape(8, 128, 32, 128).transpose(2, 1, 0, 3)
        .reshape(32, 128, 1024))
    out["f_W2"] = np.ascontiguousarray(
        g["L0_ffn_W2"].reshape(32, 128, 8, 128).transpose(2, 1, 0, 3)
        .reshape(8, 128, 4096))
    for k in ("ln1_g", "ln1_b", "ln2_g", "ln2_b", "b2"):
        out["f_" + k] = _pack_cols(g["L0_ffn_" + k])
    out["f_b1"] = _pack_cols(g["L0_ffn_b1"])
    out.update(host_consts(cfg))
    return out



class Build:
    def __init__(self, cfg):
        self.cfg = cfg
        self.nc = bass.Bass("TRN2", target_bir_lowering=False)
        self.P = Planner(self.nc)

    def dram_in(self, name, shape, dt=F32):
        return self.nc.dram_tensor(name, list(shape), dt, kind="ExternalInput").ap()

    def build(self):
        cfg = self.cfg
        nc, P = self.nc, self.P
        import contextlib
        self._st = contextlib.ExitStack()
        en = self._st.enter_context
        S, NSEQ, L, D, DFF = cfg["S"], cfg["NSEQ"], cfg["L"], cfg["D"], cfg["DFF"]
        KT, MT, H, NTT = cfg["KT"], cfg["MT"], cfg["H"], cfg["NTT"]

        self.x = self.dram_in("x", (NSEQ, S, D))
        self.out = nc.dram_tensor("out", [NSEQ, S, D], F32, kind="ExternalOutput").ap()
        self.aW = {k: self.dram_in("a_" + k, (L, D, D)) for k in ("Wq", "Wk", "Wv", "Wo")}
        self.aW1 = self.dram_in("a_W1", (L, 32, 128, 1024))
        self.aW2 = self.dram_in("a_W2", (L, 8, 128, 4096))
        self.ab = {k: self.dram_in("a_" + k, (L, 128, KT)) for k in
                   ("bq", "bk", "bo", "ln1_g", "ln1_b", "ln2_g", "ln2_b", "bf2")}
        self.abf1 = self.dram_in("a_bf1", (L, 128, MT))
        self.abv = self.dram_in("a_bv", (L, D))
        self.aom4 = self.dram_in("a_omega4", (L, 128, 128))
        self.fWqt = self.dram_in("f_Wqt", (D, 1024))
        self.fWkt = self.dram_in("f_Wkt", (D, 1024))
        self.fWselv = self.dram_in("f_Wselv", (D, 128))
        self.fWselo = self.dram_in("f_Wselo", (D, 128))
        self.fWv = self.dram_in("f_Wv", (16, 128, 4096))
        self.fWo = self.dram_in("f_Wo", (8, 128, 8192))
        self.fW1 = self.dram_in("f_W1", (32, 128, 1024))
        self.fW2 = self.dram_in("f_W2", (8, 128, 4096))
        self.fb = {k: self.dram_in("f_" + k, (128, KT)) for k in
                   ("ln1_g", "ln1_b", "ln2_g", "ln2_b", "b2")}
        self.fb1 = self.dram_in("f_b1", (128, MT))
        self.ident_in = self.dram_in("ident", (128, 128))
        self.onesc_in = self.dram_in("ones_col", (128, 1))
        self.onesr_in = self.dram_in("ones_row", (1, 128))
        self.sgn_in = self.dram_in("sgn", (128, 1))
        self.mask0_in = self.dram_in("mask0", (128, 256))
        self.E_in = self.dram_in("E", (D, H))
        self.P2b_in = self.dram_in("P2b", (H, 8 * 128))
        self.cos_in = self.dram_in("rope_cos", (S, 16))
        self.sin_in = self.dram_in("rope_sin", (S, 16))

        self.wb = {}
        mir = lambda nm, ap: self.wb.setdefault(
            nm, nc.dram_tensor("wb_" + nm, list(ap.shape), BF16, kind="Internal").ap())
        for k in ("Wq", "Wk", "Wv", "Wo"):
            mir("a_" + k, self.aW[k])
        mir("a_W1", self.aW1)
        mir("a_W2", self.aW2)
        mir("f_Wqt", self.fWqt)
        mir("f_Wkt", self.fWkt)
        mir("f_Wselv", self.fWselv)
        mir("f_Wselo", self.fWselo)
        mir("f_Wv", self.fWv)
        mir("f_Wo", self.fWo)
        mir("f_W1", self.fW1)
        mir("f_W2", self.fW2)

        self.xn_sp = nc.dram_tensor("xn_sp", [KT, 128, S], BF16, kind="Internal").ap()
        self.xnf_sp = nc.dram_tensor("xnf_sp", [KT, 128, S], F32, kind="Internal").ap()
        self.q_sp = nc.dram_tensor("q_sp", [KT, 128, S], BF16, kind="Internal").ap()
        self.s_sp = nc.dram_tensor("s_sp", [KT, 128, S], F32, kind="Internal").ap()

        sb = lambda n, sh, dt: en(nc.sbuf_tensor(n, list(sh), dt))
        self.A1 = sb("A1", (128, KT * S), F32)
        self.A2 = sb("A2", (128, KT * S), BF16)
        self.BIGB = sb("BIGB", (128, 32768), BF16)
        self.CT = sb("CT", (128, 16384), BF16)
        self.POOL = sb("POOL", (128, 2304), F32)
        self.identf = sb("identf", (128, 128), F32)
        self.identb = sb("identb", (128, 128), BF16)
        self.onesc = sb("onesc", (128, 1), BF16)
        self.onesr = sb("onesr", (1, 128), F32)
        self.onesrb = sb("onesrb", (1, 128), BF16)
        self.sgn = sb("sgn_s", (128, 1), F32)
        self.P2b = sb("P2b_s", (16, 8 * 128), BF16)
        self.om4 = sb("om4", (128, 128), BF16)
        self.bvrow = sb("bvrow", (1, 1024), BF16)
        self.biasA = sb("biasA", (128, 96), F32)
        self.MRS = sb("MRS", (128, 2 * NTT), F32)

        self.Ssb = self.POOL[:, 512:1032]
        self.Sbf = self.CT[:, 15392:15912]
        self.mask0 = self.CT[:, 15912:16168]
        self.E_sb = self.CT[:, 16168:16296]

        ps = lambda n, sh, dt: en(nc.psum_tensor(n, list(sh), dt))
        self.PS = [ps(f"PS{i}", (128, 512), F32) for i in range(6)]
        self.PB = [ps(f"PB{i}", (128, 1024), BF16) for i in range(2)]
        for t in list(self.PS) + list(self.PB):
            P.psum_spaces.add(t.name)

        for nm in ("cst", "cs2", "prm", "wo_", "om4s"):
            P.stream(nm, "qg")
        for nm, n0 in (("a_Wq", 3), ("a_Wk", 3), ("a_Wv", 3), ("a_Wo", 3),
                       ("a_W1", 3), ("a_W2", 3), ("f_Wqt", 1), ("f_Wkt", 1),
                       ("f_Wselv", 1), ("f_Wselo", 1), ("f_Wv", 4), ("f_Wo", 4),
                       ("f_W1", 4), ("f_W2", 4)):
            for i in range(n0):
                P.stream(f"pp_{nm}{i}", "qg")
        for nm in ("lwq", "lwk", "lwv", "lwo"):
            P.stream(nm, "qs")
        for i in range(4):
            P.stream(f"r1{i}", "qs")
        for i in range(2):
            P.stream(f"r2{i}", "qs")
        for nm in ("cq", "x0", "x1", "spwx", "spwf", "spws", "spwq",
                   "sr0", "sr1", "sxf", "ss0", "ss1", "fs0", "fs1", "oq"):
            P.stream(nm, "qs")

        self.dbgmap = {}
        for name in cfg["debug"]:
            self.dbgmap[name] = nc.dram_tensor(
                "dbg_" + name, [NSEQ, 128, KT * S], F32, kind="ExternalOutput").ap()

        self._consts()
        for seq in range(NSEQ):
            self._phase0(seq)
            self._attn_branch(seq)
            self._ffn_branch(seq)
            self._final(seq)

        for c in reversed(P.cmds["qs"]):
            if c.kind == "dma":
                P._wait("ve", c)
                break
        P.emit()
        self._st.close()
        return nc

    def tap(self, name, seq):
        if name not in self.dbgmap:
            return
        self.P.dma("oq", self.dbgmap[name][seq], self.A1.ap(), dbg="tap" + name)

    def mmgroup(self, bank_ap, mms, mark_last=True, dbg=""):
        P = self.P
        n = len(mms)
        last = None
        for i, (l_ap, r_ap) in enumerate(mms):
            outs = [bank_ap] if (i == 0 or i == n - 1) else []
            st, sp = (i == 0), (i == n - 1)
            last = P.op("pe", lambda t, o=bank_ap, a=l_ap, b=r_ap, s1=st, s2=sp:
                        t.matmul(o, a, b, start=s1, stop=s2),
                        ins=[l_ap, r_ap], outs=outs, mark=(sp and mark_last), dbg=dbg)
        return last

    def transpose_batch(self, pairs, pb_idx, out_ap=None):
        P = self.P
        pb = self.PB[pb_idx % 2]
        n = len(pairs)
        for i, (d, s_) in enumerate(pairs):
            P.op("pe", lambda t, o=pb[:, i * 128:(i + 1) * 128], i_=s_:
                 t.transpose(o, i_, self.identb[:, :]),
                 ins=[s_, self.identb.ap()],
                 outs=[pb[:, i * 128:(i + 1) * 128]], mark=(i == n - 1))
        eng = ("ve", "sc")[pb_idx % 2]
        if out_ap is not None:
            src_ap = pb[:, 0:n * 128]
            if eng == "sc":
                P.op("sc", lambda s, o=out_ap, i=src_ap: s.copy(o, i),
                     ins=[src_ap], outs=[out_ap])
            else:
                P.op("ve", lambda v, o=out_ap, i=src_ap: v.tensor_copy(o, i),
                     ins=[src_ap], outs=[out_ap])
        else:
            for i, (d, s_) in enumerate(pairs):
                t_ap = pb[:, i * 128:(i + 1) * 128]
                if eng == "sc":
                    P.op("sc", lambda s, o=d, i_=t_ap: s.copy(o, i_),
                         ins=[t_ap], outs=[d])
                else:
                    P.op("ve", lambda v, o=d, i_=t_ap: v.tensor_copy(o, i_),
                         ins=[t_ap], outs=[d])

    def transpose_b(self, dst_ap, src_ap, pb_idx, eng=None):
        P = self.P
        pb = self.PB[pb_idx % 2]
        if eng is None:
            eng = ("ve", "sc")[pb_idx % 2]
        P.op("pe", lambda t, o=pb[:, 0:128], i=src_ap: t.transpose(
            o, i, self.identb[:, :]),
            ins=[src_ap, self.identb.ap()], outs=[pb[:, 0:128]], mark=True)
        if eng == "sc":
            P.op("sc", lambda s, o=dst_ap, i=pb[:, 0:128]: s.copy(o, i),
                 ins=[pb[:, 0:128]], outs=[dst_ap])
        else:
            P.op("ve", lambda v, o=dst_ap, i=pb[:, 0:128]: v.tensor_copy(o, i),
                 ins=[pb[:, 0:128]], outs=[dst_ap])

    def rstd_inplace(self, row_ap):
        P = self.P
        P.op("sc", lambda s, o=row_ap: s.activation(o, o, AF.Ln),
             ins=[row_ap], outs=[row_ap])
        P.op("sc", lambda s, o=row_ap: s.activation(o, o, AF.Exp, scale=-0.5),
             ins=[row_ap], outs=[row_ap])

    def _consts(self):
        P, cfg = self.P, self.cfg
        NTT = cfg["NTT"]
        P.dma("cq", self.identf[:, :], self.ident_in)
        P.dma("cst", self.identb[:, :], self.ident_in)
        P.dma("cst", self.onesc[:, :], self.onesc_in)
        P.dma("cq", self.onesr[:, :], self.onesr_in)
        P.dma("cst", self.onesrb[:, :], self.onesr_in)
        P.dma("cq", self.sgn[:, :], self.sgn_in)
        P.dma("cst", self.P2b[:, :], self.P2b_in)
        P.op("ve", lambda v, o=self.P2b[:, :]: v.tensor_scalar_mul(o, o, -1.0),
             ins=[self.P2b.ap()], outs=[self.P2b.ap()], dbg="negP2b")
        srcs = {"a_Wq": self.aW["Wq"], "a_Wk": self.aW["Wk"],
                "a_Wv": self.aW["Wv"], "a_Wo": self.aW["Wo"],
                "a_W1": self.aW1, "a_W2": self.aW2,
                "f_Wqt": self.fWqt, "f_Wkt": self.fWkt,
                "f_Wselv": self.fWselv, "f_Wselo": self.fWselo,
                "f_Wv": self.fWv, "f_Wo": self.fWo,
                "f_W1": self.fW1, "f_W2": self.fW2}
        order = ["a_Wq", "a_Wk", "a_Wv", "a_Wo", "a_W1", "a_W2",
                 "f_Wqt", "f_Wkt", "f_Wselv", "f_Wselo", "f_Wv", "f_Wo",
                 "f_W1", "f_W2"]
        nslice = {"a_Wq": 3, "a_Wk": 3, "a_Wv": 3, "a_Wo": 3, "a_W1": 3,
                  "a_W2": 3, "f_Wqt": 1, "f_Wkt": 1, "f_Wselv": 1,
                  "f_Wselo": 1, "f_Wv": 4, "f_Wo": 4, "f_W1": 4, "f_W2": 4}

        def ppk_emit(nm, i):
            s_ap, d_ap = srcs[nm], self.wb[nm]
            ns = min(nslice[nm], s_ap.shape[0])
            if ns == 1:
                P.dma(f"pp_{nm}0", d_ap, s_ap, dbg=f"ppk_{nm}")
                return
            step = s_ap.shape[0] // ns
            P.dma(f"pp_{nm}{i}", d_ap[i * step:(i + 1) * step],
                  s_ap[i * step:(i + 1) * step], dbg=f"ppk_{nm}{i}")

        self.deferred_ppk = []
        for nm in order:
            ns = min(nslice[nm], srcs[nm].shape[0])
            for i in range(ns):
                if nm in ("a_Wq", "a_Wk", "a_Wv", "a_Wo") and i == 0:
                    ppk_emit(nm, i)
                else:
                    self.deferred_ppk.append((nm, i))
        self._ppk_emit = ppk_emit

    def _phase0(self, seq):
        cfg, P = self.cfg, self.P
        S, D, KT, NTT = cfg["S"], cfg["D"], cfg["KT"], cfg["NTT"]
        A1, A2 = self.A1, self.A2
        for tt in range(NTT):
            xs = ("x0", "x1")[tt % 2]
            xt = self.POOL[:, (tt % 2) * 1024:(tt % 2) * 1024 + 1024]
            xn = self.CT[:, 9216 + (tt % 2) * 3072: 10240 + (tt % 2) * 3072].bitcast(F32)
            P.dma(xs, xt, self.x[seq, tt * 128:(tt + 1) * 128, :], dbg=f"x{tt}")
            mcol = self.MRS[:, tt:tt + 1]
            rcol = self.MRS[:, NTT + tt:NTT + tt + 1]
            P.op("ve", lambda v, o=mcol, i=xt: v.tensor_reduce(
                o.unsqueeze(2), i.rearrange("p (a n) -> p a n", a=1), AX.X, ALU.add),
                ins=[xt], outs=[mcol], dbg="p0mean")
            P.op("ve", lambda v, o=mcol: v.tensor_scalar_mul(o, o, -1.0 / D),
                 ins=[mcol], outs=[mcol])
            sq = self.CT[:, 8192 + (tt % 2) * 3072: 8192 + (tt % 2) * 3072 + 1024].bitcast(F32)
            ctb = self.CT[:, 10752 + (tt % 2) * 3072: 10768 + (tt % 2) * 3072].bitcast(F32)
            vcol = ctb[:, 0:1]
            vcol2 = ctb[:, 1:2]
            P.op("sc", lambda s, o=sq, i=xt[:, 0:512], b=mcol, a=vcol: s.activation(
                o, i, AF.Square, bias=b, scale=1.0, accum_out=a),
                ins=[xt[:, 0:512], mcol], outs=[sq, vcol], dbg="p0sq0")
            P.op("sc", lambda s, o=sq, i=xt[:, 512:1024], b=mcol, a=vcol2: s.activation(
                o, i, AF.Square, bias=b, scale=1.0, accum_out=a),
                ins=[xt[:, 512:1024], mcol], outs=[sq, vcol2], dbg="p0sq1")
            P.op("ve", lambda v, o=vcol, a=vcol, b=vcol2: v.tensor_add(o, a, b),
                 ins=[vcol, vcol2], outs=[vcol])
            P.op("ve", lambda v, o=vcol: v.tensor_scalar(
                o, o, 1.0 / D, LN_EPS, ALU.mult, ALU.add), ins=[vcol], outs=[vcol])
            self.rstd_inplace(vcol)
            P.op("ve", lambda v, o=rcol, i=vcol: v.tensor_copy(o, i),
                 ins=[vcol], outs=[rcol], dbg="p0rstd")
            for half in range(2):
                xh = xt[:, half * 512:(half + 1) * 512]
                P.op("ve", lambda v, o=xn, i=xh, m=mcol, r=rcol: v.tensor_scalar(
                    o, i, m, r, ALU.add, ALU.mult),
                    ins=[xh, mcol, rcol], outs=[xn], dbg="p0xn")
                xnb = self.CT[:, 10240 + (tt % 2) * 3072: 10752 + (tt % 2) * 3072]
                P.op("sc", lambda s, o=xnb, i=xn: s.copy(o, i), ins=[xn], outs=[xnb])
                out = A2.ap().rearrange("p (kt t) -> p kt t", t=S)[
                    :, half * 4: half * 4 + 4, tt * 128:tt * 128 + 128]
                self.transpose_batch(
                    [(None, xnb[:, k * 128:(k + 1) * 128]) for k in range(4)],
                    half, out_ap=out)
                for k in range(4):
                    kt = half * 4 + k
                    psf = self.PS[kt % 2]
                    src_f = xn[:, k * 128:(k + 1) * 128]
                    dst_f = A1[:, kt * S + tt * 128: kt * S + tt * 128 + 128]
                    P.op("pe", lambda t, o=psf[:, 0:128], i=src_f: t.transpose(
                        o, i, self.identf[:, :]),
                        ins=[src_f, self.identf.ap()], outs=[psf[:, 0:128]],
                        mark=True, dbg="p0trf")
                    P.op("sc", lambda s, o=dst_f, i=psf[:, 0:128]: s.copy(o, i),
                         ins=[psf[:, 0:128]], outs=[dst_f])
        for kt in range(KT):
            P.dma("spwx", self.xn_sp[kt], A2[:, kt * S:(kt + 1) * S], dbg="xnsp")
            P.dma("spwf", self.xnf_sp[kt], A1[:, kt * S:(kt + 1) * S], dbg="xnfsp")
        self.tap("p0_xn", seq)

    def ln_fm(self, gcol, bcol, write_a2, spill_s=False):
        cfg, P = self.cfg, self.P
        S, D, KT, NTS = cfg["S"], cfg["D"], cfg["KT"], cfg["NTS"]
        A1, A2 = self.A1, self.A2
        ROW2 = self.CT[:, 0:8192].bitcast(F32)
        ROW3 = self.CT[:, 8192:12288].bitcast(F32)
        mrow = ROW2[0:1, 0:S]
        vrow = ROW2[0:1, 2048:2048 + S]
        for ts in range(NTS):
            c0 = ts * 512
            for kt in range(KT):
                a1t = A1[:, kt * S + c0: kt * S + c0 + 512]
                a2t = A2[:, kt * S + c0: kt * S + c0 + 512]
                P.op("sc", lambda s, o=a2t, i=a1t: s.copy(o, i), ins=[a1t], outs=[a2t])
                sqt = self.CT[:, 8192 + kt * 512: 8192 + kt * 512 + 512]
                P.op("ve", lambda v, o=sqt, a=a2t, b=a2t: v.tensor_mul(o, a, b),
                     ins=[a2t], outs=[sqt])
            pssum = self.PS[2]
            pssq = self.PS[3]
            self.mmgroup(pssum[0:1, 0:512],
                         [(self.onesc[:, :], A2[:, kt * S + c0: kt * S + c0 + 512])
                          for kt in range(KT)])
            self.mmgroup(pssq[0:1, 0:512],
                         [(self.onesc[:, :],
                           self.CT[:, 8192 + kt * 512: 8192 + kt * 512 + 512])
                          for kt in range(KT)])
            P.op("ve", lambda v, o=mrow[0:1, c0:c0 + 512], i=pssum[0:1, 0:512]:
                 v.tensor_scalar_mul(o, i, -1.0 / D),
                 ins=[pssum[0:1, 0:512]], outs=[mrow[0:1, c0:c0 + 512]])
            P.op("ve", lambda v, o=vrow[0:1, c0:c0 + 512], i=pssq[0:1, 0:512]:
                 v.tensor_scalar_mul(o, i, 1.0 / D),
                 ins=[pssq[0:1, 0:512]], outs=[vrow[0:1, c0:c0 + 512]])
        m2 = ROW3[0:1, 0:S]
        P.op("sc", lambda s, o=m2, i=mrow: s.activation(o, i, AF.Square),
             ins=[mrow], outs=[m2])
        P.op("ve", lambda v, o=vrow, a=vrow, b=m2: v.tensor_sub(o, a, b),
             ins=[vrow, m2], outs=[vrow])
        P.op("ve", lambda v, o=vrow: v.tensor_scalar_add(o, o, LN_EPS),
             ins=[vrow], outs=[vrow])
        self.rstd_inplace(vrow)
        P.op("ve", lambda v, o=mrow, a=mrow, b=vrow: v.tensor_mul(o, a, b),
             ins=[mrow, vrow], outs=[mrow])
        for ts in range(NTS):
            c0 = ts * 512
            psr = self.PS[4]
            psm = self.PS[5]
            P.op("pe", lambda t, o=psr[:, 0:512], a=self.onesr[:, :],
                 b=vrow[0:1, c0:c0 + 512]: t.matmul(o, a, b, start=True, stop=True),
                 ins=[self.onesr.ap(), vrow[0:1, c0:c0 + 512]],
                 outs=[psr[:, 0:512]], mark=True)
            P.op("pe", lambda t, o=psm[:, 0:512], a=self.onesr[:, :],
                 b=mrow[0:1, c0:c0 + 512]: t.matmul(o, a, b, start=True, stop=True),
                 ins=[self.onesr.ap(), mrow[0:1, c0:c0 + 512]],
                 outs=[psm[:, 0:512]], mark=True)
            for kt in range(KT):
                a1t = A1[:, kt * S + c0: kt * S + c0 + 512]
                P.op("ve", lambda v, o=a1t, i=a1t, r=psr[:, 0:512]:
                     v.tensor_mul(o, i, r), ins=[a1t, psr[:, 0:512]], outs=[a1t])
                P.op("ve", lambda v, o=a1t, i=a1t, m=psm[:, 0:512]:
                     v.tensor_add(o, i, m), ins=[a1t, psm[:, 0:512]], outs=[a1t])
                g1 = gcol[:, kt:kt + 1]
                b1 = bcol[:, kt:kt + 1]
                P.op("ve", lambda v, o=a1t, i=a1t, g=g1, b=b1: v.tensor_scalar(
                    o, i, g, b, ALU.mult, ALU.add), ins=[a1t, g1, b1], outs=[a1t])
                if write_a2:
                    a2t = A2[:, kt * S + c0: kt * S + c0 + 512]
                    P.op("sc", lambda s, o=a2t, i=a1t: s.copy(o, i),
                         ins=[a1t], outs=[a2t])
                if spill_s:
                    P.dma("spws", self.s_sp[kt][:, c0:c0 + 512], a1t, dbg="ssp")

    def ffn_mlp(self, w1_src, w2_src, b1col, b2col):
        cfg, P = self.cfg, self.P
        S, KT, MT = cfg["S"], cfg["KT"], cfg["MT"]
        A1, A2, HID = self.A1, self.A2, self.BIGB
        tsw = min(1024, S)
        for tsl in range(S // tsw):
            for m in range(MT):
                slot = m % 4
                r1 = self.CT[:, slot * 1024: slot * 1024 + 1024]
                P.dma(f"r1{slot}", r1, w1_src(m), dbg=f"w1m{m}")
                for nh in range(tsw // 512):
                    psb = self.PS[nh % 2]
                    rhs0 = tsl * tsw + nh * 512
                    self.mmgroup(psb[:, 0:512],
                                 [(r1[:, kt * 128:(kt + 1) * 128],
                                   A2[:, kt * S + rhs0: kt * S + rhs0 + 512])
                                  for kt in range(KT)], dbg=f"W1m{m}nh{nh}")
                    ht = HID[:, m * tsw + nh * 512: m * tsw + nh * 512 + 512]
                    bc = b1col[:, m:m + 1]
                    P.op("sc", lambda s, o=ht, i=psb[:, 0:512], b=bc:
                         s.activation(o, i, AF.Relu, bias=b),
                         ins=[psb[:, 0:512], bc], outs=[ht])
            for dt in range(KT):
                slot = dt % 2
                r2 = self.CT[:, 4096 + slot * 4096: 4096 + slot * 4096 + 4096]
                P.dma(f"r2{slot}", r2, w2_src(dt), dbg=f"w2d{dt}")
                for nh in range(tsw // 512):
                    psb = self.PS[2 + nh % 2]
                    self.mmgroup(psb[:, 0:512],
                                 [(r2[:, m * 128:(m + 1) * 128],
                                   HID[:, m * tsw + nh * 512: m * tsw + nh * 512 + 512])
                                  for m in range(MT)])
                    c0 = tsl * tsw + nh * 512
                    a1t = A1[:, dt * S + c0: dt * S + c0 + 512]
                    bc = b2col[:, dt:dt + 1]
                    P.op("ve", lambda v, o=a1t, i=psb[:, 0:512], b=bc, r=a1t:
                         v.scalar_tensor_tensor(o, i, b, r, ALU.add, ALU.add),
                         ins=[psb[:, 0:512], bc, a1t], outs=[a1t])

    def _attn_branch(self, seq):
        P, cfg = self.P, self.cfg
        P.dma("cs2", self.mask0, self.mask0_in)
        P.dma("cs2", self.E_sb.rearrange("p (kt h) -> p kt h", kt=cfg["KT"]),
              self.E_in.rearrange("(kt p) h -> p kt h", p=128))
        for l in range(self.cfg["L"]):
            self._attn_layer(seq, l)

    def _attn_layer(self, seq, l):
        cfg, P = self.cfg, self.P
        S, KT, H = cfg["S"], cfg["KT"], cfg["H"]
        C, NCH, PT = cfg["C"], cfg["NCH"], cfg["PT"]
        A1, A2, CT = self.A1, self.A2, self.CT
        c4 = float(64.0 ** -0.25)

        pm = ("bq", "bk", "bo", "ln1_g", "ln1_b", "ln2_g", "ln2_b", "bf2")
        for i, k in enumerate(pm):
            P.dma("prm", self.biasA[:, i * 8:(i + 1) * 8], self.ab[k][l], dbg="prm" + k)
        P.dma("prm", self.biasA[:, 64:96], self.abf1[l], dbg="prmbf1")
        P.dma("om4s", self.om4[:, :], self.aom4[l], dbg="om4")
        for p0, c0 in ((0, 32), (64, 96)):
            sl = self.om4[p0:p0 + 64, c0:c0 + 32]
            P.op("ve", lambda v, o=sl: v.tensor_scalar_mul(o, o, -1.0),
                 ins=[sl], outs=[sl])
        P.dma("wo_", self.bvrow[:, :], self.abv[l].unsqueeze(0), dbg="bvrow")
        for i in (0, 1):
            ba = self.biasA[:, i * 8:(i + 1) * 8]
            P.op("ve", lambda v, o=ba: v.tensor_scalar_mul(o, o, c4),
                 ins=[ba], outs=[ba])

        WQ = self.BIGB[:, 0:8192]
        WK = self.BIGB[:, 8192:16384]
        WV = self.BIGB[:, 16384:24576]
        WO = self.BIGB[:, 24576:32768]
        for nm, buf, src in (("lwq", WQ, self.wb["a_Wq"]), ("lwk", WK, self.wb["a_Wk"]),
                             ("lwv", WV, self.wb["a_Wv"]), ("lwo", WO, self.wb["a_Wo"])):
            for kt in range(KT):
                P.dma(nm, buf[:, kt * 1024:(kt + 1) * 1024],
                      src[l, kt * 128:(kt + 1) * 128, :], dbg=nm + str(kt))

        XS = CT[:, 0:2048]
        SQc = CT[:, 2048:4096]
        QFc = CT[:, 4096:6144]
        KFc = CT[:, 6144:8192]
        KTm = CT[:, 8192:10240]
        ATm = CT[:, 10240:12288]
        ATs_ = [CT[:, 12288 + i * 384: 12288 + (i + 1) * 384] for i in range(2)]
        VAch = CT[:, 13056:15136]
        HLb = CT[:, 15136:15392]
        UF = self.POOL[:, 0:256]
        HL = self.POOL[0:16, 256:512]
        va3 = VAch[:, 0:PT * 16 * 65].rearrange("p (g j) -> p g j", j=65)
        P.op("ve", lambda v, ap=va3[:, :, 64:65]: v.memset(ap, 1.0),
             outs=[va3[:, :, 64:65]], dbg="va1")

        def favor(out_buf, bias_i):
            psh = self.PS[4]
            self.mmgroup(psh[0:16, 0:C],
                         [(self.E_sb[:, kt * 16:(kt + 1) * 16],
                           SQc[:, kt * C:(kt + 1) * C]) for kt in range(KT)])
            P.op("sc", lambda s, o=HLb[0:16, 0:C], i=psh[0:16, 0:C]:
                 s.activation(o, i, AF.Identity, scale=0.5),
                 ins=[psh[0:16, 0:C]], outs=[HLb[0:16, 0:C]])
            for hp in range(8):
                psu = self.PS[4 + hp % 2]
                P.op("pe", lambda t, o=psu[:, 0:C], a=self.om4[:, :],
                     b=XS[:, hp * C:(hp + 1) * C]: t.matmul(o, a, b, start=True, stop=False),
                     ins=[self.om4.ap(), XS[:, hp * C:(hp + 1) * C]],
                     outs=[psu[:, 0:C]])
                P.op("pe", lambda t, o=psu[:, 0:C],
                     a=self.P2b[:, hp * 128:(hp + 1) * 128], b=HLb[0:16, 0:C]:
                     t.matmul(o, a, b, start=False, stop=True),
                     ins=[self.P2b[:, hp * 128:(hp + 1) * 128], HLb[0:16, 0:C]],
                     outs=[psu[:, 0:C]], mark=True)
                ot = out_buf[:, hp * C:(hp + 1) * C]
                P.op("sc", lambda s, o=ot, i=psu[:, 0:C]: s.activation(o, i, AF.Exp),
                     ins=[psu[:, 0:C]], outs=[ot])

        for ch in range(NCH):
            base = ch * C
            if ch == 1 and self.deferred_ppk:
                for nm_i in self.deferred_ppk:
                    self._ppk_emit(*nm_i)
                self.deferred_ppk = []
            for side, Wbuf, bia, outf, bq in (("q", WQ, 0, QFc, 0), ("k", WK, 1, KFc, 2)):
                for mt in range(KT):
                    psb = self.PS[mt % 4]
                    self.mmgroup(psb[:, 0:C],
                                 [(Wbuf[:, kt * 1024 + mt * 128: kt * 1024 + mt * 128 + 128],
                                   A2[:, kt * S + base: kt * S + base + C])
                                  for kt in range(KT)])
                    bc = self.biasA[:, bia * 8 + mt: bia * 8 + mt + 1]
                    xst = XS[:, mt * C:(mt + 1) * C]
                    sqt = SQc[:, mt * C:(mt + 1) * C]
                    P.op("sc", lambda s, o=xst, i=psb[:, 0:C], b=bc:
                         s.activation(o, i, AF.Identity, bias=b, scale=c4),
                         ins=[psb[:, 0:C], bc], outs=[xst])
                    P.op("sc", lambda s, o=sqt, i=psb[:, 0:C], b=bc:
                         s.activation(o, i, AF.Square, bias=b, scale=c4),
                         ins=[psb[:, 0:C], bc], outs=[sqt])
                favor(outf, bia)
                if side == "q":
                    for pt in range(PT):
                        for nh in range(2):
                            psb = self.PS[nh]
                            mms = [(A2[:, kt * S + base + pt * 128: kt * S + base + pt * 128 + 128],
                                    WV[:, kt * 1024 + nh * 512: kt * 1024 + nh * 512 + 512])
                                   for kt in range(KT)]
                            mms.append((self.onesrb[0:1, 0:128],
                                        self.bvrow[0:1, nh * 512: nh * 512 + 512]))
                            self.mmgroup(psb[:, 0:512], mms)
                            dst = va3[:, pt * 16 + nh * 8: pt * 16 + (nh + 1) * 8, 0:64]
                            P.op("sc", lambda s, o=dst, i=psb[:, 0:512].rearrange(
                                "p (h d) -> p h d", d=64): s.copy(o, i),
                                ins=[psb[:, 0:512]], outs=[dst])
            for pt in range(PT):
                for half in range(2):
                    mts = range(half * 4, half * 4 + 4)
                    self.transpose_batch(
                        [(None, KFc[:, mt * C + pt * 128: mt * C + pt * 128 + 128])
                         for mt in mts], half,
                        out_ap=KTm[:, pt * 1024 + half * 512: pt * 1024 + half * 512 + 512])
            def head_at(h):
                hp, r0 = h // 2, (h % 2) * 64
                ATs = ATs_[h % 2]
                qf = lambda c0, cn: QFc[r0:r0 + 64, hp * C + c0: hp * C + c0 + cn]
                kf = lambda c0, cn: KFc[r0:r0 + 64, hp * C + c0: hp * C + c0 + cn]
                for skp in range(PT):
                    psb = self.PS[skp % 2]
                    ncols = C - skp * 128
                    P.op("pe", lambda t, o=psb[:, 0:ncols], a=kf(skp * 128, 128),
                         b=qf(skp * 128, ncols): t.matmul(o, a, b, start=True, stop=True),
                         ins=[kf(skp * 128, 128), qf(skp * 128, ncols)],
                         outs=[psb[:, 0:ncols]], mark=True)
                    at = ATs[:, skp * 256: skp * 256 + ncols]
                    P.op("ve", lambda v, o=at, i=psb[:, 0:ncols],
                         m=self.mask0[:, 0:ncols]: v.tensor_mul(o, i, m),
                         ins=[psb[:, 0:ncols], self.mask0[:, 0:ncols]], outs=[at])

            def head_rest(h, ch):
                hp, r0 = h // 2, (h % 2) * 64
                ATs = ATs_[h % 2]
                qf = lambda c0, cn: QFc[r0:r0 + 64, hp * C + c0: hp * C + c0 + cn]
                Sb = self.Sbf[r0:r0 + 64, hp * 65: hp * 65 + 65]
                Sf = self.Ssb[r0:r0 + 64, hp * 65: hp * 65 + 65]
                for pq in range(PT):
                    pso = self.PS[2 + pq % 2]
                    mms = []
                    for skp in range(pq + 1):
                        lhs = ATs[:, skp * 256 + (pq - skp) * 128:
                                  skp * 256 + (pq - skp) * 128 + 128]
                        rhs = va3[:, skp * 16 + h, :]
                        mms.append((lhs, rhs))
                    if ch > 0:
                        mms.append((qf(pq * 128, 128), Sb))
                    self.mmgroup(pso[:, 0:65], mms)
                    dc = self.POOL[:, 1032 + (h * 2 + pq) % 8:
                                    1033 + (h * 2 + pq) % 8]
                    P.op("ve", lambda v, o=dc, i=pso[:, 64:65]: v.tensor_scalar_add(
                        o, i, ATTN_EPS64), ins=[pso[:, 64:65]], outs=[dc])
                    P.op("ve", lambda v, o=dc: v.reciprocal(o, o), ins=[dc], outs=[dc])
                    av = ATm[:, pq * 1024 + h * 64: pq * 1024 + h * 64 + 64]
                    P.op("ve", lambda v, o=av, i=pso[:, 0:64], d=dc:
                         v.tensor_scalar_mul(o, i, d),
                         ins=[pso[:, 0:64], dc], outs=[av])
                pss = self.PS[4 + h % 2]
                mms = [(KTm[:, pt * 1024 + hp * 128 + r0: pt * 1024 + hp * 128 + r0 + 64],
                        va3[:, pt * 16 + h, :]) for pt in range(PT)]
                self.mmgroup(pss[r0:r0 + 64, 0:65], mms)
                if ch == 0:
                    P.op("ve", lambda v, o=Sf, i=pss[r0:r0 + 64, 0:65]:
                         v.tensor_copy(o, i), ins=[pss[r0:r0 + 64, 0:65]], outs=[Sf])
                else:
                    P.op("ve", lambda v, o=Sf, a=Sf, b=pss[r0:r0 + 64, 0:65]:
                         v.tensor_add(o, a, b),
                         ins=[Sf, pss[r0:r0 + 64, 0:65]], outs=[Sf])
                P.op("sc", lambda s, o=Sb, i=Sf: s.copy(o, i), ins=[Sf], outs=[Sb])

            for h in range(H):
                head_at(h)
                if h > 0:
                    head_rest(h - 1, ch)
            head_rest(H - 1, ch)
            AFm = QFc
            for pt in range(PT):
                for half in range(2):
                    ats = list(range(half * 4, half * 4 + 4))
                    out = AFm.rearrange("p (at c) -> p at c", c=C)[
                        :, ats[0]:ats[0] + 4, pt * 128:pt * 128 + 128]
                    self.transpose_batch(
                        [(None, ATm[:, pt * 1024 + at * 128: pt * 1024 + at * 128 + 128])
                         for at in ats], half, out_ap=out)
            for dt in range(KT):
                psb = self.PS[dt % 4]
                self.mmgroup(psb[:, 0:C],
                             [(WO[:, at * 1024 + dt * 128: at * 1024 + dt * 128 + 128],
                               AFm[:, at * C:(at + 1) * C]) for at in range(KT)])
                a1t = A1[:, dt * S + base: dt * S + base + C]
                bc = self.biasA[:, 16 + dt: 16 + dt + 1]
                P.op("ve", lambda v, o=a1t, i=psb[:, 0:C], b=bc, r=a1t:
                     v.scalar_tensor_tensor(o, i, b, r, ALU.add, ALU.add),
                     ins=[psb[:, 0:C], bc, a1t], outs=[a1t])

        self.ln_fm(self.biasA[:, 24:32], self.biasA[:, 32:40], write_a2=True)
        w1_src = lambda m: self.wb["a_W1"][l, m]
        w2_src = lambda dt: self.wb["a_W2"][l, dt]
        self.ffn_mlp(w1_src, w2_src, self.biasA[:, 64:96], self.biasA[:, 56:64])
        last = (l == self.cfg["L"] - 1)
        self.ln_fm(self.biasA[:, 40:48], self.biasA[:, 48:56],
                   write_a2=not last, spill_s=last)
        if last:
            self.tap("attn_s", seq)

    def _ffn_branch(self, seq):
        cfg, P = self.cfg, self.P
        S, KT, H, NTT, NTS = cfg["S"], cfg["KT"], cfg["H"], cfg["NTT"], cfg["NTS"]
        A1, A2, CT, BIGB = self.A1, self.A2, self.CT, self.BIGB

        WQT = BIGB[:, 0:8192]
        WKT = BIGB[:, 8192:16384]
        WSEL = BIGB[:, 16384:18432]
        XNALL = A1.ap().bitcast(BF16)
        QAS = BIGB[:, 20480:21504]
        ROPB = BIGB[:, 21504:21504 + 32 * NTT]
        SCR = BIGB[:, 22528:24064]
        GVF = BIGB[:, 28160:30208]
        go = BIGB[:, 30208:32256]
        KFM = CT

        for kt in range(KT):
            P.dma("lwq", WQT[:, kt * 1024:(kt + 1) * 1024],
                  self.wb["f_Wqt"][kt * 128:(kt + 1) * 128, :], dbg="fwq")
            P.dma("lwk", WKT[:, kt * 1024:(kt + 1) * 1024],
                  self.wb["f_Wkt"][kt * 128:(kt + 1) * 128, :], dbg="fwk")
        P.dma("lwo", WSEL[:, 0:1024].rearrange("p (kt c) -> p kt c", c=128),
              self.wb["f_Wselv"].rearrange("(kt p) c -> p kt c", p=128), dbg="fwsv")
        P.dma("lwo", WSEL[:, 1024:2048].rearrange("p (kt c) -> p kt c", c=128),
              self.wb["f_Wselo"].rearrange("(kt p) c -> p kt c", p=128), dbg="fwso")
        P.dma("wo_", ROPB[:, 0:16 * NTT].rearrange("p (t j) -> p t j", j=16),
              self.cos_in.rearrange("(t p) j -> p t j", p=128), dbg="ropc")
        P.dma("wo_", ROPB[:, 16 * NTT:32 * NTT].rearrange("p (t j) -> p t j", j=16),
              self.sin_in.rearrange("(t p) j -> p t j", p=128), dbg="rops")

        for kt in range(KT):
            P.dma("sr0", XNALL[:, kt * S:(kt + 1) * S], self.xn_sp[kt], dbg="xnall")

        def xn_tiles(tt):
            return [XNALL[:, kt * S + tt * 128: kt * S + tt * 128 + 128]
                    for kt in range(KT)]

        SCRQ = SCR[:, 0:1024]
        SCRK = BIGB[:, 24064:25088]
        E1 = SCR[:, 1024:1280]
        E2 = SCR[:, 1280:1536]
        for tt in range(NTT):
            xnt = xn_tiles(tt)
            cosb = ROPB[:, tt * 16: tt * 16 + 16]
            sinb = ROPB[:, 16 * NTT + tt * 16: 16 * NTT + tt * 16 + 16]
            cb = cosb.unsqueeze(1).broadcast_to([128, 16, 16])
            sbb = sinb.unsqueeze(1).broadcast_to([128, 16, 16])
            for which, wt, qt, b0 in (("q", WQT, SCRQ, 0), ("k", WKT, SCRK, 2)):
                for nh in range(2):
                    psb = self.PS[b0 + nh]
                    self.mmgroup(psb[:, 0:512],
                                 [(xnt[kt],
                                   wt[:, kt * 1024 + nh * 512: kt * 1024 + nh * 512 + 512])
                                  for kt in range(KT)])
                    P.op("sc", lambda s, o=qt[:, nh * 512:(nh + 1) * 512],
                         i=psb[:, 0:512]: s.copy(o, i),
                         ins=[psb[:, 0:512]], outs=[qt[:, nh * 512:(nh + 1) * 512]])
            E3 = BIGB[:, 25088:25344]
            E4 = BIGB[:, 25344:25600]
            for qt in (SCRQ, SCRK):
                q3 = qt.rearrange("p (h d) -> p h d", d=64)
                x1 = q3[:, :, 0:16]
                x2 = q3[:, :, 16:32]
                e13 = E1.rearrange("p (h d) -> p h d", d=16)
                e23 = E2.rearrange("p (h d) -> p h d", d=16)
                e33 = E3.rearrange("p (h d) -> p h d", d=16)
                e43 = E4.rearrange("p (h d) -> p h d", d=16)
                P.op("ve", lambda v, o=e13, a=x1, b=cb: v.tensor_mul(o, a, b),
                     ins=[qt, ROPB], outs=[E1])
                P.op("ve", lambda v, o=e23, a=x2, b=sbb: v.tensor_mul(o, a, b),
                     ins=[qt, ROPB], outs=[E2])
                P.op("ve", lambda v, o=e33, a=x1, b=sbb: v.tensor_mul(o, a, b),
                     ins=[qt, ROPB], outs=[E3])
                P.op("ve", lambda v, o=e43, a=x2, b=cb: v.tensor_mul(o, a, b),
                     ins=[qt, ROPB], outs=[E4])
                P.op("ve", lambda v, o=x1, a=e13, b=e23: v.tensor_sub(o, a, b),
                     ins=[E1, E2], outs=[qt])
                P.op("ve", lambda v, o=x2, a=e33, b=e43: v.tensor_add(o, a, b),
                     ins=[E3, E4], outs=[qt])
            for half in range(2):
                kds = list(range(half * 4, half * 4 + 4))
                self.transpose_batch(
                    [(None, SCRQ[:, kd * 128:(kd + 1) * 128]) for kd in kds],
                    half, out_ap=QAS[:, half * 512: half * 512 + 512])
            P.dma("spwq", self.q_sp[:, :, tt * 128:(tt + 1) * 128].rearrange(
                "kt p c -> p kt c"),
                QAS.rearrange("p (kt c) -> p kt c", c=128), dbg=f"qsp{tt}")
            for half in range(2):
                kds = list(range(half * 4, half * 4 + 4))
                out = KFM.rearrange("p (kd t) -> p kd t", t=S)[
                    :, kds[0]:kds[0] + 4, tt * 128:tt * 128 + 128]
                self.transpose_batch(
                    [(None, SCRK[:, kd * 128:(kd + 1) * 128]) for kd in kds],
                    half, out_ap=out)
        for tt in range(NTT):
            xnt = xn_tiles(tt)
            for gi in range(2):
                wsl = WSEL[:, gi * 1024:(gi + 1) * 1024]
                psb = self.PS[4 + gi]
                self.mmgroup(psb[:, 0:128],
                             [(xnt[kt], wsl[:, kt * 128:(kt + 1) * 128])
                              for kt in range(KT)])
                sg = self.POOL[:, 16 + gi * 128:144 + gi * 128]
                P.op("sc", lambda s, o=sg, i=psb[:, 0:128]: s.activation(
                    o, i, AF.Sigmoid), ins=[psb[:, 0:128]], outs=[sg])
                m8 = self.POOL[:, 0:8] if gi == 0 else self.POOL[:, 8:16]
                dst = (GVF[:, tt * 128:(tt + 1) * 128] if gi == 0
                       else go[:, tt * 128:(tt + 1) * 128])
                for hh in range(16):
                    sgh = sg[:, hh * 8:(hh + 1) * 8]
                    P.op("ve", lambda v, o=m8, i=sgh: v.max(o, i),
                         ins=[sgh], outs=[m8])
                    P.op("ve", lambda v, o=dst[:, hh * 8:(hh + 1) * 8], i=sgh,
                         m=m8[:, 1:2]: v.scalar_tensor_tensor(
                         o, i, m, i, ALU.is_ge, ALU.mult),
                         ins=[sgh, m8], outs=[dst[:, hh * 8:(hh + 1) * 8]])

        VAF = BIGB[:, 8192:24832]
        va3 = VAF[:, 0:16 * NTT * 65].rearrange("p (g j) -> p g j", j=65)
        P.op("ve", lambda v, ap=va3[:, :, 64:65]: v.memset(ap, 1.0),
             outs=[va3[:, :, 64:65]], dbg="fva1")
        WVR = [BIGB[:, i * 4096:(i + 1) * 4096] for i in range(2)]
        VG_ = [self.POOL[:, 1088 + i * 512: 1088 + (i + 1) * 512] for i in range(2)]
        for h in range(16):
            wvr = WVR[h % 2]
            P.dma(("lwv", "lwq")[h % 2], wvr, self.wb["f_Wv"][h], dbg=f"fwv{h}")
            for tt in range(NTT):
                xnt = xn_tiles(tt)
                psb = self.PS[tt % 2]
                VG = VG_[tt % 2]
                self.mmgroup(psb[:, 0:512],
                             [(xnt[kt],
                               wvr[:, kt * 512:(kt + 1) * 512]) for kt in range(KT)])
                gvs = GVF[:, tt * 128 + h * 8: tt * 128 + (h + 1) * 8]
                gb = gvs.unsqueeze(2).broadcast_to([128, 8, 64])
                P.op("ve", lambda v, o=VG.rearrange("p (e d) -> p e d", d=64),
                     a=psb[:, 0:512].rearrange("p (e d) -> p e d", d=64), b=gb:
                     v.tensor_mul(o, a, b), ins=[psb[:, 0:512], gvs], outs=[VG])
                vr = self.POOL[:, 2112 + (tt % 2) * 64: 2176 + (tt % 2) * 64]
                P.op("ve", lambda v, o=vr.unsqueeze(2),
                     i=VG.rearrange("p (e d) -> p d e", e=8):
                     v.tensor_reduce(o, i, AX.X, ALU.add), ins=[VG], outs=[vr])
                dst = va3[:, h * NTT + tt, 0:64]
                P.op("sc", lambda s, o=dst, i=vr: s.copy(o, i), ins=[vr], outs=[dst])

        XSLB = BIGB[:, 0:8192]
        QP = [BIGB[:, 24832 + i * 2048: 24832 + (i + 1) * 2048] for i in range(2)]
        for h in range(16):
            hp, r0 = h // 2, (h % 2) * 64
            if h % 2 == 0:
                P.dma(("ss0", "ss1")[hp % 2], QP[hp % 2][:, 0:S],
                      self.q_sp[hp], dbg=f"qld{hp}")
            qp = QP[hp % 2]
            for sl in range(NTS):
                for st in range(NTT):
                    psb = self.PS[st % 4]
                    P.op("pe", lambda t, o=psb[:, 0:512],
                         a=KFM[r0:r0 + 64, hp * S + st * 128: hp * S + st * 128 + 128],
                         b=qp[r0:r0 + 64, sl * 512:(sl + 1) * 512]:
                         t.matmul(o, a, b, start=True, stop=True),
                         ins=[KFM[r0:r0 + 64, hp * S + st * 128: hp * S + st * 128 + 128],
                              qp[r0:r0 + 64, sl * 512:(sl + 1) * 512]],
                         outs=[psb[:, 0:512]], mark=True)
                    xt = XSLB[:, st * 512:(st + 1) * 512]
                    P.op("sc", lambda s, o=xt, i=psb[:, 0:512]: s.activation(
                        o, i, AF.Exp, scale=0.125), ins=[psb[:, 0:512]], outs=[xt])
                for sqc in range(4):
                    pso = self.PS[4 + sqc % 2]
                    self.mmgroup(pso[:, 0:65],
                                 [(XSLB[:, st * 512 + sqc * 128: st * 512 + sqc * 128 + 128],
                                   va3[:, h * NTT + st, :]) for st in range(NTT)])
                    dc = self.POOL[:, 1032 + sqc % 8:1033 + sqc % 8]
                    P.op("ve", lambda v, o=dc, i=pso[:, 64:65]: v.reciprocal(o, i),
                         ins=[pso[:, 64:65]], outs=[dc])
                    tt = sl * 4 + sqc
                    ot = A2[:, tt * 1024 + h * 64: tt * 1024 + h * 64 + 64]
                    P.op("ve", lambda v, o=ot, i=pso[:, 0:64], d=dc:
                         v.tensor_scalar_mul(o, i, d),
                         ins=[pso[:, 0:64], dc], outs=[ot])

        for kt in range(KT):
            P.dma("sxf", A1[:, kt * S:(kt + 1) * S], self.xnf_sp[kt], dbg="xnfld")
        OGF = BIGB[:, 0:16384]
        OGT_ = [BIGB[:, 16384 + i * 1024: 16384 + (i + 1) * 1024] for i in range(2)]
        WOH = BIGB[:, 18432:26624]
        for hg in range(8):
            P.dma(("lwo", "lwv")[hg % 2], WOH, self.wb["f_Wo"][hg], dbg=f"fwo{hg}")

            def outproj_sl(sl):
                for dt in range(KT):
                    psb = self.PS[dt % 4]
                    self.mmgroup(psb[:, 0:512],
                                 [(WOH[:, r * 1024 + dt * 128: r * 1024 + dt * 128 + 128],
                                   OGF[:, r * S + sl * 512: r * S + sl * 512 + 512])
                                  for r in range(KT)])
                    a1t = A1[:, dt * S + sl * 512: dt * S + sl * 512 + 512]
                    P.op("ve", lambda v, o=a1t, a=a1t, b=psb[:, 0:512]:
                         v.tensor_add(o, a, b),
                         ins=[a1t, psb[:, 0:512]], outs=[a1t])

            for tt in range(NTT):
                OGT = OGT_[tt % 2]
                for hh in range(2):
                    h = hg * 2 + hh
                    ov = A2[:, tt * 1024 + h * 64: tt * 1024 + h * 64 + 64]
                    gos = go[:, tt * 128 + h * 8: tt * 128 + (h + 1) * 8]
                    o_in = ov.unsqueeze(1).broadcast_to([128, 8, 64])
                    g_in = gos.unsqueeze(2).broadcast_to([128, 8, 64])
                    ogt = OGT[:, hh * 512:(hh + 1) * 512]
                    P.op("ve", lambda v, o=ogt.rearrange("p (e d) -> p e d", d=64),
                         a=o_in, b=g_in: v.tensor_mul(o, a, b),
                         ins=[ov, gos], outs=[ogt])
                for half in range(2):
                    rs = list(range(half * 4, half * 4 + 4))
                    out = OGF.rearrange("p (r t) -> p r t", t=S)[
                        :, rs[0]:rs[0] + 4, tt * 128:tt * 128 + 128]
                    self.transpose_batch(
                        [(None, OGT[:, r * 128:(r + 1) * 128]) for r in rs],
                        half, out_ap=out)
                if tt % 4 == 3:
                    outproj_sl(tt // 4)

        for i, k in enumerate(("ln1_g", "ln1_b", "ln2_g", "ln2_b", "b2")):
            P.dma("prm", self.biasA[:, i * 8:(i + 1) * 8], self.fb[k], dbg="fprm")
        P.dma("prm", self.biasA[:, 64:96], self.fb1, dbg="fprmb1")
        self.ln_fm(self.biasA[:, 0:8], self.biasA[:, 8:16], write_a2=True)
        w1_src = lambda m: self.wb["f_W1"][m]
        w2_src = lambda dt: self.wb["f_W2"][dt]
        self.ffn_mlp(w1_src, w2_src, self.biasA[:, 64:96], self.biasA[:, 32:40])
        self.ln_fm(self.biasA[:, 16:24], self.biasA[:, 24:32], write_a2=False)
        self.tap("ffn_g", seq)

    def _final(self, seq):
        cfg, P = self.cfg, self.P
        S, KT, NTT = cfg["S"], cfg["KT"], cfg["NTT"]
        A1 = self.A1
        CTF = self.CT.ap().bitcast(F32)
        for tt in range(NTT):
            par = tt % 2
            xt = CTF[:, 2048:3072]
            st_ = CTF[:, par * 1024: par * 1024 + 1024]
            P.dma(("x0", "x1")[par], xt,
                  self.x[seq, tt * 128:(tt + 1) * 128, :], dbg=f"fx{tt}")
            P.dma(("fs0", "fs1")[par],
                  st_.rearrange("p (kt c) -> p kt c", c=128),
                  self.s_sp[:, :, tt * 128:(tt + 1) * 128].rearrange(
                      "kt p c -> p kt c"), dbg=f"fsld{tt}")
            for kt in range(KT):
                k2 = kt % 2
                gtile = CTF[:, 3072 + k2 * 128: 3200 + k2 * 128]
                sg = CTF[:, 3328 + k2 * 128: 3456 + k2 * 128]
                ob = CTF[:, 3584 + k2 * 128: 3712 + k2 * 128]
                a1t = A1[:, kt * S + tt * 128: kt * S + tt * 128 + 128]
                P.op("sc", lambda s, o=gtile, i=a1t: s.activation(o, i, AF.Silu),
                     ins=[a1t], outs=[gtile])
                P.op("ve", lambda v, o=sg, a=st_[:, kt * 128:(kt + 1) * 128],
                     b=gtile: v.tensor_mul(o, a, b),
                     ins=[st_, gtile], outs=[sg])
                psf = self.PS[4 + k2]
                P.op("pe", lambda t, o=psf[:, 0:128], i=sg: t.transpose(
                    o, i, self.identf[:, :]),
                    ins=[sg, self.identf.ap()], outs=[psf[:, 0:128]], mark=True)
                P.op("ve", lambda v, o=ob, a=psf[:, 0:128],
                     b=xt[:, kt * 128:(kt + 1) * 128]: v.tensor_add(o, a, b),
                     ins=[psf[:, 0:128], xt], outs=[ob], mark=True)
                P.dma("oq", self.out[seq, tt * 128:(tt + 1) * 128,
                                     kt * 128:(kt + 1) * 128], ob, dbg="outw")



def kernel(**inputs):
    cfg = make_cfg()
    lay = host_layout(cfg, inputs)
    x = np.asarray(inputs["L0_attn_x"], dtype=np.float32)
    b = Build(cfg)
    nc = b.build()
    in_maps = []
    for core in range(8):
        m = dict(lay)
        m["x"] = np.ascontiguousarray(x[core * 4:(core + 1) * 4])
        in_maps.append(m)
    res = run_bass_kernel_spmd(nc, in_maps, core_ids=list(range(8)))
    return np.concatenate([r["out"] for r in res.results], axis=0)
